# version 08
# baseline (speedup 1.0000x reference)
import math
from contextlib import ExitStack

import numpy as np
import ml_dtypes

import concourse.bass as bass
import concourse.mybir as mybir
from concourse.bass_utils import run_bass_kernel_spmd

F32 = mybir.dt.float32
BF16 = mybir.dt.bfloat16
AF = mybir.ActivationFunctionType
ALU = mybir.AluOpType
AX = mybir.AxisListType

B, S, D = 2, 8192, 1024
NCORE = 8
T = 2048
NT = 4
NTT = 16
HD = 64
NQH, NKH = 16, 4
NABH = 16
HID = 3072
NE = 4
EPS_LN = 1e-5
EPS_RMS = 1.1920928955078125e-07
UBLK = [3, 7, 11, 15]
import os
PIPELINE_AV = os.environ.get("K_PIPE_AV", "1") == "1"
ROT_ROPE = os.environ.get("K_ROT", "1") == "1"

CC_GROUPS = [[0, 1, 2, 3], [4, 5, 6, 7]]

FFN_QORD = [0, 4, 1, 5, 2, 6, 3, 7, 8, 12, 9, 13, 10, 14, 11, 15]
SLOT_OF_F = {FFN_QORD[s]: s for s in range(16)}


def own_tiles(cig):
    return sorted([cig, 7 - cig, 8 + cig, 15 - cig])


_OWNER = {}
for _c in range(4):
    for _slot, _g in enumerate(own_tiles(_c)):
        _OWNER[_g] = (_c, _slot)


def tile_owner(gt):
    return _OWNER[gt]


def _positions(cig):
    return np.concatenate([np.arange(gt * 512, (gt + 1) * 512)
                           for gt in own_tiles(cig)])


def make_tables(cig):
    pos = _positions(cig).astype(np.float64)[None, :]
    inv = 10000.0 ** (-np.arange(0, HD, 2, dtype=np.float64) / HD)
    ang = pos * inv[:, None][np.arange(64) % 32]
    cf = np.cos(ang); sf = np.sin(ang)
    tabC_ffn = np.concatenate([cf, cf], 0) * 0.125
    tabS_ffn = np.concatenate([sf, sf], 0) * 0.125
    inv2 = 10000.0 ** (-np.arange(0, 32, 2, dtype=np.float64) / 32)
    ang2 = pos * inv2[:, None][np.arange(32) // 2]
    ca = np.concatenate([np.cos(ang2), np.ones((32, T))], 0)
    sa = np.concatenate([np.sin(ang2), np.zeros((32, T))], 0)
    tabC_ab = np.concatenate([ca, ca], 0) * 0.125
    tabS_ab = np.concatenate([sa, sa], 0) * 0.125
    return (tabC_ffn.astype(np.float32), tabS_ffn.astype(np.float32),
            tabC_ab.astype(np.float32), tabS_ab.astype(np.float32))


def make_perms():
    pf = np.zeros((128, 128), np.float32)
    for h in range(2):
        o = 64 * h
        for j in range(32):
            pf[o + 32 + j, o + j] = -1.0
            pf[o + j, o + 32 + j] = 1.0
    pa = np.zeros((128, 128), np.float32)
    for h in range(2):
        o = 64 * h
        for i in range(16):
            pa[o + 2 * i + 1, o + 2 * i] = -1.0
            pa[o + 2 * i, o + 2 * i + 1] = 1.0
    return pf, pa


def make_masks():
    out = np.zeros((4, 128, 512), np.float32)
    for r in range(4):
        pr = np.arange(128)[:, None] + r * 128
        c = np.arange(512)[None, :]
        out[r] = (pr <= c).astype(np.float32)
    return out


def make_vmask(cig):
    out = np.zeros((128, 64), np.float32)
    tiles = own_tiles(cig)
    for qt in range(4):
        for j in range(16):
            out[:, qt * 16 + j] = 1.0 if j < tiles[qt] else 0.0
    return out


ENG_SEM = dict(pe="pe", act="act", dve="dve", pool="pool", sp="sp")
DMA_SEM = dict(sp="d_sp", act="d_act", pool="d_pool", dve="d_dve")


class Buf:
    __slots__ = ("name", "w", "readers", "pending")

    def __init__(self, name):
        self.name = name
        self.w = None
        self.readers = []
        self.pending = []


class Prog:
    def __init__(self, nc, es):
        self.nc = nc
        self.es = es
        self.q = dict(pe=nc.tensor, act=nc.scalar, dve=nc.vector,
                      pool=nc.gpsimd, sp=nc.sync)
        self.sem = {}
        self.cnt = {}
        self.waited = {e: {} for e in self.q}
        for name in ("pe", "act", "dve", "pool", "sp",
                     "d_sp", "d_act", "d_pool", "d_dve", "cc"):
            self.sem[name] = es.enter_context(nc.semaphore(name))
            self.cnt[name] = 0

    def getsem(self, name):
        if name not in self.sem:
            self.sem[name] = self.es.enter_context(self.nc.semaphore(name))
            self.cnt[name] = 0
        return self.sem[name]

    def mark(self, instr, sem, inc):
        instr.then_inc(self.sem[sem], inc)
        self.cnt[sem] += inc
        return (sem, self.cnt[sem])

    def wait(self, eng, tok):
        if tok is None:
            return
        sem, v = tok
        if self.waited[eng].get(sem, 0) >= v:
            return
        self.q[eng].wait_ge(self.sem[sem], v)
        self.waited[eng][sem] = v

    def _pre(self, eng, ins, outs):
        need = {}
        def add(tok):
            if tok is not None and tok[1] > need.get(tok[0], -1):
                need[tok[0]] = tok[1]
        for b in ins:
            add(b.w)
        for b in outs:
            add(b.w)
            for t in b.readers:
                add(t)
        for sem, v in need.items():
            self.wait(eng, (sem, v))

    def _post(self, tok, ins, outs):
        for b in ins:
            b.readers.append(tok)
        for b in outs:
            b.w = tok
            b.readers = []

    def op(self, eng, emit, ins=(), outs=(), sem=None, inc=1):
        self._pre(eng, ins, outs)
        instr = emit(self.q[eng])
        tok = self.mark(instr, sem or ENG_SEM[eng], inc)
        self._post(tok, ins, outs)
        return tok

    def dma(self, qname, out_ap, in_ap, ins=(), outs=(), dsem=None, **kw):
        if dsem is not None:
            self.getsem(dsem)
        return self.op(qname, lambda q: q.dma_start(out=out_ap, in_=in_ap, **kw),
                       ins=ins, outs=outs, sem=dsem or DMA_SEM[qname], inc=16)

    def mm_group(self, mms, ins=(), outs=()):
        self._pre("pe", ins, outs)
        n = len(mms)
        instr = None
        for i, (o, l, r) in enumerate(mms):
            instr = self.q["pe"].matmul(o, l, r, start=(i == 0), stop=(i == n - 1))
        tok = self.mark(instr, "pe", 1)
        self._post(tok, ins, outs)
        return tok

    def acc_mm(self, pbuf, first, last, emit, ins, force_mark=False):
        if first:
            self._pre("pe", ins, [pbuf])
        else:
            self._pre("pe", ins, [])
        instr = emit(self.q["pe"])
        pbuf.pending.extend(ins)
        if last or force_mark:
            tok = self.mark(instr, "pe", 1)
            for b in pbuf.pending:
                b.readers.append(tok)
            pbuf.pending = []
            if last:
                pbuf.w = tok
                pbuf.readers = []
            return tok
        return None

    def barrier(self):
        toks = [(s, self.cnt[s]) for s in self.cnt if self.cnt[s] > 0]
        for eng in self.q:
            for t in toks:
                self.wait(eng, t)


def build_program(dbg=(), upto=None):
    nc = bass.Bass("TRN2", target_bir_lowering=False)
    dbg = set(dbg)

    def din(name, shape, dt=F32):
        return nc.dram_tensor(name, list(shape), dt, kind="ExternalInput")

    x_in = din("x_local", (T, D))
    wq = din("ffn_wq", (D, D))
    wk = din("ffn_wk", (D, 256))
    wv = din("ffn_wv", (D, 256))
    wo = din("ffn_wo", (D, D))
    gate = din("ffn_gate", (D, NE))
    w1 = din("ffn_w1", (NE * D, HID))
    w2 = din("ffn_w2", (NE * HID, D))
    w3 = din("ffn_w3", (NE * D, HID))
    w_in = din("ab_w_in", (D, 3 * D))
    w_out = din("ab_w_out", (D, D))
    b_in = din("ab_b_in", (1, 3 * D))
    b_out = din("ab_b_out", (1, D))
    g1v = din("ffn_g1", (1, D))
    g2v = din("ffn_g2", (1, D))
    gammav = din("ab_gamma", (1, D))
    betav = din("ab_beta", (1, D))
    tabCf = din("tabC_ffn", (128, T))
    tabSf = din("tabS_ffn", (128, T))
    tabCa = din("tabC_ab", (128, T))
    tabSa = din("tabS_ab", (128, T))
    permF = din("perm_ffn", (128, 128))
    permA = din("perm_ab", (128, 128))
    masks_in = din("masks", (4 * 128, 512))
    ident_in = din("ident", (128, 128))
    onesb_in = din("ones_bf", (1, 128), BF16)
    onescol_in = din("ones_col", (128, 1), BF16)
    cvec_in = din("cvec", (128, 8))
    vmask_in = din("vmask", (128, 64))

    out_t = nc.dram_tensor("out_local", [T, D], F32, kind="ExternalOutput")

    r1 = nc.dram_tensor("r1", [T, D], F32)
    r2 = nc.dram_tensor("r2", [T, D], F32)
    ys = nc.dram_tensor("ystream", [T, D], F32)
    h2s = nc.dram_tensor("h2stream", [T, D], F32)
    kstg_f = nc.dram_tensor("kstg_f", [256, T], BF16)
    kg_f = nc.dram_tensor("kg_f", [1024, T], BF16)
    vstg_f = nc.dram_tensor("vstg_f", [512, 1040], BF16)
    vg_f = nc.dram_tensor("vg_f", [2048, 1040], BF16)
    kstg_a = nc.dram_tensor("kstg_a", [1024, T], BF16)
    kg_a = nc.dram_tensor("kg_a", [4096, T], BF16)
    vstg_a = nc.dram_tensor("vstg_a", [1024, 2080], BF16)
    vg_a = nc.dram_tensor("vg_a", [4096, 2080], BF16)
    kms_f = nc.dram_tensor("kms_f", [1, NKH], F32)
    kmg_f = nc.dram_tensor("kmg_f", [4, NKH], F32)
    kms_a = nc.dram_tensor("kms_a", [1, NABH], F32)
    kmg_a = nc.dram_tensor("kmg_a", [4, NABH], F32)
    w1b = nc.dram_tensor("w1b", [NE * D, HID], BF16)
    w2b = nc.dram_tensor("w2b", [NE * HID, D], BF16)
    w3b = nc.dram_tensor("w3b", [NE * D, HID], BF16)


    dump_outs = {}

    def dbg_out(name, shape, dt):
        if name in dbg and name not in dump_outs:
            dump_outs[name] = nc.dram_tensor(f"dbg_{name}", list(shape), dt,
                                             kind="ExternalOutput")
        return dump_outs.get(name)

    es = ExitStack()
    with es:
        p = Prog(nc, es)
        _uid = [0]

        def _nm(n):
            _uid[0] += 1
            return f"{n}_{_uid[0]}"

        sb = lambda name, shape, dt: es.enter_context(
            nc.sbuf_tensor(name, list(shape), dt))

        xnT = sb("xnT", (128, 16 * 1024), BF16)
        smv = sb("smv", (128, 4096), F32)
        msc = sb("msc", (128, 4 * 1024), F32)
        identf = sb("identf", (128, 128), F32)
        identb = sb("identb", (128, 128), BF16)
        permFs = sb("permFs", (128, 128), F32)
        permAs = sb("permAs", (128, 128), F32)
        onesb = sb("onesb", (1, 512), BF16)
        onescol = sb("onescol", (128, 1), BF16)
        parms = sb("parms", (128, 32), F32)
        cbias = sb("cbias", (128, 16), F32)
        cvec = sb("cvec_s", (128, 8), F32)
        vmsk = sb("vmsk", (128, 64), F32)
        rrb = sb("rrb", (1, 512), BF16)

        bq = {}

        def grid(name, *dims):
            import itertools
            if not dims:
                bq[(name,)] = Buf(name)
                return
            for idx in itertools.product(*[range(d) for d in dims]):
                bq[(name,) + idx] = Buf(f"{name}{idx}")

        def g(name, *idx):
            return bq[(name,) + tuple(idx)]

        grid("xnT", 8, NT)
        grid("qT", 8, NT)
        grid("msc", 4)
        grid("smv")
        grid("q2s")
        grid("cbias")
        grid("con")
        grid("parms")
        grid("rrb")
        for t_ in ("x_in", "r1", "r2", "ys", "h2s", "kstg_f", "kg_f", "vstg_f",
                   "vg_f", "kstg_a", "kg_a", "vstg_a", "vg_a", "kms_f", "kmg_f",
                   "kms_a", "kmg_a", "out"):
            grid(t_)

        con = g("con")
        rrbB = g("rrb")
        for t_ in ("w1b", "w2b", "w3b"):
            grid(t_)
        moe_cast_done = [False]

        def emit_moe_casts():
            if moe_cast_done[0]:
                return
            moe_cast_done[0] = True
            for srcT, dstT, nel, bname, dsm in (
                    (w1, w1b, NE * D * HID, "w1b", "dc1"),
                    (w3, w3b, NE * D * HID, "w3b", "dc3"),
                    (w2, w2b, NE * HID * D, "w2b", "dc2")):
                nchunk = 8
                step = nel // nchunk
                for ci2 in range(nchunk):
                    p.dma("pool",
                          bass.AP(dstT, ci2 * step, [[step, 1], [1, step]]),
                          bass.AP(srcT, ci2 * step, [[step, 1], [1, step]]),
                          outs=[g(bname)], dsem=dsm)

        for o, i in [(identf[:, :], ident_in.ap()),
                     (permFs[:, :], permF.ap()),
                     (permAs[:, :], permA.ap()),
                     (onesb[0:1, 0:128], onesb_in.ap()),
                     (onescol[:, :], onescol_in.ap()),
                     (cvec[:, :], cvec_in.ap()),
                     (vmsk[:, :], vmask_in.ap())]:
            p.dma("sp", o, i, outs=[con])
        p.dma("pool", identb[:, :], ident_in.ap(), outs=[con])
        for ci, v in enumerate((g1v, g2v, gammav, betav)):
            for c in range(8):
                p.dma("sp", parms[:, ci * 8 + c: ci * 8 + c + 1],
                      bass.AP(v, c * 128, [[1, 128], [1, 1]]), outs=[g("parms")])
        for i in (1, 2, 3):
            p.op("dve", lambda q, i=i: q.tensor_copy(
                onesb[0:1, i * 128:(i + 1) * 128], onesb[0:1, 0:128]),
                ins=[con], outs=[con])
        nc.const_aps.register(F32, EPS_LN, cvec[:, 0:1])
        nc.const_aps.register(F32, EPS_RMS, cvec[:, 1:2])
        nc.const_aps.register(F32, 1.0, cvec[:, 2:3])
        nc.const_aps.register(F32, 0.0, cvec[:, 3:4])
        p.barrier()

        final_toks = []

        def maybe_dump(name, ap, shape, dt, inbufs):
            d = dbg_out(name, shape, dt)
            if d is not None:
                final_toks.append(p.dma("sp", d.ap(), ap, ins=list(inbufs)))

        def maybe_dump_dram(name, src_dram, src_buf):
            d = dbg_out(name, (T, D), F32)
            if d is not None:
                final_toks.append(p.dma("sp", d.ap(), src_dram.ap(),
                                        ins=[src_buf]))

        def ln_rms_tile(src_buf, src_ap, dst_ap, dst_buf, mode, ri=0,
                        x2ap=None, x2=None):
            if x2ap is None:
                x2 = g("msc", 3)
                x2ap = msc[:, 3 * 1024:4 * 1024]
            sv = g("smv")
            C0 = ri * 8
            smc = lambda i: smv[:, C0 + i:C0 + i + 1]
            p.op("act", lambda q: q.activation(x2ap, src_ap, AF.Square),
                 ins=[src_buf], outs=[x2])
            p.op("dve", lambda q: q.tensor_reduce(smc(0), x2ap, AX.X,
                                                  ALU.add), ins=[x2], outs=[sv])
            if mode == "ln":
                p.op("dve", lambda q: q.tensor_reduce(smc(1), src_ap, AX.X,
                                                      ALU.add),
                     ins=[src_buf], outs=[sv])
                p.op("dve", lambda q: q.tensor_scalar(smc(1), smc(1),
                                                      1.0 / D, None, ALU.mult),
                     ins=[sv], outs=[sv])
                p.op("dve", lambda q: q.tensor_tensor(smc(2), smc(1),
                                                      smc(1), ALU.mult),
                     ins=[sv], outs=[sv])
                p.op("dve", lambda q: q.tensor_scalar(smc(3), smc(0),
                                                      1.0 / D, None, ALU.mult),
                     ins=[sv], outs=[sv])
                p.op("dve", lambda q: q.tensor_tensor(smc(2), smc(3),
                                                      smc(2), ALU.subtract),
                     ins=[sv], outs=[sv])
                var_col, eps = smc(2), EPS_LN
            else:
                p.op("dve", lambda q: q.tensor_scalar(smc(2), smc(0),
                                                      1.0 / D, None, ALU.mult),
                     ins=[sv], outs=[sv])
                var_col, eps = smc(2), EPS_RMS
            p.op("act", lambda q: q.activation(smc(4), var_col, AF.Sqrt,
                                               bias=eps), ins=[sv], outs=[sv])
            p.op("dve", lambda q: q.reciprocal(smc(5), smc(4)),
                 ins=[sv], outs=[sv])
            if mode == "ln":
                p.op("dve", lambda q: q.tensor_tensor(smc(6), smc(1),
                                                      smc(5), ALU.mult),
                     ins=[sv], outs=[sv])
                p.op("dve", lambda q: q.tensor_scalar(smc(6), smc(6),
                                                      -1.0, None, ALU.mult),
                     ins=[sv], outs=[sv])
                p.op("dve", lambda q: q.tensor_scalar(dst_ap, src_ap,
                                                      smc(5), smc(6),
                                                      ALU.mult, ALU.add),
                     ins=[src_buf, sv], outs=[dst_buf])
            else:
                p.op("dve", lambda q: q.tensor_scalar(dst_ap, src_ap,
                                                      smc(5), None,
                                                      ALU.mult),
                     ins=[src_buf, sv], outs=[dst_buf])

        def norm_transpose_phase(src_dram, src_buf, kind, store_stream=None,
                                 store_buf=None):
            with ExitStack() as ph:
                pt2 = [ph.enter_context(nc.psum_tensor(_nm("p_tr"), [128, 512],
                                                       BF16)) for _ in (0, 1)]
                zt2 = [ph.enter_context(nc.sbuf_tensor(_nm("zt"), [128, 1024],
                                                       BF16)) for _ in (0, 1)]
                x2b = ph.enter_context(nc.sbuf_tensor(_nm("x2b"), [128, 2048],
                                                      F32))
                ptb2 = [Buf("p_tr0"), Buf("p_tr1")]
                ztb2 = [Buf("zt0"), Buf("zt1")]
                x2B = [Buf("x2b0"), Buf("x2b1")]
                gcol0 = {"ffn": 0, "ffn2": 8, "ab": 16}[kind]
                for tt in range(NTT):
                    ri = tt % 2
                    pt, zt, ptb, ztb = pt2[ri], zt2[ri], ptb2[ri], ztb2[ri]
                    x2ap_ = x2b[:, ri * 1024:(ri + 1) * 1024]
                    xt, xt_ap = g("msc", 2 * ri), msc[:, 2 * ri * 1024:
                                                      (2 * ri + 1) * 1024]
                    yt, yt_ap = g("msc", 2 * ri + 1), msc[:, (2 * ri + 1) * 1024:
                                                          (2 * ri + 2) * 1024]
                    p.dma("sp", xt_ap,
                          bass.AP(src_dram, tt * 128 * D, [[D, 128], [1, D]]),
                          ins=[src_buf], outs=[xt], dsem=f"dm{ri}")
                    kw = dict(ri=ri, x2ap=x2ap_, x2=x2B[ri])
                    if kind == "ffn":
                        ln_rms_tile(xt, xt_ap, yt_ap, yt, "ln", **kw)
                        if store_stream is not None:
                            p.dma("sp", bass.AP(store_stream, tt * 128 * D,
                                                [[D, 128], [1, D]]), yt_ap,
                                  ins=[yt], outs=[store_buf],
                                  dsem=f"ds_ys{ri}")
                        ln_rms_tile(yt, yt_ap, zt[:, :], ztb, "rms", **kw)
                    elif kind == "ffn2":
                        ln_rms_tile(xt, xt_ap, zt[:, :], ztb, "rms", **kw)
                    else:
                        ln_rms_tile(xt, xt_ap, yt_ap, yt, "ln", **kw)
                        if store_stream is not None:
                            p.dma("sp", bass.AP(store_stream, tt * 128 * D,
                                                [[D, 128], [1, D]]), yt_ap,
                                  ins=[yt], outs=[store_buf],
                                  dsem=f"ds_h2{ri}")
                        ln_rms_tile(yt, yt_ap, zt[:, :], ztb, "ln", **kw)
                    qt, sub = divmod(tt, 4)
                    for c in range(8):
                        p.op("pe", lambda q, c=c: q.transpose(
                            pt[:, 0:128], zt[:, c * 128:(c + 1) * 128],
                            identb[:, :]), ins=[ztb, con], outs=[ptb])
                        dst = xnT[:, c * 2048 + qt * 512 + sub * 128:
                                  c * 2048 + qt * 512 + sub * 128 + 128]
                        if kind == "ab":
                            p.op("dve", lambda q, c=c, dst=dst: q.tensor_scalar(
                                dst, pt[:, 0:128], parms[:, 16 + c:17 + c],
                                parms[:, 24 + c:25 + c], ALU.mult, ALU.add),
                                ins=[ptb, g("parms")], outs=[g("xnT", c, qt)])
                        else:
                            p.op("dve", lambda q, c=c, dst=dst: q.tensor_scalar(
                                dst, pt[:, 0:128],
                                parms[:, gcol0 + c:gcol0 + c + 1], None,
                                ALU.mult),
                                ins=[ptb, g("parms")], outs=[g("xnT", c, qt)])
                p.barrier()

        KM2, QM2, KH2, QH2 = 0, 20, 40, 110
        KG16, KMX, KMAP, QP2, KP2, CRW = 200, 266, 290, 310, 330, 350
        RECR = 512

        def attn_invocation(tag):
            ffn = tag == "ffn"
            perm_s = permFs if ffn else permAs
            kstg, vstg, kms, kmg = ((kstg_f, vstg_f, kms_f, kmg_f) if ffn else
                                    (kstg_a, vstg_a, kms_a, kmg_a))
            kg_, vg_ = (kg_f, vg_f) if ffn else (kg_a, vg_a)
            kstgB = g("kstg_f" if ffn else "kstg_a")
            vstgB = g("vstg_f" if ffn else "vstg_a")
            kmsB = g("kms_f" if ffn else "kms_a")
            kmgB = g("kmg_f" if ffn else "kmg_a")
            kgB = g("kg_f" if ffn else "kg_a")
            vgB = g("vg_f" if ffn else "vg_a")
            slot_of = SLOT_OF_F if ffn else {h: h for h in range(16)}
            nkh = 4 if ffn else 16

            with ExitStack() as inv:
                sbi = lambda n, s, d: inv.enter_context(
                    nc.sbuf_tensor(_nm(n), list(s), d))
                qT = sbi("qT", (128, 16 * 1024), BF16)
                if ffn:
                    klocal = sbi("klocal", (128, 2 * T), BF16)
                    vlocal = sbi("vlocal", (128, 4 * 1040), BF16)
                    klB, vlB = Buf("klocal"), Buf("vlocal")
                    cons_masks = sbi("cmask", (128, 4 * 512), BF16)
                    cmB = Buf("cmask")
                    p.dma("pool",
                          cons_masks[:, :].rearrange("p (r c) -> p r c", r=4),
                          bass.AP(masks_in, 0,
                                  [[512, 128], [128 * 512, 4], [1, 512]]),
                          outs=[cmB])
                crowb = sbi("crowb", (1, 16), BF16)
                q2sb = [sbi(f"q2sb{i}", (128, 512), BF16) for i in (0, 1)]
                crB = Buf("crowb")
                q2B = [Buf("q2s0"), Buf("q2s1")]
                for oc_ in range(8):
                    for qt_ in range(NT):
                        bq[("qT", oc_, qt_)] = Buf(f"qT{oc_},{qt_}")

                with ExitStack() as ph:
                    sbp = lambda n, s, d: ph.enter_context(
                        nc.sbuf_tensor(_nm(n), list(s), d))
                    psp = lambda n, s, d=F32: ph.enter_context(
                        nc.psum_tensor(_nm(n), list(s), d))
                    wcol = [sbp(f"wcol{i}", (128, 8 * 512), BF16)
                            for i in (0, 1)]
                    tabC = sbp("tabC", (128, T), F32)
                    tabS = sbp("tabS", (128, T), F32)
                    kslot = sbp("kslot", (128, T), BF16)
                    vslot = sbp("vslot", (128, 16 * 130), BF16)
                    qraw = [sbp(f"qraw{i}", (128, 512), F32) for i in (0, 1)]
                    t1s = [sbp(f"t1s{i}", (128, 512), F32) for i in (0, 1)]
                    t2s = [sbp(f"t2s{i}", (128, 512), F32) for i in (0, 1)]
                    vtmp = [sbp(f"vtmp{i}", (128, 512), BF16) for i in (0, 1)]
                    binb = sbp("binb", (1, 3 * D), BF16) if not ffn else None
                    pq = [psp(f"pq{i}", (128, 512)) for i in (0, 1)]
                    ppm = [psp(f"ppm{i}", (128, 512)) for i in (0, 1)]
                    ptv = psp("ptv", (128, 512), BF16)
                    p1r = psp("p1r", (1, 512))
                    pcb = psp("pcb", (128, 16))
                    wcB = [[Buf(f"wc{i}_{k}") for k in range(8)]
                           for i in (0, 1)]
                    tabB = Buf("tab")
                    kslB, vslB = Buf("kslot"), Buf("vslot")
                    qrB = [Buf("qraw0"), Buf("qraw1")]
                    t1B = [Buf("t1s0"), Buf("t1s1")]
                    t2B = [Buf("t2s0"), Buf("t2s1")]
                    vtB = [Buf("vtmp0"), Buf("vtmp1")]
                    binB = Buf("binb")
                    pqB = [Buf("pq0"), Buf("pq1")]
                    ppmB = [Buf("ppm0"), Buf("ppm1")]
                    ptvB, p1B, pcbB = Buf("ptv"), Buf("p1r"), Buf("pcb")

                    p.dma("sp", tabC[:, :], (tabCf if ffn else tabCa).ap(),
                          outs=[tabB], dsem="dtab")
                    p.dma("sp", tabS[:, :], (tabSf if ffn else tabSa).ap(),
                          outs=[tabB], dsem="dtab")
                    if not ffn:
                        p.dma("pool", binb[:, :], b_in.ap(), outs=[binB])

                    def load_wcols(wb_i, wt, wt_cols, c0, width):
                        for kc in range(8):
                            p.dma("pool",
                                  wcol[wb_i][:, kc * width:(kc + 1) * width],
                                  bass.AP(wt, kc * 128 * wt_cols + c0,
                                          [[wt_cols, 128], [1, width]]),
                                  outs=[wcB[wb_i][kc]], dsem=f"dw{wb_i}")

                    def proj_group(pb, pbuf, wb_i, width, occ, qt, bias_col):
                        mms = []
                        ins = list(wcB[wb_i]) + [con] + [g("xnT", kc, qt)
                                                         for kc in range(8)]
                        if bias_col is not None:
                            mms.append((pb[:, :],
                                        binb[0:1, bias_col:bias_col + 128],
                                        onesb[0:1, 0:512]))
                            ins.append(binB)
                        for kc in range(8):
                            mms.append((pb[:, :],
                                        wcol[wb_i][:, kc * width + occ * 128:
                                                   kc * width + occ * 128 + 128],
                                        xnT[:, kc * 2048 + qt * 512:
                                            kc * 2048 + qt * 512 + 512]))
                        p.mm_group(mms, ins=ins, outs=[pbuf])

                    def rope_out(pb, pbuf, qt, dst_ap, dst_buf, ri):
                        ri = ri if ROT_ROPE else 0
                        p.op("act", lambda q: q.copy(qraw[ri][:, :], pb[:, :]),
                             ins=[pbuf], outs=[qrB[ri]])
                        p.op("pe", lambda q: q.matmul(ppm[ri][:, :],
                                                      perm_s[:, :],
                                                      qraw[ri][:, :],
                                                      start=True, stop=True),
                             ins=[qrB[ri], con], outs=[ppmB[ri]])
                        p.op("dve", lambda q: q.tensor_tensor(
                            t1s[ri][:, :], qraw[ri][:, :],
                            tabC[:, qt * 512:(qt + 1) * 512], ALU.mult),
                            ins=[qrB[ri], tabB], outs=[t1B[ri]])
                        p.op("dve", lambda q: q.tensor_tensor(
                            t2s[ri][:, :], ppm[ri][:, :],
                            tabS[:, qt * 512:(qt + 1) * 512], ALU.mult),
                            ins=[ppmB[ri], tabB], outs=[t2B[ri]])
                        p.op("dve", lambda q, dst_ap=dst_ap: q.tensor_tensor(
                            dst_ap, t1s[ri][:, :], t2s[ri][:, :], ALU.add),
                            ins=[t1B[ri], t2B[ri]], outs=[dst_buf])

                    def sqmax(src_ap, src_buf, col_even, col_odd, ri):
                        ri = ri if ROT_ROPE else 0
                        p.op("act", lambda q: q.activation(q2sb[ri][:, :],
                                                           src_ap, AF.Square),
                             ins=[src_buf], outs=[q2B[ri]])
                        for base, colx in ((0, col_even), (64, col_odd)):
                            p.op("pe", lambda q, base=base: q.matmul(
                                p1r[0:1, :], onescol[base:base + 64, 0:1],
                                q2sb[ri][base:base + 64, :], start=True,
                                stop=True),
                                ins=[q2B[ri], con], outs=[p1B])
                            p.op("dve", lambda q, colx=colx: q.tensor_reduce(
                                smv[0:1, colx:colx + 1], p1r[0:1, :], AX.X,
                                ALU.max), ins=[p1B], outs=[g("smv")])

                    for blk in range(2):
                        wb_i = blk % 2
                        load_wcols(wb_i, wq if ffn else w_in,
                                   D if ffn else 3 * D, blk * 512, 512)
                        for occ in range(4):
                            oc = blk * 4 + occ
                            for qt in range(NT):
                                bb = (oc * NT + qt) % 2
                                proj_group(pq[bb], pqB[bb], wb_i, 512, occ, qt,
                                           None if ffn else oc * 128)
                                dst = qT[:, oc * 2048 + qt * 512:
                                         oc * 2048 + qt * 512 + 512]
                                rope_out(pq[bb], pqB[bb], qt, dst,
                                         g("qT", oc, qt), bb)
                                sqmax(dst, g("qT", oc, qt),
                                      QH2 + (2 * oc) * 4 + qt,
                                      QH2 + (2 * oc + 1) * 4 + qt, bb)
                    for s_ in range(16):
                        p.op("dve", lambda q, s_=s_: q.tensor_reduce(
                            smv[0:1, QM2 + s_:QM2 + s_ + 1],
                            smv[0:1, QH2 + s_ * 4:QH2 + s_ * 4 + 4], AX.X,
                            ALU.max), ins=[g("smv")], outs=[g("smv")])

                    kwidth = 256 if ffn else 512
                    kblocks = 1 if ffn else 2
                    for blk in range(kblocks):
                        wb_i = blk % 2
                        if ffn:
                            load_wcols(wb_i, wk, 256, 0, 256)
                        else:
                            load_wcols(wb_i, w_in, 3 * D, 1024 + blk * 512, 512)
                        for occ in range(2 if ffn else 4):
                            oc = blk * 4 + occ
                            for qt in range(NT):
                                bb = (oc * NT + qt) % 2
                                proj_group(pq[bb], pqB[bb], wb_i, kwidth, occ,
                                           qt, None if ffn else 1024 + oc * 128)
                                dst = kslot[:, qt * 512:(qt + 1) * 512]
                                rope_out(pq[bb], pqB[bb], qt, dst, kslB, bb)
                                sqmax(dst, kslB, KH2 + (2 * oc) * 4 + qt,
                                      KH2 + (2 * oc + 1) * 4 + qt, bb)
                            for half in (0, 1):
                                kh = 2 * oc + half
                                p.op("dve", lambda q, kh=kh: q.tensor_reduce(
                                    smv[0:1, KM2 + kh:KM2 + kh + 1],
                                    smv[0:1, KH2 + kh * 4:KH2 + kh * 4 + 4],
                                    AX.X, ALU.max), ins=[g("smv")],
                                    outs=[g("smv")])
                            p.dma("sp", bass.AP(kstg, oc * 128 * T,
                                                [[T, 128], [1, T]]),
                                  kslot[:, :], ins=[kslB], outs=[kstgB],
                                  dsem="dstg_k")
                            if ffn:
                                p.op("dve", lambda q, oc=oc: q.tensor_copy(
                                    klocal[:, oc * T:(oc + 1) * T],
                                    kslot[:, :]), ins=[kslB], outs=[klB])

                    p.dma("sp", kms.ap(), smv[0:1, KM2:KM2 + nkh],
                          ins=[g("smv")], outs=[kmsB], dsem="dstg_m")
                    p.op("pool", lambda q: q.collective_compute(
                        "AllGather", ALU.bypass, replica_groups=CC_GROUPS,
                        ins=[kstg.ap().opt()], outs=[kg_.ap().opt()]),
                        ins=[kstgB], outs=[kgB], sem="cc", inc=1)
                    p.op("pool", lambda q: q.collective_compute(
                        "AllGather", ALU.bypass, replica_groups=CC_GROUPS,
                        ins=[kms.ap().opt()], outs=[kmg.ap().opt()]),
                        ins=[kmsB], outs=[kmgB], sem="cc", inc=1)

                    for blk in range(kblocks):
                        wb_i = blk % 2
                        if ffn:
                            load_wcols(wb_i, wv, 256, 0, 256)
                        else:
                            load_wcols(wb_i, w_in, 3 * D, 2048 + blk * 512, 512)
                        for occ in range(2 if ffn else 4):
                            oc = blk * 4 + occ
                            p.op("dve", lambda q: q.memset(vslot[:, :], 1.0),
                                 ins=[], outs=[vslB])
                            for qt in range(NT):
                                bb = (oc * NT + qt) % 2
                                proj_group(pq[bb], pqB[bb], wb_i, kwidth, occ,
                                           qt, None if ffn else 2048 + oc * 128)
                                p.op("act", lambda q, bb=bb: q.copy(
                                    vtmp[bb][:, :], pq[bb][:, :]),
                                    ins=[pqB[bb]], outs=[vtB[bb]])
                                for j in range(4):
                                    tt = qt * 4 + j
                                    p.op("pe", lambda q, j=j, bb=bb: q.transpose(
                                        ptv[:, 0:128],
                                        vtmp[bb][:, j * 128:(j + 1) * 128],
                                        identb[:, :]), ins=[vtB[bb], con],
                                        outs=[ptvB])
                                    for half in (0, 1):
                                        c0 = ((half * 1040 + tt * 65) if ffn
                                              else (tt * 130 + half * 65))
                                        p.op("dve",
                                             lambda q, c0=c0, half=half:
                                             q.tensor_copy(
                                                 vslot[:, c0:c0 + 64],
                                                 ptv[:, half * 64:
                                                     half * 64 + 64]),
                                             ins=[ptvB], outs=[vslB])
                            if ffn:
                                for half in (0, 1):
                                    kh = 2 * oc + half
                                    p.dma("sp",
                                          bass.AP(vstg, kh * 128 * 1040,
                                                  [[1040, 128], [1, 1040]]),
                                          vslot[:, half * 1040:
                                                (half + 1) * 1040],
                                          ins=[vslB], outs=[vstgB],
                                          dsem=f"dstg_v{half}")
                                    p.op("dve",
                                         lambda q, kh=kh, half=half:
                                         q.tensor_copy(
                                             vlocal[:, kh * 1040:
                                                    (kh + 1) * 1040],
                                             vslot[:, half * 1040:
                                                   (half + 1) * 1040]),
                                         ins=[vslB], outs=[vlB])
                            else:
                                p.dma("sp", bass.AP(vstg, oc * 128 * 2080,
                                                    [[2080, 128], [1, 2080]]),
                                      vslot[:, :], ins=[vslB], outs=[vstgB],
                                      dsem=f"dstg_v{oc % 2}")
                    p.op("pool", lambda q: q.collective_compute(
                        "AllGather", ALU.bypass, replica_groups=CC_GROUPS,
                        ins=[vstg.ap().opt()], outs=[vg_.ap().opt()]),
                        ins=[vstgB], outs=[vgB], sem="cc", inc=1)

                    p.dma("sp", smv[0:1, KG16:KG16 + 4 * nkh],
                          bass.AP(kmg, 0, [[4 * nkh, 1], [1, 4 * nkh]]),
                          ins=[kmgB], outs=[g("smv")], dsem="dsm")
                    p.op("dve", lambda q: q.tensor_tensor(
                        smv[0:1, KMX:KMX + nkh], smv[0:1, KG16:KG16 + nkh],
                        smv[0:1, KG16 + nkh:KG16 + 2 * nkh], ALU.max),
                        ins=[g("smv")], outs=[g("smv")])
                    for r_ in (2, 3):
                        p.op("dve", lambda q, r_=r_: q.tensor_tensor(
                            smv[0:1, KMX:KMX + nkh], smv[0:1, KMX:KMX + nkh],
                            smv[0:1, KG16 + r_ * nkh:KG16 + (r_ + 1) * nkh],
                            ALU.max), ins=[g("smv")], outs=[g("smv")])
                    for s_ in range(16):
                        kh = ((FFN_QORD[s_] // 4) if ffn else s_)
                        p.op("dve", lambda q, s_=s_, kh=kh: q.tensor_scalar(
                            smv[0:1, KMAP + s_:KMAP + s_ + 1],
                            smv[0:1, KMAP + s_:KMAP + s_ + 1], 0.0,
                            smv[0:1, KMX + kh:KMX + kh + 1], ALU.mult,
                            ALU.add), ins=[g("smv")], outs=[g("smv")])
                    pair_slots = []
                    if ffn:
                        for kh in range(4):
                            for u in (0, 1):
                                pair_slots.append((slot_of[4 * kh + 2 * u],
                                                   slot_of[4 * kh + 2 * u + 1]))
                    else:
                        for pr in range(8):
                            pair_slots.append((2 * pr, 2 * pr + 1))
                    for pp, (sa, sb_) in enumerate(pair_slots):
                        p.op("dve",
                             lambda q, pp=pp, sa=sa, sb_=sb_: q.tensor_tensor(
                                 smv[0:1, QP2 + pp:QP2 + pp + 1],
                                 smv[0:1, QM2 + sa:QM2 + sa + 1],
                                 smv[0:1, QM2 + sb_:QM2 + sb_ + 1], ALU.max),
                             ins=[g("smv")], outs=[g("smv")])
                        p.op("dve",
                             lambda q, pp=pp, sa=sa, sb_=sb_: q.tensor_tensor(
                                 smv[0:1, KP2 + pp:KP2 + pp + 1],
                                 smv[0:1, KMAP + sa:KMAP + sa + 1],
                                 smv[0:1, KMAP + sb_:KMAP + sb_ + 1], ALU.max),
                             ins=[g("smv")], outs=[g("smv")])
                    p.op("dve", lambda q: q.tensor_tensor(
                        smv[0:1, CRW:CRW + 8], smv[0:1, QP2:QP2 + 8],
                        smv[0:1, KP2:KP2 + 8], ALU.mult),
                        ins=[g("smv")], outs=[g("smv")])
                    p.op("act", lambda q: q.activation(
                        smv[0:1, CRW:CRW + 8], smv[0:1, CRW:CRW + 8],
                        AF.Sqrt, bias=0.0), ins=[g("smv")], outs=[g("smv")])
                    p.op("dve", lambda q: q.tensor_scalar(
                        smv[0:1, CRW:CRW + 8], smv[0:1, CRW:CRW + 8], -8.0,
                        None, ALU.mult), ins=[g("smv")], outs=[g("smv")])
                    p.op("dve", lambda q: q.tensor_copy(
                        crowb[0:1, 0:8], smv[0:1, CRW:CRW + 8]),
                        ins=[g("smv")], outs=[crB])
                    for pp in range(8):
                        p.op("pe", lambda q, pp=pp: q.matmul(
                            pcb[:, pp:pp + 1], onesb[0:1, 0:128],
                            crowb[0:1, pp:pp + 1], start=True, stop=True),
                            ins=[crB, con], outs=[pcbB])
                    p.op("dve", lambda q: q.tensor_copy(cbias[:, 0:8],
                                                        pcb[:, 0:8]),
                         ins=[pcbB], outs=[g("cbias")])
                    p.barrier()

                with ExitStack() as ph:
                    sbp = lambda n, s, d: ph.enter_context(
                        nc.sbuf_tensor(_nm(n), list(s), d))
                    psp = lambda n, s, d=F32: ph.enter_context(
                        nc.psum_tensor(_nm(n), list(s), d))
                    kwin = [sbp(f"kwin{i}", (128, 2048), BF16) for i in (0, 1)]
                    vwin = [sbp(f"vwin{i}", (128, 2080), BF16) for i in (0, 1)]
                    pP = sbp("pPex", (128, 2 * 1024), BF16)
                    ctxn = sbp("ctxn", (128, 8 * 512), BF16)
                    wob = sbp("wob", (128, 8 * 1024), BF16)
                    dsc = sbp("dsc", (128, 512), F32)
                    dts = sbp("dts", (128, 4 * 1024), F32)
                    rbc = sbp("rbc", (128, 512), F32)
                    boutb = sbp("boutb", (1, D), BF16) if not ffn else None
                    pS = [psp(f"pS{i}", (128, 1024)) for i in (0, 1)]
                    pAV = [psp(f"pAV{i}", (128, 512)) for i in range(4)]
                    kwinB = [Buf("kw0"), Buf("kw1")]
                    vwinB = [Buf("vw0"), Buf("vw1")]
                    pPB = [Buf("pP0"), Buf("pP1")]
                    wobB, dscB, rbcB = Buf("wob"), Buf("dsc"), Buf("rbc")
                    boutB = Buf("boutb")
                    pSB = [[Buf("pS00"), Buf("pS01")],
                           [Buf("pS10"), Buf("pS11")]]
                    pAVB = [Buf(f"pAV{i}") for i in range(4)]
                    dtsB = [Buf(f"dts{j}") for j in range(NT)]
                    ctxB = [Buf(f"ctx{c}") for c in range(8)]

                    for kc in range(8):
                        p.dma("pool", wob[:, kc * 1024:(kc + 1) * 1024],
                              bass.AP(wo if ffn else w_out, kc * 128 * D,
                                      [[D, 128], [1, D]]), outs=[wobB],
                              dsem="dwo")
                    if ffn:
                        emit_moe_casts()
                    if not ffn:
                        p.dma("pool", boutb[:, :], b_out.ap(), outs=[boutB])

                    pend = []

                    def _do_av(v_ap_of, vwbuf, heads, av_first, av_last, sb_,
                               mark=False):
                        for v_ in range(2):
                            qi, s_ = heads[v_]
                            p.acc_mm(pAVB[qi], av_first, av_last,
                                     lambda q, qi=qi, v_=v_: q.matmul(
                                         pAV[qi][0:65, :], v_ap_of(v_),
                                         pP[:, sb_ * 1024 + v_ * 512:
                                            sb_ * 1024 + v_ * 512 + 512],
                                         start=av_first, stop=av_last),
                                     ins=[vwbuf, pPB[sb_]],
                                     force_mark=(mark and v_ == 1))

                    def flush_av(mark=False):
                        while pend:
                            _do_av(*pend.pop(0), mark=mark)

                    def s_exp_av(k_ap_of, kwbuf, v_ap_of, vwbuf, tcol, qt,
                                 heads, av_first, av_last, sb_, pp_pair,
                                 diag_t=None):
                        for v_ in range(2):
                            qi, s_ = heads[v_]
                            base = 64 * (s_ % 2)
                            p.op("pe", lambda q, v_=v_, s_=s_, base=base:
                                 q.matmul(
                                     pS[sb_][:, v_ * 512:v_ * 512 + 512],
                                     k_ap_of(base, tcol),
                                     qT[base:base + 64,
                                        (s_ // 2) * 2048 + qt * 512:
                                        (s_ // 2) * 2048 + qt * 512 + 512],
                                     start=True, stop=True),
                                 ins=[kwbuf, g("qT", s_ // 2, qt)],
                                 outs=[pSB[sb_][v_]])
                        p.op("act", lambda q: q.activation(
                            pP[:, sb_ * 1024:sb_ * 1024 + 1024],
                            pS[sb_][:, :], AF.Exp,
                            bias=cbias[:, pp_pair:pp_pair + 1], scale=8.0),
                            ins=[pSB[sb_][0], pSB[sb_][1], g("cbias")],
                            outs=[pPB[sb_]])
                        if diag_t is not None:
                            for v_ in range(2):
                                p.op("dve", lambda q, v_=v_: q.tensor_tensor(
                                    pP[:, sb_ * 1024 + v_ * 512:
                                       sb_ * 1024 + v_ * 512 + 512],
                                    pP[:, sb_ * 1024 + v_ * 512:
                                       sb_ * 1024 + v_ * 512 + 512],
                                    cons_masks[:, diag_t * 512:
                                               (diag_t + 1) * 512],
                                    ALU.mult), ins=[pPB[sb_], cmB],
                                    outs=[pPB[sb_]])
                        pend.append((v_ap_of, vwbuf, heads, av_first,
                                     av_last, sb_))
                        if len(pend) > (1 if PIPELINE_AV else 0):
                            _do_av(*pend.pop(0))

                    def normalize(qi, s_, qt):
                        p.op("dve", lambda q: q.reciprocal(
                            smv[0:1, RECR:RECR + 512], pAV[qi][64:65, 0:512]),
                            ins=[pAVB[qi]], outs=[g("smv")])
                        p.op("dve", lambda q: q.tensor_copy(
                            rrb[0:1, :], smv[0:1, RECR:RECR + 512]),
                            ins=[g("smv")], outs=[rrbB])
                        p.op("pe", lambda q: q.matmul(
                            pS[0][0:64, 0:512], onesb[0:1, 0:64], rrb[0:1, :],
                            start=True, stop=True), ins=[rrbB, con],
                            outs=[pSB[0][0]])
                        p.op("dve", lambda q: q.tensor_copy(
                            rbc[0:64, :], pS[0][0:64, 0:512]),
                            ins=[pSB[0][0]], outs=[rbcB])
                        base = 64 * (s_ % 2)
                        dst = ctxn[base:base + 64,
                                   (s_ // 2) * 512:(s_ // 2) * 512 + 512]
                        p.op("dve", lambda q, dst=dst: q.tensor_tensor(
                            dst, pAV[qi][0:64, 0:512], rbc[0:64, :], ALU.mult),
                            ins=[pAVB[qi], rbcB], outs=[ctxB[s_ // 2]])

                    for qt in range(NT):
                        if ffn:
                            head_groups = [([(i, slot_of[4 * kh + i])
                                             for i in range(4)], kh)
                                           for kh in range(4)]
                        else:
                            head_groups = [([(i, 4 * grp + i)
                                             for i in range(4)], grp)
                                           for grp in range(4)]
                        for heads, gid in head_groups:
                            it = 0
                            if ffn:
                                kh = gid
                                kbase = 64 * (kh % 2)
                                for j in range(UBLK[qt]):
                                    r_, sl = tile_owner(j)
                                    kb = j % 2
                                    flush_av(mark=True)
                                    p.dma("sp",
                                          kwin[kb][kbase:kbase + 64, 0:512],
                                          bass.AP(kg_, (r_ * 256 + kh * 64) * T
                                                  + sl * 512,
                                                  [[T, 64], [1, 512]]),
                                          ins=[kgB], outs=[kwinB[kb]],
                                          dsem=f"dk{kb}")
                                    p.dma("sp", vwin[kb][:, 0:260],
                                          bass.AP(vg_, (r_ * 512 + kh * 128) *
                                                  1040 + sl * 260,
                                                  [[1040, 128], [1, 260]]),
                                          ins=[vgB], outs=[vwinB[kb]],
                                          dsem=f"dv{kb}")
                                    p.op("dve", lambda q, kb=kb, qt=qt, j=j:
                                         q.tensor_scalar(
                                             vwin[kb][:, 0:260],
                                             vwin[kb][:, 0:260],
                                             vmsk[:, qt * 16 + j:
                                                  qt * 16 + j + 1],
                                             None, ALU.mult),
                                         ins=[vwinB[kb], con],
                                         outs=[vwinB[kb]])
                                    for t_ in range(4):
                                        for u in (0, 1):
                                            hp = [heads[2 * u],
                                                  heads[2 * u + 1]]
                                            s_exp_av(
                                                lambda base, tcol, kb=kb:
                                                kwin[kb][base:base + 64,
                                                         tcol:tcol + 128],
                                                kwinB[kb],
                                                lambda v_, kb=kb, t_=t_:
                                                vwin[kb][:, t_ * 65:
                                                         t_ * 65 + 65],
                                                vwinB[kb], t_ * 128, qt, hp,
                                                it == 0, False,
                                                u % 2,
                                                gid * 2 + u)
                                        it += 1
                                for t_ in range(4):
                                    last = (t_ == 3)
                                    for u in (0, 1):
                                        hp = [heads[2 * u], heads[2 * u + 1]]
                                        s_exp_av(
                                            lambda base, tcol, kh=kh, qt=qt:
                                            klocal[base:base + 64,
                                                   (kh // 2) * T + qt * 512 +
                                                   tcol:
                                                   (kh // 2) * T + qt * 512 +
                                                   tcol + 128],
                                            klB,
                                            lambda v_, kh=kh, qt=qt, t_=t_:
                                            vlocal[:, kh * 1040 +
                                                   (qt * 4 + t_) * 65:
                                                   kh * 1040 +
                                                   (qt * 4 + t_) * 65 + 65],
                                            vlB, t_ * 128, qt, hp,
                                            it == 0, last,
                                            u % 2,
                                            gid * 2 + u, diag_t=t_)
                                    it += 1
                            else:
                                grp = gid
                                for r_ in range(4):
                                    flush_av(mark=True)
                                    for pr in (0, 1):
                                        pair = grp * 2 + pr
                                        p.dma("sp", kwin[pr][:, :],
                                              bass.AP(kg_, (r_ * 1024 +
                                                            pair * 128) * T,
                                                      [[T, 128], [1, T]]),
                                              ins=[kgB], outs=[kwinB[pr]],
                                              dsem=f"dk{pr}")
                                        p.dma("sp", vwin[pr][:, :],
                                              bass.AP(vg_, (r_ * 1024 +
                                                            pair * 128) * 2080,
                                                      [[2080, 128], [1, 2080]]),
                                              ins=[vgB], outs=[vwinB[pr]],
                                              dsem=f"dv{pr}")
                                    for jj in range(16):
                                        for pr in (0, 1):
                                            hp = [heads[2 * pr],
                                                  heads[2 * pr + 1]]
                                            s_exp_av(
                                                lambda base, tcol, pr=pr:
                                                kwin[pr][base:base + 64,
                                                         tcol:tcol + 128],
                                                kwinB[pr],
                                                lambda v_, pr=pr, jj=jj:
                                                vwin[pr][:, jj * 130 +
                                                         v_ * 65:
                                                         jj * 130 +
                                                         v_ * 65 + 65],
                                                vwinB[pr], jj * 128, qt, hp,
                                                r_ == 0 and jj == 0,
                                                r_ == 3 and jj == 15,
                                                it % 2, grp * 2 + pr)
                                            it += 1
                            flush_av()
                            for qi, s_ in heads:
                                normalize(qi, s_, qt)

                        for oc in range(8):
                            mms = []
                            ins = [wobB, con] + [ctxB[kc] for kc in range(8)]
                            if not ffn:
                                mms.append((pAV[0][:, :],
                                            boutb[0:1, oc * 128:oc * 128 + 128],
                                            onesb[0:1, 0:512]))
                                ins.append(boutB)
                            for kc in range(8):
                                mms.append((pAV[0][:, :],
                                            wob[:, kc * 1024 + oc * 128:
                                                kc * 1024 + oc * 128 + 128],
                                            ctxn[:, kc * 512:(kc + 1) * 512]))
                            p.mm_group(mms, ins=ins, outs=[pAVB[0]])
                            p.op("act", lambda q: q.copy(dsc[:, :],
                                                         pAV[0][:, :]),
                                 ins=[pAVB[0]], outs=[dscB])
                            for j in range(4):
                                p.op("pe", lambda q, j=j: q.transpose(
                                    pAV[1][:, 0:128],
                                    dsc[:, j * 128:(j + 1) * 128],
                                    identf[:, :]), ins=[dscB, con],
                                    outs=[pAVB[1]])
                                p.op("dve", lambda q, j=j, oc=oc: q.tensor_copy(
                                    dts[:, j * 1024 + oc * 128:
                                        j * 1024 + oc * 128 + 128],
                                    pAV[1][:, 0:128]), ins=[pAVB[1]],
                                    outs=[dtsB[j]])

                        for j in range(4):
                            off = (qt * 512 + j * 128) * D
                            if ffn:
                                p.dma("sp", msc[:, 0:1024],
                                      bass.AP(ys, off, [[D, 128], [1, D]]),
                                      ins=[g("ys")], outs=[g("msc", 0)],
                                      dsem="dm0")
                                p.op("dve", lambda q, j=j: q.tensor_tensor(
                                    msc[:, 1024:2048], msc[:, 0:1024],
                                    dts[:, j * 1024:(j + 1) * 1024], ALU.add),
                                    ins=[g("msc", 0), dtsB[j]],
                                    outs=[g("msc", 1)])
                                p.dma("sp",
                                      bass.AP(ys, off, [[D, 128], [1, D]]),
                                      msc[:, 1024:2048], ins=[g("msc", 1)],
                                      outs=[g("ys")], dsem="ds_ys0")
                            else:
                                p.dma("sp", msc[:, 0:1024],
                                      bass.AP(r1, off, [[D, 128], [1, D]]),
                                      ins=[g("r1")], outs=[g("msc", 0)],
                                      dsem="dm0")
                                p.dma("sp", msc[:, 2048:3072],
                                      bass.AP(h2s, off, [[D, 128], [1, D]]),
                                      ins=[g("h2s")], outs=[g("msc", 2)],
                                      dsem="dm2")
                                p.op("dve", lambda q, j=j: q.tensor_tensor(
                                    msc[:, 1024:2048], msc[:, 2048:3072],
                                    dts[:, j * 1024:(j + 1) * 1024], ALU.add),
                                    ins=[g("msc", 2), dtsB[j]],
                                    outs=[g("msc", 1)])
                                p.op("dve", lambda q: q.tensor_tensor(
                                    msc[:, 1024:2048], msc[:, 0:1024],
                                    msc[:, 1024:2048], ALU.add),
                                    ins=[g("msc", 0), g("msc", 1)],
                                    outs=[g("msc", 1)])
                                p.dma("sp",
                                      bass.AP(r2, off, [[D, 128], [1, D]]),
                                      msc[:, 1024:2048], ins=[g("msc", 1)],
                                      outs=[g("r2")], dsem="ds_r2")
                    p.barrier()

        GL0, GW0 = 0, 2048

        def moe_and_close(src_dram, src_buf, dst_dram, dst_buf):
            norm_transpose_phase(ys, g("ys"), "ffn2")
            with ExitStack() as mo:
                sbm = lambda n, s, d: mo.enter_context(
                    nc.sbuf_tensor(_nm(n), list(s), d))
                acc = sbm("acc", (128, 8 * 1024), F32)
                cbm = sbm("cbm", (128, 8 * 512), BF16)
                gateb = sbm("gateb", (128, 32), BF16)
                accB = Buf("acc")
                cbmB = [Buf(f"cbm{i}") for i in range(8)]
                gateB = Buf("gateb")
                p.dma("pool", gateb[:, :].rearrange("p (k e) -> p k e", k=8),
                      bass.AP(gate, 0, [[4, 128], [512, 8], [1, 4]]),
                      outs=[gateB], dsem="dsm")

                for half in range(2):
                    with ExitStack() as gh:
                        pg = [gh.enter_context(
                            nc.psum_tensor(_nm(f"pg{i}"), [1, 512], F32))
                            for i in range(4)]
                        pcm = gh.enter_context(
                            nc.psum_tensor(_nm("pcm"), [128, 512], F32))
                        pgB = [Buf(f"pg{i}") for i in range(4)]
                        pcmB = Buf("pcm")
                        for tq in range(2):
                            qt = half * 2 + tq
                            L = lambda e: smv[0:1, GL0 + e * 512:
                                              GL0 + (e + 1) * 512]
                            W_ = lambda i: smv[0:1, GW0 + i * 512:
                                               GW0 + (i + 1) * 512]
                            for e in range(NE):
                                p.mm_group(
                                    [(pg[e][0:1, :],
                                      gateb[:, kc * 4 + e:kc * 4 + e + 1],
                                      xnT[:, kc * 2048 + qt * 512:
                                          kc * 2048 + qt * 512 + 512])
                                     for kc in range(8)],
                                    ins=[gateB] + [g("xnT", kc, qt)
                                                   for kc in range(8)],
                                    outs=[pgB[e]])
                                p.op("dve", lambda q, e=e: q.tensor_copy(
                                    L(e), pg[e][0:1, :]),
                                    ins=[pgB[e]], outs=[g("smv")])
                            sop = lambda o, a, b2, alu: p.op(
                                "dve", lambda q: q.tensor_tensor(o, a, b2,
                                                                 alu),
                                ins=[g("smv")], outs=[g("smv")])
                            sop(W_(0), L(0), L(1), ALU.max)
                            sop(W_(1), L(0), L(1), ALU.min)
                            sop(W_(2), L(2), L(3), ALU.max)
                            sop(W_(3), L(2), L(3), ALU.min)
                            sop(W_(1), W_(1), W_(3), ALU.max)
                            sop(W_(3), W_(0), W_(2), ALU.min)
                            sop(W_(0), W_(0), W_(2), ALU.max)
                            sop(W_(1), W_(1), W_(3), ALU.max)
                            sop(W_(2), W_(1), W_(0), ALU.subtract)
                            p.op("act", lambda q: q.activation(
                                W_(2), W_(2), AF.Exp, bias=0.0),
                                ins=[g("smv")], outs=[g("smv")])
                            p.op("dve", lambda q: q.tensor_scalar(
                                W_(2), W_(2), 1.0, None, ALU.add),
                                ins=[g("smv")], outs=[g("smv")])
                            p.op("dve", lambda q: q.reciprocal(W_(3), W_(2)),
                                 ins=[g("smv")], outs=[g("smv")])
                            for e in range(NE):
                                sop(W_(2), L(e), W_(0), ALU.subtract)
                                p.op("act", lambda q: q.activation(
                                    W_(2), W_(2), AF.Exp, bias=0.0),
                                    ins=[g("smv")], outs=[g("smv")])
                                sop(L(e), L(e), W_(1), ALU.is_ge)
                                sop(W_(2), W_(2), L(e), ALU.mult)
                                sop(W_(2), W_(2), W_(3), ALU.mult)
                                p.op("dve", lambda q: q.tensor_copy(
                                    rrb[0:1, :], W_(2)),
                                    ins=[g("smv")], outs=[rrbB])
                                p.op("pe", lambda q: q.matmul(
                                    pcm[:, :], onesb[0:1, 0:128],
                                    rrb[0:1, :], start=True, stop=True),
                                    ins=[rrbB, con], outs=[pcmB])
                                ci = e * 2 + tq
                                p.op("dve", lambda q, ci=ci: q.tensor_copy(
                                    cbm[:, ci * 512:(ci + 1) * 512],
                                    pcm[:, :]), ins=[pcmB], outs=[cbmB[ci]])
                        p.barrier()

                    with ExitStack() as eh:
                        sbe = lambda n, s, d: eh.enter_context(
                            nc.sbuf_tensor(_nm(n), list(s), d))
                        pse = lambda n: eh.enter_context(
                            nc.psum_tensor(_nm(n), [128, 512], F32))
                        wbuf = [sbe(f"wmoe{i}", (128, 36 * 512), BF16)
                                for i in (0, 1)]
                        xne = sbe("xne", (128, 2 * 8 * 512), BF16)
                        hbuf = sbe("hbuf", (128, 6 * 512), BF16)
                        ssl = [sbe(f"ssl{i}", (128, 512), BF16) for i in (0, 1)]
                        pa_ = [pse(f"pa{i}") for i in (0, 1)]
                        pb_ = [pse(f"pb{i}") for i in (0, 1)]
                        pw_ = [pse(f"pw{i}") for i in (0, 1)]
                        wB = [[Buf(f"w{i}_{k}") for k in range(22)]
                              for i in (0, 1)]
                        xneB = [Buf("xne0"), Buf("xne1")]
                        sslB = [Buf("ssl0"), Buf("ssl1")]
                        paB = [Buf("pa0"), Buf("pa1")]
                        pbB = [Buf("pb0"), Buf("pb1")]
                        pwB = [Buf("pw0"), Buf("pw1")]
                        hcB = [Buf(f"hc{i}") for i in range(6)]

                        p.op("dve", lambda q: q.memset(acc[:, :], 0.0),
                             ins=[], outs=[accB])

                        def load_moe_w(wi, e, hq):
                            for kc in range(8):
                                p.dma("pool",
                                      wbuf[wi][:, kc * 768:(kc + 1) * 768],
                                      bass.AP(w1b, (e * 1024 + kc * 128) * HID +
                                              hq * 768,
                                              [[HID, 128], [1, 768]]),
                                      ins=[g("w1b")], outs=[wB[wi][kc]],
                                      dsem=f"dw{wi}a")
                                p.dma("pool",
                                      wbuf[wi][:, 6144 + kc * 768:
                                               6144 + (kc + 1) * 768],
                                      bass.AP(w3b, (e * 1024 + kc * 128) * HID +
                                              hq * 768,
                                              [[HID, 128], [1, 768]]),
                                      ins=[g("w3b")], outs=[wB[wi][8 + kc]],
                                      dsem=f"dw{wi}b")
                            for kc2 in range(6):
                                p.dma("pool",
                                      wbuf[wi][:, 12288 + kc2 * 1024:
                                               12288 + (kc2 + 1) * 1024],
                                      bass.AP(w2b, (e * HID + hq * 768 +
                                                   kc2 * 128) * D,
                                              [[D, 128], [1, D]]),
                                      ins=[g("w2b")], outs=[wB[wi][16 + kc2]],
                                      dsem=f"dw{wi}c")

                        phases = [(e, hq) for e in range(NE) for hq in range(4)]
                        load_moe_w(0, *phases[0])
                        xne_done = set()
                        for pi, (e, hq) in enumerate(phases):
                            wi = pi % 2
                            if pi + 1 < len(phases):
                                load_moe_w(1 - wi, *phases[pi + 1])
                            if e not in xne_done:
                                xne_done.add(e)
                                for tq0 in range(2):
                                    qt0 = half * 2 + tq0
                                    ci0 = e * 2 + tq0
                                    for kc in range(8):
                                        p.op("dve",
                                             lambda q, kc=kc, ci0=ci0, qt0=qt0,
                                             tq0=tq0: q.tensor_tensor(
                                                 xne[:, tq0 * 4096 + kc * 512:
                                                     tq0 * 4096 + (kc + 1) * 512],
                                                 xnT[:, kc * 2048 + qt0 * 512:
                                                     kc * 2048 + qt0 * 512 + 512],
                                                 cbm[:, ci0 * 512:
                                                     (ci0 + 1) * 512],
                                                 ALU.mult),
                                             ins=[g("xnT", kc, qt0), cbmB[ci0]],
                                             outs=[xneB[tq0]])
                            for tq in range(2):
                                qt = half * 2 + tq
                                ci = e * 2 + tq
                                for hc in range(6):
                                    bb = hc % 2
                                    p.mm_group(
                                        [(pa_[bb][:, :],
                                          wbuf[wi][:, kc * 768 + hc * 128:
                                                   kc * 768 + hc * 128 + 128],
                                          xnT[:, kc * 2048 + qt * 512:
                                              kc * 2048 + qt * 512 + 512])
                                         for kc in range(8)],
                                        ins=wB[wi][0:8] + [g("xnT", kc, qt)
                                                           for kc in range(8)],
                                        outs=[paB[bb]])
                                    p.mm_group(
                                        [(pb_[bb][:, :],
                                          wbuf[wi][:, 6144 + kc * 768 +
                                                   hc * 128:
                                                   6144 + kc * 768 +
                                                   hc * 128 + 128],
                                          xne[:, tq * 4096 + kc * 512:
                                              tq * 4096 + (kc + 1) * 512])
                                         for kc in range(8)],
                                        ins=wB[wi][8:16] + [xneB[tq]],
                                        outs=[pbB[bb]])
                                    p.op("act", lambda q, bb=bb: q.activation(
                                        ssl[bb][:, :], pa_[bb][:, :], AF.Silu),
                                        ins=[paB[bb]], outs=[sslB[bb]])
                                    p.op("dve", lambda q, hc=hc, bb=bb:
                                         q.tensor_tensor(
                                             hbuf[:, hc * 512:(hc + 1) * 512],
                                             ssl[bb][:, :], pb_[bb][:, :],
                                             ALU.mult),
                                         ins=[sslB[bb], pbB[bb]], outs=[hcB[hc]])
                                for oc in range(8):
                                    bb = oc % 2
                                    p.mm_group(
                                        [(pw_[bb][:, :],
                                          wbuf[wi][:, 12288 + kc2 * 1024 +
                                                   oc * 128:
                                                   12288 + kc2 * 1024 +
                                                   oc * 128 + 128],
                                          hbuf[:, kc2 * 512:(kc2 + 1) * 512])
                                         for kc2 in range(6)],
                                        ins=wB[wi][16:22] + hcB,
                                        outs=[pwB[bb]])
                                    p.op("dve", lambda q, oc=oc, tq=tq, bb=bb:
                                         q.tensor_tensor(
                                             acc[:, oc * 1024 + tq * 512:
                                                 oc * 1024 + tq * 512 + 512],
                                             acc[:, oc * 1024 + tq * 512:
                                                 oc * 1024 + tq * 512 + 512],
                                             pw_[bb][:, :], ALU.add),
                                         ins=[pwB[bb]], outs=[accB])
                        p.barrier()

                    with ExitStack() as ch:
                        ptc = ch.enter_context(
                            nc.psum_tensor(_nm("ptc"), [128, 512], F32))
                        dtile = ch.enter_context(
                            nc.sbuf_tensor(_nm("dtile"), [128, 1024], F32))
                        ptcB, dtB = Buf("ptc"), Buf("dtile")
                        for tt in range(8):
                            tt_g = half * 8 + tt
                            off = tt_g * 128 * D
                            for c in range(8):
                                p.op("pe", lambda q, c=c, tt=tt: q.transpose(
                                    ptc[:, 0:128],
                                    acc[:, c * 1024 + tt * 128:
                                        c * 1024 + tt * 128 + 128],
                                    identf[:, :]), ins=[accB, con],
                                    outs=[ptcB])
                                p.op("dve", lambda q, c=c: q.tensor_copy(
                                    dtile[:, c * 128:(c + 1) * 128],
                                    ptc[:, 0:128]), ins=[ptcB], outs=[dtB])
                            p.dma("sp", msc[:, 0:1024],
                                  bass.AP(ys, off, [[D, 128], [1, D]]),
                                  ins=[g("ys")], outs=[g("msc", 0)],
                                  dsem="dm0")
                            p.dma("sp", msc[:, 2048:3072],
                                  bass.AP(src_dram, off, [[D, 128], [1, D]]),
                                  ins=[src_buf], outs=[g("msc", 2)],
                                  dsem="dm2")
                            p.op("dve", lambda q: q.tensor_tensor(
                                msc[:, 1024:2048], msc[:, 0:1024],
                                dtile[:, :], ALU.add),
                                ins=[g("msc", 0), dtB], outs=[g("msc", 1)])
                            p.op("dve", lambda q: q.scalar_tensor_tensor(
                                msc[:, 1024:2048], msc[:, 1024:2048], 0.5,
                                msc[:, 2048:3072], ALU.mult, ALU.add),
                                ins=[g("msc", 1), g("msc", 2)],
                                outs=[g("msc", 1)])
                            p.dma("sp",
                                  bass.AP(dst_dram, off, [[D, 128], [1, D]]),
                                  msc[:, 1024:2048], ins=[g("msc", 1)],
                                  outs=[dst_buf], dsem="ds_out")
                    p.barrier()

        norm_transpose_phase(x_in, g("x_in"), "ffn", store_stream=ys,
                             store_buf=g("ys"))
        maybe_dump_dram("y0", ys, g("ys"))
        maybe_dump("xnT1", xnT[:, :], (128, 16 * 1024), BF16,
                   [g("xnT", c, t_) for c in range(8) for t_ in range(NT)])
        if upto != "norm1":
            attn_invocation("ffn")
            maybe_dump_dram("y1a", ys, g("ys"))
            if upto != "attn1":
                moe_and_close(x_in, g("x_in"), r1, g("r1"))
                maybe_dump_dram("r1d", r1, g("r1"))
                if upto != "moe1":
                    norm_transpose_phase(r1, g("r1"), "ab", store_stream=h2s,
                                         store_buf=g("h2s"))
                    attn_invocation("ab")
                    maybe_dump_dram("r2d", r2, g("r2"))
                    if upto != "ab":
                        norm_transpose_phase(r2, g("r2"), "ffn",
                                             store_stream=ys,
                                             store_buf=g("ys"))
                        attn_invocation("ffn")
                        moe_and_close(r2, g("r2"), out_t, g("out"))

        if upto is not None:
            final_toks.append(p.dma("sp", out_t.ap(), x_in.ap(),
                                    ins=[g("x_in")]))

        for tok in final_toks:
            p.wait("sp", tok)
        p.wait("sp", ("d_sp", p.cnt["d_sp"]))
        p.barrier()

    return nc


def _prep_inputs(inputs, cig, batch):
    x = inputs["L0_attn_X"]
    tiles = own_tiles(cig)
    x_local = np.concatenate([x[batch, gt * 512:(gt + 1) * 512, :]
                              for gt in tiles], 0)
    qperm = np.concatenate([np.arange(64 * h, 64 * h + 64) for h in FFN_QORD])
    wq_r = inputs["L0_ffn_wq"][:, qperm]
    wo_r = inputs["L0_ffn_wo"][qperm, :]
    tabCf, tabSf, tabCa, tabSa = make_tables(cig)
    pf, pa = make_perms()
    masks = make_masks().reshape(4 * 128, 512)
    m = {
        "x_local": x_local,
        "ffn_wq": wq_r,
        "ffn_wk": inputs["L0_ffn_wk"],
        "ffn_wv": inputs["L0_ffn_wv"],
        "ffn_wo": wo_r,
        "ffn_gate": inputs["L0_ffn_gate"],
        "ffn_w1": inputs["L0_ffn_w1"].reshape(NE * D, HID),
        "ffn_w2": inputs["L0_ffn_w2"].reshape(NE * HID, D),
        "ffn_w3": inputs["L0_ffn_w3"].reshape(NE * D, HID),
        "ffn_g1": inputs["L0_ffn_g1"].reshape(1, D),
        "ffn_g2": inputs["L0_ffn_g2"].reshape(1, D),
        "ab_w_in": inputs["L0_attn_W_in"],
        "ab_w_out": inputs["L0_attn_W_out"],
        "ab_b_in": inputs["L0_attn_b_in"].reshape(1, 3 * D),
        "ab_b_out": inputs["L0_attn_b_out"].reshape(1, D),
        "ab_gamma": inputs["L0_attn_gamma"].reshape(1, D),
        "ab_beta": inputs["L0_attn_beta"].reshape(1, D),
        "tabC_ffn": tabCf, "tabS_ffn": tabSf,
        "tabC_ab": tabCa, "tabS_ab": tabSa,
        "perm_ffn": pf, "perm_ab": pa,
        "masks": masks,
        "ident": np.eye(128, dtype=np.float32),
        "ones_bf": np.ones((1, 128), ml_dtypes.bfloat16),
        "ones_col": np.ones((128, 1), ml_dtypes.bfloat16),
        "cvec": np.tile(np.array([[EPS_LN, EPS_RMS, 1.0, 0.0, 0, 0, 0, 0]],
                                 np.float32), (128, 1)),
        "vmask": make_vmask(cig),
    }
    return {k: np.ascontiguousarray(v) for k, v in m.items()}


def kernel(**inputs):
    nc = build_program()
    in_maps = []
    for core in range(NCORE):
        batch, cig = divmod(core, 4)
        in_maps.append(_prep_inputs(inputs, cig, batch))
    res = run_bass_kernel_spmd(nc, in_maps, core_ids=list(range(NCORE)))
    out = np.zeros((B, S, D), np.float32)
    for core in range(NCORE):
        batch, cig = divmod(core, 4)
        o = res.results[core]["out_local"]
        for slot, gt in enumerate(own_tiles(cig)):
            out[batch, gt * 512:(gt + 1) * 512, :] = \
                o[slot * 512:(slot + 1) * 512]
    return out


# version 10
# speedup vs baseline: 1.1063x; 1.1063x over previous
import math
from contextlib import ExitStack

import numpy as np
import ml_dtypes

import concourse.bass as bass
import concourse.mybir as mybir
from concourse.bass_utils import run_bass_kernel_spmd

F32 = mybir.dt.float32
BF16 = mybir.dt.bfloat16
AF = mybir.ActivationFunctionType
ALU = mybir.AluOpType
AX = mybir.AxisListType

B, S, D = 2, 8192, 1024
NCORE = 8
T = 2048
NT = 4
NTT = 16
HD = 64
NQH, NKH = 16, 4
NABH = 16
HID = 3072
NE = 4
EPS_LN = 1e-5
EPS_RMS = 1.1920928955078125e-07
UBLK = [3, 7, 11, 15]
import os
PIPELINE_AV = os.environ.get("K_PIPE_AV", "1") == "1"
ROT_ROPE = os.environ.get("K_ROT", "1") == "1"

CC_GROUPS = [[0, 1, 2, 3], [4, 5, 6, 7]]

FFN_QORD = [0, 4, 1, 5, 2, 6, 3, 7, 8, 12, 9, 13, 10, 14, 11, 15]
SLOT_OF_F = {FFN_QORD[s]: s for s in range(16)}


def own_tiles(cig):
    return sorted([cig, 7 - cig, 8 + cig, 15 - cig])


_OWNER = {}
for _c in range(4):
    for _slot, _g in enumerate(own_tiles(_c)):
        _OWNER[_g] = (_c, _slot)


def tile_owner(gt):
    return _OWNER[gt]


def _positions(cig):
    return np.concatenate([np.arange(gt * 512, (gt + 1) * 512)
                           for gt in own_tiles(cig)])


def make_tables(cig):
    pos = _positions(cig).astype(np.float64)[None, :]
    inv = 10000.0 ** (-np.arange(0, HD, 2, dtype=np.float64) / HD)
    ang = pos * inv[:, None][np.arange(64) % 32]
    cf = np.cos(ang); sf = np.sin(ang)
    tabC_ffn = np.concatenate([cf, cf], 0) * 0.125
    tabS_ffn = np.concatenate([sf, sf], 0) * 0.125
    inv2 = 10000.0 ** (-np.arange(0, 32, 2, dtype=np.float64) / 32)
    ang2 = pos * inv2[:, None][np.arange(32) // 2]
    ca = np.concatenate([np.cos(ang2), np.ones((32, T))], 0)
    sa = np.concatenate([np.sin(ang2), np.zeros((32, T))], 0)
    tabC_ab = np.concatenate([ca, ca], 0) * 0.125
    tabS_ab = np.concatenate([sa, sa], 0) * 0.125
    return (tabC_ffn.astype(np.float32), tabS_ffn.astype(np.float32),
            tabC_ab.astype(np.float32), tabS_ab.astype(np.float32))


def make_perms():
    pf = np.zeros((128, 128), np.float32)
    for h in range(2):
        o = 64 * h
        for j in range(32):
            pf[o + 32 + j, o + j] = -1.0
            pf[o + j, o + 32 + j] = 1.0
    pa = np.zeros((128, 128), np.float32)
    for h in range(2):
        o = 64 * h
        for i in range(16):
            pa[o + 2 * i + 1, o + 2 * i] = -1.0
            pa[o + 2 * i, o + 2 * i + 1] = 1.0
    return pf, pa


def make_masks():
    out = np.zeros((4, 128, 512), np.float32)
    for r in range(4):
        pr = np.arange(128)[:, None] + r * 128
        c = np.arange(512)[None, :]
        out[r] = (pr <= c).astype(np.float32)
    return out


def make_vmask(cig):
    out = np.zeros((128, 64), np.float32)
    tiles = own_tiles(cig)
    for qt in range(4):
        for j in range(16):
            out[:, qt * 16 + j] = 1.0 if j < tiles[qt] else 0.0
    return out


ENG_SEM = dict(pe="pe", act="act", dve="dve", pool="pool", sp="sp")
DMA_SEM = dict(sp="d_sp", act="d_act", pool="d_pool", dve="d_dve")


class Buf:
    __slots__ = ("name", "w", "readers", "pending")

    def __init__(self, name):
        self.name = name
        self.w = None
        self.readers = []
        self.pending = []


class Prog:
    def __init__(self, nc, es):
        self.nc = nc
        self.es = es
        self.q = dict(pe=nc.tensor, act=nc.scalar, dve=nc.vector,
                      pool=nc.gpsimd, sp=nc.sync)
        self.sem = {}
        self.cnt = {}
        self.waited = {e: {} for e in self.q}
        for name in ("pe", "act", "dve", "pool", "sp",
                     "d_sp", "d_act", "d_pool", "d_dve", "cc"):
            self.sem[name] = es.enter_context(nc.semaphore(name))
            self.cnt[name] = 0

    def getsem(self, name):
        if name not in self.sem:
            self.sem[name] = self.es.enter_context(self.nc.semaphore(name))
            self.cnt[name] = 0
        return self.sem[name]

    def mark(self, instr, sem, inc):
        instr.then_inc(self.sem[sem], inc)
        self.cnt[sem] += inc
        return (sem, self.cnt[sem])

    def wait(self, eng, tok):
        if tok is None:
            return
        sem, v = tok
        if self.waited[eng].get(sem, 0) >= v:
            return
        self.q[eng].wait_ge(self.sem[sem], v)
        self.waited[eng][sem] = v

    def _pre(self, eng, ins, outs):
        need = {}
        def add(tok):
            if tok is not None and tok[1] > need.get(tok[0], -1):
                need[tok[0]] = tok[1]
        for b in ins:
            add(b.w)
        for b in outs:
            add(b.w)
            for t in b.readers:
                add(t)
        for sem, v in need.items():
            self.wait(eng, (sem, v))

    def _post(self, tok, ins, outs):
        for b in ins:
            b.readers.append(tok)
        for b in outs:
            b.w = tok
            b.readers = []

    def op(self, eng, emit, ins=(), outs=(), sem=None, inc=1):
        self._pre(eng, ins, outs)
        instr = emit(self.q[eng])
        tok = self.mark(instr, sem or ENG_SEM[eng], inc)
        self._post(tok, ins, outs)
        return tok

    def dma(self, qname, out_ap, in_ap, ins=(), outs=(), dsem=None, **kw):
        if dsem is not None:
            self.getsem(dsem)
        return self.op(qname, lambda q: q.dma_start(out=out_ap, in_=in_ap, **kw),
                       ins=ins, outs=outs, sem=dsem or DMA_SEM[qname], inc=16)

    def mm_group(self, mms, ins=(), outs=()):
        self._pre("pe", ins, outs)
        n = len(mms)
        instr = None
        for i, (o, l, r) in enumerate(mms):
            instr = self.q["pe"].matmul(o, l, r, start=(i == 0), stop=(i == n - 1))
        tok = self.mark(instr, "pe", 1)
        self._post(tok, ins, outs)
        return tok

    def acc_mm(self, pbuf, first, last, emit, ins, force_mark=False):
        if first:
            self._pre("pe", ins, [pbuf])
        else:
            self._pre("pe", ins, [])
        instr = emit(self.q["pe"])
        pbuf.pending.extend(ins)
        if last or force_mark:
            tok = self.mark(instr, "pe", 1)
            for b in pbuf.pending:
                b.readers.append(tok)
            pbuf.pending = []
            if last:
                pbuf.w = tok
                pbuf.readers = []
            return tok
        return None

    def barrier(self):
        toks = [(s, self.cnt[s]) for s in self.cnt if self.cnt[s] > 0]
        for eng in self.q:
            for t in toks:
                self.wait(eng, t)


def build_program(dbg=(), upto=None):
    nc = bass.Bass("TRN2", target_bir_lowering=False)
    dbg = set(dbg)

    def din(name, shape, dt=F32):
        return nc.dram_tensor(name, list(shape), dt, kind="ExternalInput")

    x_in = din("x_local", (T, D))
    wq = din("ffn_wq", (D, D))
    wk = din("ffn_wk", (D, 256))
    wv = din("ffn_wv", (D, 256))
    wo = din("ffn_wo", (D, D))
    gate = din("ffn_gate", (D, NE))
    w1 = din("ffn_w1", (NE * D, HID))
    w2 = din("ffn_w2", (NE * HID, D))
    w3 = din("ffn_w3", (NE * D, HID))
    w_in = din("ab_w_in", (D, 3 * D))
    w_out = din("ab_w_out", (D, D))
    b_in = din("ab_b_in", (1, 3 * D))
    b_out = din("ab_b_out", (1, D))
    g1v = din("ffn_g1", (1, D))
    g2v = din("ffn_g2", (1, D))
    gammav = din("ab_gamma", (1, D))
    betav = din("ab_beta", (1, D))
    tabCf = din("tabC_ffn", (128, T))
    tabSf = din("tabS_ffn", (128, T))
    tabCa = din("tabC_ab", (128, T))
    tabSa = din("tabS_ab", (128, T))
    permF = din("perm_ffn", (128, 128))
    permA = din("perm_ab", (128, 128))
    masks_in = din("masks", (4 * 128, 512))
    ident_in = din("ident", (128, 128))
    onesb_in = din("ones_bf", (1, 128), BF16)
    onescol_in = din("ones_col", (128, 1), BF16)
    cvec_in = din("cvec", (128, 8))
    vmask_in = din("vmask", (128, 64))

    out_t = nc.dram_tensor("out_local", [T, D], F32, kind="ExternalOutput")

    r1 = nc.dram_tensor("r1", [T, D], F32)
    r2 = nc.dram_tensor("r2", [T, D], F32)
    ys = nc.dram_tensor("ystream", [T, D], F32)
    h2s = nc.dram_tensor("h2stream", [T, D], F32)
    kstg_f = nc.dram_tensor("kstg_f", [256, T], BF16)
    kg_f = nc.dram_tensor("kg_f", [1024, T], BF16)
    vstg_f = nc.dram_tensor("vstg_f", [512, 1040], BF16)
    vg_f = nc.dram_tensor("vg_f", [2048, 1040], BF16)
    kstg_a = nc.dram_tensor("kstg_a", [1024, T], BF16)
    kg_a = nc.dram_tensor("kg_a", [4096, T], BF16)
    vstg_a = nc.dram_tensor("vstg_a", [1024, 2080], BF16)
    vg_a = nc.dram_tensor("vg_a", [4096, 2080], BF16)
    kms_f = nc.dram_tensor("kms_f", [1, NKH], F32)
    kmg_f = nc.dram_tensor("kmg_f", [4, NKH], F32)
    kms_a = nc.dram_tensor("kms_a", [1, NABH], F32)
    kmg_a = nc.dram_tensor("kmg_a", [4, NABH], F32)
    w1b = nc.dram_tensor("w1b", [NE * D, HID], BF16)
    w2b = nc.dram_tensor("w2b", [NE * HID, D], BF16)
    w3b = nc.dram_tensor("w3b", [NE * D, HID], BF16)


    dump_outs = {}

    def dbg_out(name, shape, dt):
        if name in dbg and name not in dump_outs:
            dump_outs[name] = nc.dram_tensor(f"dbg_{name}", list(shape), dt,
                                             kind="ExternalOutput")
        return dump_outs.get(name)

    es = ExitStack()
    with es:
        p = Prog(nc, es)
        _uid = [0]

        def _nm(n):
            _uid[0] += 1
            return f"{n}_{_uid[0]}"

        sb = lambda name, shape, dt: es.enter_context(
            nc.sbuf_tensor(name, list(shape), dt))

        xnT = sb("xnT", (128, 16 * 1024), BF16)
        smv = sb("smv", (128, 4096), F32)
        msc = sb("msc", (128, 4 * 1024), F32)
        identf = sb("identf", (128, 128), F32)
        identb = sb("identb", (128, 128), BF16)
        permFs = sb("permFs", (128, 128), F32)
        permAs = sb("permAs", (128, 128), F32)
        onesb = sb("onesb", (1, 512), BF16)
        onescol = sb("onescol", (128, 1), BF16)
        parms = sb("parms", (128, 32), F32)
        cbias = sb("cbias", (128, 16), F32)
        cvec = sb("cvec_s", (128, 8), F32)
        vmsk = sb("vmsk", (128, 64), F32)
        rrb = sb("rrb", (1, 512), BF16)

        bq = {}

        def grid(name, *dims):
            import itertools
            if not dims:
                bq[(name,)] = Buf(name)
                return
            for idx in itertools.product(*[range(d) for d in dims]):
                bq[(name,) + idx] = Buf(f"{name}{idx}")

        def g(name, *idx):
            return bq[(name,) + tuple(idx)]

        grid("xnT", 8, NT)
        grid("qT", 8, NT)
        grid("msc", 4)
        grid("smv")
        grid("q2s")
        grid("cbias")
        grid("con")
        grid("parms")
        grid("rrb")
        for t_ in ("x_in", "r1", "r2", "ys", "h2s", "kstg_f", "kg_f", "vstg_f",
                   "vg_f", "kstg_a", "kg_a", "vstg_a", "vg_a", "kms_f", "kmg_f",
                   "kms_a", "kmg_a", "out"):
            grid(t_)

        con = g("con")
        rrbB = g("rrb")
        for t_ in ("w1b", "w2b", "w3b"):
            grid(t_)
        moe_cast_done = [False]

        def emit_moe_casts():
            if moe_cast_done[0]:
                return
            moe_cast_done[0] = True
            for srcT, dstT, nel, bname, dsm in (
                    (w1, w1b, NE * D * HID, "w1b", "dc1"),
                    (w3, w3b, NE * D * HID, "w3b", "dc3"),
                    (w2, w2b, NE * HID * D, "w2b", "dc2")):
                nchunk = 8
                step = nel // nchunk
                for ci2 in range(nchunk):
                    p.dma("pool",
                          bass.AP(dstT, ci2 * step, [[step, 1], [1, step]]),
                          bass.AP(srcT, ci2 * step, [[step, 1], [1, step]]),
                          outs=[g(bname)], dsem=dsm)

        for o, i in [(identf[:, :], ident_in.ap()),
                     (permFs[:, :], permF.ap()),
                     (permAs[:, :], permA.ap()),
                     (onesb[0:1, 0:128], onesb_in.ap()),
                     (onescol[:, :], onescol_in.ap()),
                     (cvec[:, :], cvec_in.ap()),
                     (vmsk[:, :], vmask_in.ap())]:
            p.dma("sp", o, i, outs=[con])
        p.dma("pool", identb[:, :], ident_in.ap(), outs=[con])
        for ci, v in enumerate((g1v, g2v, gammav, betav)):
            for c in range(8):
                p.dma("sp", parms[:, ci * 8 + c: ci * 8 + c + 1],
                      bass.AP(v, c * 128, [[1, 128], [1, 1]]), outs=[g("parms")])
        for i in (1, 2, 3):
            p.op("dve", lambda q, i=i: q.tensor_copy(
                onesb[0:1, i * 128:(i + 1) * 128], onesb[0:1, 0:128]),
                ins=[con], outs=[con])
        nc.const_aps.register(F32, EPS_LN, cvec[:, 0:1])
        nc.const_aps.register(F32, EPS_RMS, cvec[:, 1:2])
        nc.const_aps.register(F32, 1.0, cvec[:, 2:3])
        nc.const_aps.register(F32, 0.0, cvec[:, 3:4])
        p.barrier()

        final_toks = []

        def maybe_dump(name, ap, shape, dt, inbufs):
            d = dbg_out(name, shape, dt)
            if d is not None:
                final_toks.append(p.dma("sp", d.ap(), ap, ins=list(inbufs)))

        def maybe_dump_dram(name, src_dram, src_buf):
            d = dbg_out(name, (T, D), F32)
            if d is not None:
                final_toks.append(p.dma("sp", d.ap(), src_dram.ap(),
                                        ins=[src_buf]))

        def ln_rms_tile(src_buf, src_ap, dst_ap, dst_buf, mode, ri=0,
                        x2ap=None, x2=None):
            if x2ap is None:
                x2 = g("msc", 3)
                x2ap = msc[:, 3 * 1024:4 * 1024]
            sv = g("smv")
            C0 = ri * 8
            smc = lambda i: smv[:, C0 + i:C0 + i + 1]
            p.op("act", lambda q: q.activation(x2ap, src_ap, AF.Square),
                 ins=[src_buf], outs=[x2])
            p.op("dve", lambda q: q.tensor_reduce(smc(0), x2ap, AX.X,
                                                  ALU.add), ins=[x2], outs=[sv])
            if mode == "ln":
                p.op("dve", lambda q: q.tensor_reduce(smc(1), src_ap, AX.X,
                                                      ALU.add),
                     ins=[src_buf], outs=[sv])
                p.op("dve", lambda q: q.tensor_scalar(smc(1), smc(1),
                                                      1.0 / D, None, ALU.mult),
                     ins=[sv], outs=[sv])
                p.op("dve", lambda q: q.tensor_tensor(smc(2), smc(1),
                                                      smc(1), ALU.mult),
                     ins=[sv], outs=[sv])
                p.op("dve", lambda q: q.tensor_scalar(smc(3), smc(0),
                                                      1.0 / D, None, ALU.mult),
                     ins=[sv], outs=[sv])
                p.op("dve", lambda q: q.tensor_tensor(smc(2), smc(3),
                                                      smc(2), ALU.subtract),
                     ins=[sv], outs=[sv])
                var_col, eps = smc(2), EPS_LN
            else:
                p.op("dve", lambda q: q.tensor_scalar(smc(2), smc(0),
                                                      1.0 / D, None, ALU.mult),
                     ins=[sv], outs=[sv])
                var_col, eps = smc(2), EPS_RMS
            p.op("act", lambda q: q.activation(smc(4), var_col, AF.Sqrt,
                                               bias=eps), ins=[sv], outs=[sv])
            p.op("dve", lambda q: q.reciprocal(smc(5), smc(4)),
                 ins=[sv], outs=[sv])
            if mode == "ln":
                p.op("dve", lambda q: q.tensor_tensor(smc(6), smc(1),
                                                      smc(5), ALU.mult),
                     ins=[sv], outs=[sv])
                p.op("dve", lambda q: q.tensor_scalar(smc(6), smc(6),
                                                      -1.0, None, ALU.mult),
                     ins=[sv], outs=[sv])
                p.op("dve", lambda q: q.tensor_scalar(dst_ap, src_ap,
                                                      smc(5), smc(6),
                                                      ALU.mult, ALU.add),
                     ins=[src_buf, sv], outs=[dst_buf])
            else:
                p.op("dve", lambda q: q.tensor_scalar(dst_ap, src_ap,
                                                      smc(5), None,
                                                      ALU.mult),
                     ins=[src_buf, sv], outs=[dst_buf])

        def norm_transpose_phase(src_dram, src_buf, kind, store_stream=None,
                                 store_buf=None):
            with ExitStack() as ph:
                pt2 = [ph.enter_context(nc.psum_tensor(_nm("p_tr"), [128, 512],
                                                       BF16)) for _ in (0, 1)]
                zt2 = [ph.enter_context(nc.sbuf_tensor(_nm("zt"), [128, 1024],
                                                       BF16)) for _ in (0, 1)]
                x2b = ph.enter_context(nc.sbuf_tensor(_nm("x2b"), [128, 2048],
                                                      F32))
                ptb2 = [Buf("p_tr0"), Buf("p_tr1")]
                ztb2 = [Buf("zt0"), Buf("zt1")]
                x2B = [Buf("x2b0"), Buf("x2b1")]
                gcol0 = {"ffn": 0, "ffn2": 8, "ab": 16}[kind]
                for tt in range(NTT):
                    ri = tt % 2
                    pt, zt, ptb, ztb = pt2[ri], zt2[ri], ptb2[ri], ztb2[ri]
                    x2ap_ = x2b[:, ri * 1024:(ri + 1) * 1024]
                    xt, xt_ap = g("msc", 2 * ri), msc[:, 2 * ri * 1024:
                                                      (2 * ri + 1) * 1024]
                    yt, yt_ap = g("msc", 2 * ri + 1), msc[:, (2 * ri + 1) * 1024:
                                                          (2 * ri + 2) * 1024]
                    p.dma("sp", xt_ap,
                          bass.AP(src_dram, tt * 128 * D, [[D, 128], [1, D]]),
                          ins=[src_buf], outs=[xt], dsem=f"dm{ri}")
                    kw = dict(ri=ri, x2ap=x2ap_, x2=x2B[ri])
                    if kind == "ffn":
                        ln_rms_tile(xt, xt_ap, yt_ap, yt, "ln", **kw)
                        if store_stream is not None:
                            p.dma("sp", bass.AP(store_stream, tt * 128 * D,
                                                [[D, 128], [1, D]]), yt_ap,
                                  ins=[yt], outs=[store_buf],
                                  dsem=f"ds_ys{ri}")
                        ln_rms_tile(yt, yt_ap, zt[:, :], ztb, "rms", **kw)
                    elif kind == "ffn2":
                        ln_rms_tile(xt, xt_ap, zt[:, :], ztb, "rms", **kw)
                    else:
                        ln_rms_tile(xt, xt_ap, yt_ap, yt, "ln", **kw)
                        if store_stream is not None:
                            p.dma("sp", bass.AP(store_stream, tt * 128 * D,
                                                [[D, 128], [1, D]]), yt_ap,
                                  ins=[yt], outs=[store_buf],
                                  dsem=f"ds_h2{ri}")
                        ln_rms_tile(yt, yt_ap, zt[:, :], ztb, "ln", **kw)
                    qt, sub = divmod(tt, 4)
                    for c in range(8):
                        p.op("pe", lambda q, c=c: q.transpose(
                            pt[:, 0:128], zt[:, c * 128:(c + 1) * 128],
                            identb[:, :]), ins=[ztb, con], outs=[ptb])
                        dst = xnT[:, c * 2048 + qt * 512 + sub * 128:
                                  c * 2048 + qt * 512 + sub * 128 + 128]
                        if kind == "ab":
                            p.op("dve", lambda q, c=c, dst=dst: q.tensor_scalar(
                                dst, pt[:, 0:128], parms[:, 16 + c:17 + c],
                                parms[:, 24 + c:25 + c], ALU.mult, ALU.add),
                                ins=[ptb, g("parms")], outs=[g("xnT", c, qt)])
                        else:
                            p.op("dve", lambda q, c=c, dst=dst: q.tensor_scalar(
                                dst, pt[:, 0:128],
                                parms[:, gcol0 + c:gcol0 + c + 1], None,
                                ALU.mult),
                                ins=[ptb, g("parms")], outs=[g("xnT", c, qt)])
                p.barrier()

        KM2, QM2, KH2, QH2 = 0, 20, 40, 110
        KG16, KMX, KMAP, QP2, KP2, CRW = 200, 266, 290, 310, 330, 350
        RECR = 512

        def attn_invocation(tag):
            ffn = tag == "ffn"
            perm_s = permFs if ffn else permAs
            kstg, vstg, kms, kmg = ((kstg_f, vstg_f, kms_f, kmg_f) if ffn else
                                    (kstg_a, vstg_a, kms_a, kmg_a))
            kg_, vg_ = (kg_f, vg_f) if ffn else (kg_a, vg_a)
            kstgB = g("kstg_f" if ffn else "kstg_a")
            vstgB = g("vstg_f" if ffn else "vstg_a")
            kmsB = g("kms_f" if ffn else "kms_a")
            kmgB = g("kmg_f" if ffn else "kmg_a")
            kgB = g("kg_f" if ffn else "kg_a")
            vgB = g("vg_f" if ffn else "vg_a")
            slot_of = SLOT_OF_F if ffn else {h: h for h in range(16)}
            nkh = 4 if ffn else 16

            with ExitStack() as inv:
                sbi = lambda n, s, d: inv.enter_context(
                    nc.sbuf_tensor(_nm(n), list(s), d))
                qT = sbi("qT", (128, 16 * 1024), BF16)
                if ffn:
                    klocal = sbi("klocal", (128, 2 * T), BF16)
                    vlocal = sbi("vlocal", (128, 4 * 1040), BF16)
                    klB, vlB = Buf("klocal"), Buf("vlocal")
                    cons_masks = sbi("cmask", (128, 4 * 512), BF16)
                    cmB = Buf("cmask")
                    p.dma("pool",
                          cons_masks[:, :].rearrange("p (r c) -> p r c", r=4),
                          bass.AP(masks_in, 0,
                                  [[512, 128], [128 * 512, 4], [1, 512]]),
                          outs=[cmB])
                crowb = sbi("crowb", (1, 16), BF16)
                q2sb = [sbi(f"q2sb{i}", (128, 512), BF16) for i in (0, 1)]
                crB = Buf("crowb")
                q2B = [Buf("q2s0"), Buf("q2s1")]
                for oc_ in range(8):
                    for qt_ in range(NT):
                        bq[("qT", oc_, qt_)] = Buf(f"qT{oc_},{qt_}")

                with ExitStack() as ph:
                    sbp = lambda n, s, d: ph.enter_context(
                        nc.sbuf_tensor(_nm(n), list(s), d))
                    psp = lambda n, s, d=F32: ph.enter_context(
                        nc.psum_tensor(_nm(n), list(s), d))
                    wcol = [sbp(f"wcol{i}", (128, 8 * 512), BF16)
                            for i in (0, 1)]
                    tabC = sbp("tabC", (128, T), F32)
                    tabS = sbp("tabS", (128, T), F32)
                    kslot = sbp("kslot", (128, T), BF16)
                    vslot = sbp("vslot", (128, 16 * 130), BF16)
                    qraw = [sbp(f"qraw{i}", (128, 512), F32) for i in (0, 1)]
                    t1s = [sbp(f"t1s{i}", (128, 512), F32) for i in (0, 1)]
                    t2s = [sbp(f"t2s{i}", (128, 512), F32) for i in (0, 1)]
                    vtmp = [sbp(f"vtmp{i}", (128, 512), BF16) for i in (0, 1)]
                    binb = sbp("binb", (1, 3 * D), BF16) if not ffn else None
                    pq = [psp(f"pq{i}", (128, 512)) for i in (0, 1)]
                    ppm = [psp(f"ppm{i}", (128, 512)) for i in (0, 1)]
                    ptv = psp("ptv", (128, 512), BF16)
                    p1r = [psp(f"p1r{i}", (1, 512)) for i in (0, 1)]
                    pcb = psp("pcb", (128, 16))
                    wcB = [[Buf(f"wc{i}_{k}") for k in range(8)]
                           for i in (0, 1)]
                    tabB = Buf("tab")
                    kslB, vslB = Buf("kslot"), Buf("vslot")
                    qrB = [Buf("qraw0"), Buf("qraw1")]
                    t1B = [Buf("t1s0"), Buf("t1s1")]
                    t2B = [Buf("t2s0"), Buf("t2s1")]
                    vtB = [Buf("vtmp0"), Buf("vtmp1")]
                    binB = Buf("binb")
                    pqB = [Buf("pq0"), Buf("pq1")]
                    ppmB = [Buf("ppm0"), Buf("ppm1")]
                    ptvB, pcbB = Buf("ptv"), Buf("pcb")
                    p1B = [Buf("p1r0"), Buf("p1r1")]

                    p.dma("sp", tabC[:, :], (tabCf if ffn else tabCa).ap(),
                          outs=[tabB], dsem="dtab")
                    p.dma("sp", tabS[:, :], (tabSf if ffn else tabSa).ap(),
                          outs=[tabB], dsem="dtab")
                    if not ffn:
                        p.dma("pool", binb[:, :], b_in.ap(), outs=[binB])

                    def load_wcols(wb_i, wt, wt_cols, c0, width):
                        for kc in range(8):
                            p.dma("pool",
                                  wcol[wb_i][:, kc * width:(kc + 1) * width],
                                  bass.AP(wt, kc * 128 * wt_cols + c0,
                                          [[wt_cols, 128], [1, width]]),
                                  outs=[wcB[wb_i][kc]], dsem=f"dw{wb_i}")

                    def proj_group(pb, pbuf, wb_i, width, occ, qt, bias_col):
                        mms = []
                        ins = list(wcB[wb_i]) + [con] + [g("xnT", kc, qt)
                                                         for kc in range(8)]
                        if bias_col is not None:
                            mms.append((pb[:, :],
                                        binb[0:1, bias_col:bias_col + 128],
                                        onesb[0:1, 0:512]))
                            ins.append(binB)
                        for kc in range(8):
                            mms.append((pb[:, :],
                                        wcol[wb_i][:, kc * width + occ * 128:
                                                   kc * width + occ * 128 + 128],
                                        xnT[:, kc * 2048 + qt * 512:
                                            kc * 2048 + qt * 512 + 512]))
                        p.mm_group(mms, ins=ins, outs=[pbuf])

                    def rope_out(pb, pbuf, qt, dst_ap, dst_buf, ri):
                        ri = ri if ROT_ROPE else 0
                        p.op("act", lambda q: q.copy(qraw[ri][:, :], pb[:, :]),
                             ins=[pbuf], outs=[qrB[ri]])
                        p.op("pe", lambda q: q.matmul(ppm[ri][:, :],
                                                      perm_s[:, :],
                                                      qraw[ri][:, :],
                                                      start=True, stop=True),
                             ins=[qrB[ri], con], outs=[ppmB[ri]])
                        p.op("dve", lambda q: q.tensor_tensor(
                            t1s[ri][:, :], qraw[ri][:, :],
                            tabC[:, qt * 512:(qt + 1) * 512], ALU.mult),
                            ins=[qrB[ri], tabB], outs=[t1B[ri]])
                        p.op("dve", lambda q: q.tensor_tensor(
                            t2s[ri][:, :], ppm[ri][:, :],
                            tabS[:, qt * 512:(qt + 1) * 512], ALU.mult),
                            ins=[ppmB[ri], tabB], outs=[t2B[ri]])
                        p.op("dve", lambda q, dst_ap=dst_ap: q.tensor_tensor(
                            dst_ap, t1s[ri][:, :], t2s[ri][:, :], ALU.add),
                            ins=[t1B[ri], t2B[ri]], outs=[dst_buf])

                    def sqmax(src_ap, src_buf, col_even, col_odd, ri):
                        ri = ri if ROT_ROPE else 0
                        p.op("act", lambda q: q.activation(q2sb[ri][:, :],
                                                           src_ap, AF.Square),
                             ins=[src_buf], outs=[q2B[ri]])
                        for half2, (base, colx) in enumerate(
                                ((0, col_even), (64, col_odd))):
                            p.op("pe", lambda q, base=base, half2=half2:
                                 q.matmul(
                                     p1r[half2][0:1, :],
                                     onescol[base:base + 64, 0:1],
                                     q2sb[ri][base:base + 64, :], start=True,
                                     stop=True),
                                 ins=[q2B[ri], con], outs=[p1B[half2]])
                            p.op("dve", lambda q, colx=colx, half2=half2:
                                 q.tensor_reduce(
                                     smv[0:1, colx:colx + 1],
                                     p1r[half2][0:1, :], AX.X,
                                     ALU.max), ins=[p1B[half2]],
                                 outs=[g("smv")])

                    for blk in range(2):
                        wb_i = blk % 2
                        load_wcols(wb_i, wq if ffn else w_in,
                                   D if ffn else 3 * D, blk * 512, 512)
                        for occ in range(4):
                            oc = blk * 4 + occ
                            for qt in range(NT):
                                bb = (oc * NT + qt) % 2
                                proj_group(pq[bb], pqB[bb], wb_i, 512, occ, qt,
                                           None if ffn else oc * 128)
                                dst = qT[:, oc * 2048 + qt * 512:
                                         oc * 2048 + qt * 512 + 512]
                                rope_out(pq[bb], pqB[bb], qt, dst,
                                         g("qT", oc, qt), bb)
                                sqmax(dst, g("qT", oc, qt),
                                      QH2 + (2 * oc) * 4 + qt,
                                      QH2 + (2 * oc + 1) * 4 + qt, bb)
                    for s_ in range(16):
                        p.op("dve", lambda q, s_=s_: q.tensor_reduce(
                            smv[0:1, QM2 + s_:QM2 + s_ + 1],
                            smv[0:1, QH2 + s_ * 4:QH2 + s_ * 4 + 4], AX.X,
                            ALU.max), ins=[g("smv")], outs=[g("smv")])

                    kwidth = 256 if ffn else 512
                    kblocks = 1 if ffn else 2
                    for blk in range(kblocks):
                        wb_i = blk % 2
                        if ffn:
                            load_wcols(wb_i, wk, 256, 0, 256)
                        else:
                            load_wcols(wb_i, w_in, 3 * D, 1024 + blk * 512, 512)
                        for occ in range(2 if ffn else 4):
                            oc = blk * 4 + occ
                            for qt in range(NT):
                                bb = (oc * NT + qt) % 2
                                proj_group(pq[bb], pqB[bb], wb_i, kwidth, occ,
                                           qt, None if ffn else 1024 + oc * 128)
                                dst = kslot[:, qt * 512:(qt + 1) * 512]
                                rope_out(pq[bb], pqB[bb], qt, dst, kslB, bb)
                                sqmax(dst, kslB, KH2 + (2 * oc) * 4 + qt,
                                      KH2 + (2 * oc + 1) * 4 + qt, bb)
                            for half in (0, 1):
                                kh = 2 * oc + half
                                p.op("dve", lambda q, kh=kh: q.tensor_reduce(
                                    smv[0:1, KM2 + kh:KM2 + kh + 1],
                                    smv[0:1, KH2 + kh * 4:KH2 + kh * 4 + 4],
                                    AX.X, ALU.max), ins=[g("smv")],
                                    outs=[g("smv")])
                            p.dma("sp", bass.AP(kstg, oc * 128 * T,
                                                [[T, 128], [1, T]]),
                                  kslot[:, :], ins=[kslB], outs=[kstgB],
                                  dsem="dstg_k")
                            if ffn:
                                p.op("dve", lambda q, oc=oc: q.tensor_copy(
                                    klocal[:, oc * T:(oc + 1) * T],
                                    kslot[:, :]), ins=[kslB], outs=[klB])

                    p.dma("sp", kms.ap(), smv[0:1, KM2:KM2 + nkh],
                          ins=[g("smv")], outs=[kmsB], dsem="dstg_m")
                    p.op("pool", lambda q: q.collective_compute(
                        "AllGather", ALU.bypass, replica_groups=CC_GROUPS,
                        ins=[kstg.ap().opt()], outs=[kg_.ap().opt()]),
                        ins=[kstgB], outs=[kgB], sem="cc", inc=1)
                    p.op("pool", lambda q: q.collective_compute(
                        "AllGather", ALU.bypass, replica_groups=CC_GROUPS,
                        ins=[kms.ap().opt()], outs=[kmg.ap().opt()]),
                        ins=[kmsB], outs=[kmgB], sem="cc", inc=1)

                    for blk in range(kblocks):
                        wb_i = blk % 2
                        if ffn:
                            load_wcols(wb_i, wv, 256, 0, 256)
                        else:
                            load_wcols(wb_i, w_in, 3 * D, 2048 + blk * 512, 512)
                        for occ in range(2 if ffn else 4):
                            oc = blk * 4 + occ
                            p.op("dve", lambda q: q.memset(vslot[:, :], 1.0),
                                 ins=[], outs=[vslB])
                            for qt in range(NT):
                                bb = (oc * NT + qt) % 2
                                proj_group(pq[bb], pqB[bb], wb_i, kwidth, occ,
                                           qt, None if ffn else 2048 + oc * 128)
                                p.op("act", lambda q, bb=bb: q.copy(
                                    vtmp[bb][:, :], pq[bb][:, :]),
                                    ins=[pqB[bb]], outs=[vtB[bb]])
                                for j in range(4):
                                    tt = qt * 4 + j
                                    p.op("pe", lambda q, j=j, bb=bb: q.transpose(
                                        ptv[:, 0:128],
                                        vtmp[bb][:, j * 128:(j + 1) * 128],
                                        identb[:, :]), ins=[vtB[bb], con],
                                        outs=[ptvB])
                                    for half in (0, 1):
                                        c0 = ((half * 1040 + tt * 65) if ffn
                                              else (tt * 130 + half * 65))
                                        p.op("dve",
                                             lambda q, c0=c0, half=half:
                                             q.tensor_copy(
                                                 vslot[:, c0:c0 + 64],
                                                 ptv[:, half * 64:
                                                     half * 64 + 64]),
                                             ins=[ptvB], outs=[vslB])
                            if ffn:
                                for half in (0, 1):
                                    kh = 2 * oc + half
                                    p.dma("sp",
                                          bass.AP(vstg, kh * 128 * 1040,
                                                  [[1040, 128], [1, 1040]]),
                                          vslot[:, half * 1040:
                                                (half + 1) * 1040],
                                          ins=[vslB], outs=[vstgB],
                                          dsem=f"dstg_v{half}")
                                    p.op("dve",
                                         lambda q, kh=kh, half=half:
                                         q.tensor_copy(
                                             vlocal[:, kh * 1040:
                                                    (kh + 1) * 1040],
                                             vslot[:, half * 1040:
                                                   (half + 1) * 1040]),
                                         ins=[vslB], outs=[vlB])
                            else:
                                p.dma("sp", bass.AP(vstg, oc * 128 * 2080,
                                                    [[2080, 128], [1, 2080]]),
                                      vslot[:, :], ins=[vslB], outs=[vstgB],
                                      dsem=f"dstg_v{oc % 2}")
                    p.op("pool", lambda q: q.collective_compute(
                        "AllGather", ALU.bypass, replica_groups=CC_GROUPS,
                        ins=[vstg.ap().opt()], outs=[vg_.ap().opt()]),
                        ins=[vstgB], outs=[vgB], sem="cc", inc=1)

                    p.dma("sp", smv[0:1, KG16:KG16 + 4 * nkh],
                          bass.AP(kmg, 0, [[4 * nkh, 1], [1, 4 * nkh]]),
                          ins=[kmgB], outs=[g("smv")], dsem="dsm")
                    p.op("dve", lambda q: q.tensor_tensor(
                        smv[0:1, KMX:KMX + nkh], smv[0:1, KG16:KG16 + nkh],
                        smv[0:1, KG16 + nkh:KG16 + 2 * nkh], ALU.max),
                        ins=[g("smv")], outs=[g("smv")])
                    for r_ in (2, 3):
                        p.op("dve", lambda q, r_=r_: q.tensor_tensor(
                            smv[0:1, KMX:KMX + nkh], smv[0:1, KMX:KMX + nkh],
                            smv[0:1, KG16 + r_ * nkh:KG16 + (r_ + 1) * nkh],
                            ALU.max), ins=[g("smv")], outs=[g("smv")])
                    for s_ in range(16):
                        kh = ((FFN_QORD[s_] // 4) if ffn else s_)
                        p.op("dve", lambda q, s_=s_, kh=kh: q.tensor_scalar(
                            smv[0:1, KMAP + s_:KMAP + s_ + 1],
                            smv[0:1, KMAP + s_:KMAP + s_ + 1], 0.0,
                            smv[0:1, KMX + kh:KMX + kh + 1], ALU.mult,
                            ALU.add), ins=[g("smv")], outs=[g("smv")])
                    pair_slots = []
                    if ffn:
                        for kh in range(4):
                            for u in (0, 1):
                                pair_slots.append((slot_of[4 * kh + 2 * u],
                                                   slot_of[4 * kh + 2 * u + 1]))
                    else:
                        for pr in range(8):
                            pair_slots.append((2 * pr, 2 * pr + 1))
                    for pp, (sa, sb_) in enumerate(pair_slots):
                        p.op("dve",
                             lambda q, pp=pp, sa=sa, sb_=sb_: q.tensor_tensor(
                                 smv[0:1, QP2 + pp:QP2 + pp + 1],
                                 smv[0:1, QM2 + sa:QM2 + sa + 1],
                                 smv[0:1, QM2 + sb_:QM2 + sb_ + 1], ALU.max),
                             ins=[g("smv")], outs=[g("smv")])
                        p.op("dve",
                             lambda q, pp=pp, sa=sa, sb_=sb_: q.tensor_tensor(
                                 smv[0:1, KP2 + pp:KP2 + pp + 1],
                                 smv[0:1, KMAP + sa:KMAP + sa + 1],
                                 smv[0:1, KMAP + sb_:KMAP + sb_ + 1], ALU.max),
                             ins=[g("smv")], outs=[g("smv")])
                    p.op("dve", lambda q: q.tensor_tensor(
                        smv[0:1, CRW:CRW + 8], smv[0:1, QP2:QP2 + 8],
                        smv[0:1, KP2:KP2 + 8], ALU.mult),
                        ins=[g("smv")], outs=[g("smv")])
                    p.op("act", lambda q: q.activation(
                        smv[0:1, CRW:CRW + 8], smv[0:1, CRW:CRW + 8],
                        AF.Sqrt, bias=0.0), ins=[g("smv")], outs=[g("smv")])
                    p.op("dve", lambda q: q.tensor_scalar(
                        smv[0:1, CRW:CRW + 8], smv[0:1, CRW:CRW + 8], -8.0,
                        None, ALU.mult), ins=[g("smv")], outs=[g("smv")])
                    p.op("dve", lambda q: q.tensor_copy(
                        crowb[0:1, 0:8], smv[0:1, CRW:CRW + 8]),
                        ins=[g("smv")], outs=[crB])
                    for pp in range(8):
                        p.op("pe", lambda q, pp=pp: q.matmul(
                            pcb[:, pp:pp + 1], onesb[0:1, 0:128],
                            crowb[0:1, pp:pp + 1], start=True, stop=True),
                            ins=[crB, con], outs=[pcbB])
                    p.op("dve", lambda q: q.tensor_copy(cbias[:, 0:8],
                                                        pcb[:, 0:8]),
                         ins=[pcbB], outs=[g("cbias")])
                    p.barrier()

                with ExitStack() as ph:
                    sbp = lambda n, s, d: ph.enter_context(
                        nc.sbuf_tensor(_nm(n), list(s), d))
                    psp = lambda n, s, d=F32: ph.enter_context(
                        nc.psum_tensor(_nm(n), list(s), d))
                    kwin = [sbp(f"kwin{i}", (128, 2048), BF16) for i in (0, 1)]
                    vwin = [sbp(f"vwin{i}", (128, 2080), BF16) for i in (0, 1)]
                    pP = sbp("pPex", (128, 3 * 1024), BF16)
                    ctxn = sbp("ctxn", (128, 8 * 512), BF16)
                    wob = sbp("wob", (128, 8 * 1024), BF16)
                    dsc = sbp("dsc", (128, 512), F32)
                    dts = sbp("dts", (128, 4 * 1024), F32)
                    rbc = sbp("rbc", (128, 512), F32)
                    boutb = sbp("boutb", (1, D), BF16) if not ffn else None
                    pS = [psp(f"pS{i}", (128, 1024)) for i in (0, 1)]
                    pAV = [psp(f"pAV{i}", (128, 512)) for i in range(4)]
                    kwinB = [Buf("kw0"), Buf("kw1")]
                    vwinB = [Buf("vw0"), Buf("vw1")]
                    pPB = [Buf("pP0"), Buf("pP1"), Buf("pP2")]
                    wobB, dscB, rbcB = Buf("wob"), Buf("dsc"), Buf("rbc")
                    boutB = Buf("boutb")
                    pSB = [[Buf("pS00"), Buf("pS01")],
                           [Buf("pS10"), Buf("pS11")]]
                    pAVB = [Buf(f"pAV{i}") for i in range(4)]
                    dtsB = [Buf(f"dts{j}") for j in range(NT)]
                    ctxB = [Buf(f"ctx{c}") for c in range(8)]

                    for kc in range(8):
                        p.dma("pool", wob[:, kc * 1024:(kc + 1) * 1024],
                              bass.AP(wo if ffn else w_out, kc * 128 * D,
                                      [[D, 128], [1, D]]), outs=[wobB],
                              dsem="dwo")
                    if ffn:
                        emit_moe_casts()
                    if not ffn:
                        p.dma("pool", boutb[:, :], b_out.ap(), outs=[boutB])

                    pend = []

                    def _do_av(v_ap_of, vwbuf, heads, av_first, av_last,
                               pslot, mark=False):
                        for v_ in range(2):
                            qi, s_ = heads[v_]
                            p.acc_mm(pAVB[qi], av_first, av_last,
                                     lambda q, qi=qi, v_=v_: q.matmul(
                                         pAV[qi][0:65, :], v_ap_of(v_),
                                         pP[:, pslot * 1024 + v_ * 512:
                                            pslot * 1024 + v_ * 512 + 512],
                                         start=av_first, stop=av_last),
                                     ins=[vwbuf, pPB[pslot]],
                                     force_mark=(mark and v_ == 1))

                    def flush_av(mark=False):
                        while pend:
                            _do_av(*pend.pop(0), mark=mark)

                    it_ctr = [0]

                    def s_exp_av(k_ap_of, kwbuf, v_ap_of, vwbuf, tcol, qt,
                                 heads, av_first, av_last, sb_, pp_pair,
                                 diag_t=None):
                        sb_ = it_ctr[0] % 2
                        pslot = it_ctr[0] % 3
                        it_ctr[0] += 1
                        for v_ in range(2):
                            qi, s_ = heads[v_]
                            base = 64 * (s_ % 2)
                            p.op("pe", lambda q, v_=v_, s_=s_, base=base:
                                 q.matmul(
                                     pS[sb_][:, v_ * 512:v_ * 512 + 512],
                                     k_ap_of(base, tcol),
                                     qT[base:base + 64,
                                        (s_ // 2) * 2048 + qt * 512:
                                        (s_ // 2) * 2048 + qt * 512 + 512],
                                     start=True, stop=True),
                                 ins=[kwbuf, g("qT", s_ // 2, qt)],
                                 outs=[pSB[sb_][v_]])
                        p.op("act", lambda q: q.activation(
                            pP[:, pslot * 1024:pslot * 1024 + 1024],
                            pS[sb_][:, :], AF.Exp,
                            bias=cbias[:, pp_pair:pp_pair + 1], scale=8.0),
                            ins=[pSB[sb_][0], pSB[sb_][1], g("cbias")],
                            outs=[pPB[pslot]])
                        if diag_t is not None:
                            for v_ in range(2):
                                p.op("dve", lambda q, v_=v_: q.tensor_tensor(
                                    pP[:, pslot * 1024 + v_ * 512:
                                       pslot * 1024 + v_ * 512 + 512],
                                    pP[:, pslot * 1024 + v_ * 512:
                                       pslot * 1024 + v_ * 512 + 512],
                                    cons_masks[:, diag_t * 512:
                                               (diag_t + 1) * 512],
                                    ALU.mult), ins=[pPB[pslot], cmB],
                                    outs=[pPB[pslot]])
                        pend.append((v_ap_of, vwbuf, heads, av_first,
                                     av_last, pslot))
                        if len(pend) > (2 if PIPELINE_AV else 0):
                            _do_av(*pend.pop(0))

                    def normalize(qi, s_, qt):
                        p.op("dve", lambda q: q.reciprocal(
                            smv[0:1, RECR:RECR + 512], pAV[qi][64:65, 0:512]),
                            ins=[pAVB[qi]], outs=[g("smv")])
                        p.op("dve", lambda q: q.tensor_copy(
                            rrb[0:1, :], smv[0:1, RECR:RECR + 512]),
                            ins=[g("smv")], outs=[rrbB])
                        p.op("pe", lambda q: q.matmul(
                            pS[0][0:64, 0:512], onesb[0:1, 0:64], rrb[0:1, :],
                            start=True, stop=True), ins=[rrbB, con],
                            outs=[pSB[0][0]])
                        p.op("dve", lambda q: q.tensor_copy(
                            rbc[0:64, :], pS[0][0:64, 0:512]),
                            ins=[pSB[0][0]], outs=[rbcB])
                        base = 64 * (s_ % 2)
                        dst = ctxn[base:base + 64,
                                   (s_ // 2) * 512:(s_ // 2) * 512 + 512]
                        p.op("dve", lambda q, dst=dst: q.tensor_tensor(
                            dst, pAV[qi][0:64, 0:512], rbc[0:64, :], ALU.mult),
                            ins=[pAVB[qi], rbcB], outs=[ctxB[s_ // 2]])

                    for qt in range(NT):
                        if ffn:
                            head_groups = [([(i, slot_of[4 * kh + i])
                                             for i in range(4)], kh)
                                           for kh in range(4)]
                        else:
                            head_groups = [([(i, 4 * grp + i)
                                             for i in range(4)], grp)
                                           for grp in range(4)]
                        for heads, gid in head_groups:
                            it = 0
                            if ffn:
                                kh = gid
                                kbase = 64 * (kh % 2)
                                for j in range(UBLK[qt]):
                                    r_, sl = tile_owner(j)
                                    kb = j % 2
                                    flush_av(mark=True)
                                    p.dma("sp",
                                          kwin[kb][kbase:kbase + 64, 0:512],
                                          bass.AP(kg_, (r_ * 256 + kh * 64) * T
                                                  + sl * 512,
                                                  [[T, 64], [1, 512]]),
                                          ins=[kgB], outs=[kwinB[kb]],
                                          dsem=f"dk{kb}")
                                    p.dma("sp", vwin[kb][:, 0:260],
                                          bass.AP(vg_, (r_ * 512 + kh * 128) *
                                                  1040 + sl * 260,
                                                  [[1040, 128], [1, 260]]),
                                          ins=[vgB], outs=[vwinB[kb]],
                                          dsem=f"dv{kb}")
                                    p.op("dve", lambda q, kb=kb, qt=qt, j=j:
                                         q.tensor_scalar(
                                             vwin[kb][:, 0:260],
                                             vwin[kb][:, 0:260],
                                             vmsk[:, qt * 16 + j:
                                                  qt * 16 + j + 1],
                                             None, ALU.mult),
                                         ins=[vwinB[kb], con],
                                         outs=[vwinB[kb]])
                                    for t_ in range(4):
                                        for u in (0, 1):
                                            hp = [heads[2 * u],
                                                  heads[2 * u + 1]]
                                            s_exp_av(
                                                lambda base, tcol, kb=kb:
                                                kwin[kb][base:base + 64,
                                                         tcol:tcol + 128],
                                                kwinB[kb],
                                                lambda v_, kb=kb, t_=t_:
                                                vwin[kb][:, t_ * 65:
                                                         t_ * 65 + 65],
                                                vwinB[kb], t_ * 128, qt, hp,
                                                it == 0, False,
                                                u % 2,
                                                gid * 2 + u)
                                        it += 1
                                for t_ in range(4):
                                    last = (t_ == 3)
                                    for u in (0, 1):
                                        hp = [heads[2 * u], heads[2 * u + 1]]
                                        s_exp_av(
                                            lambda base, tcol, kh=kh, qt=qt:
                                            klocal[base:base + 64,
                                                   (kh // 2) * T + qt * 512 +
                                                   tcol:
                                                   (kh // 2) * T + qt * 512 +
                                                   tcol + 128],
                                            klB,
                                            lambda v_, kh=kh, qt=qt, t_=t_:
                                            vlocal[:, kh * 1040 +
                                                   (qt * 4 + t_) * 65:
                                                   kh * 1040 +
                                                   (qt * 4 + t_) * 65 + 65],
                                            vlB, t_ * 128, qt, hp,
                                            it == 0, last,
                                            u % 2,
                                            gid * 2 + u, diag_t=t_)
                                    it += 1
                            else:
                                grp = gid
                                for r_ in range(4):
                                    flush_av(mark=True)
                                    for pr in (0, 1):
                                        pair = grp * 2 + pr
                                        p.dma("sp", kwin[pr][:, :],
                                              bass.AP(kg_, (r_ * 1024 +
                                                            pair * 128) * T,
                                                      [[T, 128], [1, T]]),
                                              ins=[kgB], outs=[kwinB[pr]],
                                              dsem=f"dk{pr}")
                                        p.dma("sp", vwin[pr][:, :],
                                              bass.AP(vg_, (r_ * 1024 +
                                                            pair * 128) * 2080,
                                                      [[2080, 128], [1, 2080]]),
                                              ins=[vgB], outs=[vwinB[pr]],
                                              dsem=f"dv{pr}")
                                    for jj in range(16):
                                        for pr in (0, 1):
                                            hp = [heads[2 * pr],
                                                  heads[2 * pr + 1]]
                                            s_exp_av(
                                                lambda base, tcol, pr=pr:
                                                kwin[pr][base:base + 64,
                                                         tcol:tcol + 128],
                                                kwinB[pr],
                                                lambda v_, pr=pr, jj=jj:
                                                vwin[pr][:, jj * 130 +
                                                         v_ * 65:
                                                         jj * 130 +
                                                         v_ * 65 + 65],
                                                vwinB[pr], jj * 128, qt, hp,
                                                r_ == 0 and jj == 0,
                                                r_ == 3 and jj == 15,
                                                it % 2, grp * 2 + pr)
                                            it += 1
                            flush_av()
                            for qi, s_ in heads:
                                normalize(qi, s_, qt)

                        for oc in range(8):
                            mms = []
                            ins = [wobB, con] + [ctxB[kc] for kc in range(8)]
                            if not ffn:
                                mms.append((pAV[0][:, :],
                                            boutb[0:1, oc * 128:oc * 128 + 128],
                                            onesb[0:1, 0:512]))
                                ins.append(boutB)
                            for kc in range(8):
                                mms.append((pAV[0][:, :],
                                            wob[:, kc * 1024 + oc * 128:
                                                kc * 1024 + oc * 128 + 128],
                                            ctxn[:, kc * 512:(kc + 1) * 512]))
                            p.mm_group(mms, ins=ins, outs=[pAVB[0]])
                            p.op("act", lambda q: q.copy(dsc[:, :],
                                                         pAV[0][:, :]),
                                 ins=[pAVB[0]], outs=[dscB])
                            for j in range(4):
                                p.op("pe", lambda q, j=j: q.transpose(
                                    pAV[1][:, 0:128],
                                    dsc[:, j * 128:(j + 1) * 128],
                                    identf[:, :]), ins=[dscB, con],
                                    outs=[pAVB[1]])
                                p.op("dve", lambda q, j=j, oc=oc: q.tensor_copy(
                                    dts[:, j * 1024 + oc * 128:
                                        j * 1024 + oc * 128 + 128],
                                    pAV[1][:, 0:128]), ins=[pAVB[1]],
                                    outs=[dtsB[j]])

                        for j in range(4):
                            off = (qt * 512 + j * 128) * D
                            if ffn:
                                p.dma("sp", msc[:, 0:1024],
                                      bass.AP(ys, off, [[D, 128], [1, D]]),
                                      ins=[g("ys")], outs=[g("msc", 0)],
                                      dsem="dm0")
                                p.op("dve", lambda q, j=j: q.tensor_tensor(
                                    msc[:, 1024:2048], msc[:, 0:1024],
                                    dts[:, j * 1024:(j + 1) * 1024], ALU.add),
                                    ins=[g("msc", 0), dtsB[j]],
                                    outs=[g("msc", 1)])
                                p.dma("sp",
                                      bass.AP(ys, off, [[D, 128], [1, D]]),
                                      msc[:, 1024:2048], ins=[g("msc", 1)],
                                      outs=[g("ys")], dsem="ds_ys0")
                            else:
                                p.dma("sp", msc[:, 0:1024],
                                      bass.AP(r1, off, [[D, 128], [1, D]]),
                                      ins=[g("r1")], outs=[g("msc", 0)],
                                      dsem="dm0")
                                p.dma("sp", msc[:, 2048:3072],
                                      bass.AP(h2s, off, [[D, 128], [1, D]]),
                                      ins=[g("h2s")], outs=[g("msc", 2)],
                                      dsem="dm2")
                                p.op("dve", lambda q, j=j: q.tensor_tensor(
                                    msc[:, 1024:2048], msc[:, 2048:3072],
                                    dts[:, j * 1024:(j + 1) * 1024], ALU.add),
                                    ins=[g("msc", 2), dtsB[j]],
                                    outs=[g("msc", 1)])
                                p.op("dve", lambda q: q.tensor_tensor(
                                    msc[:, 1024:2048], msc[:, 0:1024],
                                    msc[:, 1024:2048], ALU.add),
                                    ins=[g("msc", 0), g("msc", 1)],
                                    outs=[g("msc", 1)])
                                p.dma("sp",
                                      bass.AP(r2, off, [[D, 128], [1, D]]),
                                      msc[:, 1024:2048], ins=[g("msc", 1)],
                                      outs=[g("r2")], dsem="ds_r2")
                    p.barrier()

        GL0, GW0 = 0, 2048

        def moe_and_close(src_dram, src_buf, dst_dram, dst_buf):
            norm_transpose_phase(ys, g("ys"), "ffn2")
            with ExitStack() as mo:
                sbm = lambda n, s, d: mo.enter_context(
                    nc.sbuf_tensor(_nm(n), list(s), d))
                psm = lambda n: mo.enter_context(
                    nc.psum_tensor(_nm(n), [128, 512], F32))
                acc = sbm("acc", (128, 8 * 1024), F32)
                cbm = sbm("cbm", (128, 8 * 512), BF16)
                gateb = sbm("gateb", (128, 32), BF16)
                wbuf = [sbm(f"wmoe{i}", (128, 36 * 512), BF16) for i in (0, 1)]
                xne = sbm("xne", (128, 2 * 8 * 512), BF16)
                hbuf = sbm("hbuf", (128, 6 * 512), BF16)
                _ssl = sbm("ssl0", (128, 512), BF16)
                ssl = [_ssl, _ssl]
                dtile = [sbm("dtile0", (128, 1024), F32)] * 2
                pa_ = [psm(f"pa{i}") for i in (0, 1)]
                pb_ = [psm(f"pb{i}") for i in (0, 1)]
                pw_ = [psm(f"pw{i}") for i in (0, 1)]
                pcm = psm("pcm")
                ptc0 = psm("ptc0")
                accB = Buf("acc")
                cbmB = [Buf(f"cbm{i}") for i in range(8)]
                gateB = Buf("gateb")
                xneB = [Buf("xne0"), Buf("xne1")]
                _sslB = Buf("ssl0")
                sslB = [_sslB, _sslB]
                paB = [Buf("pa0"), Buf("pa1")]
                pbB = [Buf("pb0"), Buf("pb1")]
                pwB = [Buf("pw0"), Buf("pw1")]
                pcmB = Buf("pcm")
                ptcB = [Buf("ptc0"), Buf("ptc1")]
                _dt = Buf("dt0")
                dtB2 = [_dt, _dt]
                hcB = [Buf(f"hc{i}") for i in range(6)]
                wB = [[Buf(f"w{i}_{k}") for k in range(22)] for i in (0, 1)]
                pgt = [pa_[0], pa_[1], pb_[0], pb_[1]]
                pgtB = [paB[0], paB[1], pbB[0], pbB[1]]

                p.dma("pool", gateb[:, :].rearrange("p (k e) -> p k e", k=8),
                      bass.AP(gate, 0, [[4, 128], [512, 8], [1, 4]]),
                      outs=[gateB], dsem="dsm")

                def load_moe_w(wi, e, hq):
                    for kc in range(8):
                        p.dma("pool",
                              wbuf[wi][:, kc * 768:(kc + 1) * 768],
                              bass.AP(w1b, (e * 1024 + kc * 128) * HID +
                                      hq * 768, [[HID, 128], [1, 768]]),
                              ins=[g("w1b")], outs=[wB[wi][kc]],
                              dsem=f"dw{wi}a")
                        p.dma("pool",
                              wbuf[wi][:, 6144 + kc * 768:
                                       6144 + (kc + 1) * 768],
                              bass.AP(w3b, (e * 1024 + kc * 128) * HID +
                                      hq * 768, [[HID, 128], [1, 768]]),
                              ins=[g("w3b")], outs=[wB[wi][8 + kc]],
                              dsem=f"dw{wi}b")
                    for kc2 in range(6):
                        p.dma("pool",
                              wbuf[wi][:, 12288 + kc2 * 1024:
                                       12288 + (kc2 + 1) * 1024],
                              bass.AP(w2b, (e * HID + hq * 768 +
                                           kc2 * 128) * D,
                                      [[D, 128], [1, D]]),
                              ins=[g("w2b")], outs=[wB[wi][16 + kc2]],
                              dsem=f"dw{wi}c")

                def gate_comb(half):
                    for tq in range(2):
                        qt = half * 2 + tq
                        L = lambda e: smv[0:1, e * 512:(e + 1) * 512]
                        W_ = lambda i: smv[0:1, 2048 + i * 512:
                                           2048 + (i + 1) * 512]
                        for e in range(NE):
                            p.mm_group(
                                [(pgt[e][0:1, :],
                                  gateb[:, kc * 4 + e:kc * 4 + e + 1],
                                  xnT[:, kc * 2048 + qt * 512:
                                      kc * 2048 + qt * 512 + 512])
                                 for kc in range(8)],
                                ins=[gateB] + [g("xnT", kc, qt)
                                               for kc in range(8)],
                                outs=[pgtB[e]])
                            p.op("dve", lambda q, e=e: q.tensor_copy(
                                L(e), pgt[e][0:1, :]),
                                ins=[pgtB[e]], outs=[g("smv")])
                        sop = lambda o, a, b2, alu: p.op(
                            "dve", lambda q: q.tensor_tensor(o, a, b2, alu),
                            ins=[g("smv")], outs=[g("smv")])
                        sop(W_(0), L(0), L(1), ALU.max)
                        sop(W_(1), L(0), L(1), ALU.min)
                        sop(W_(2), L(2), L(3), ALU.max)
                        sop(W_(3), L(2), L(3), ALU.min)
                        sop(W_(1), W_(1), W_(3), ALU.max)
                        sop(W_(3), W_(0), W_(2), ALU.min)
                        sop(W_(0), W_(0), W_(2), ALU.max)
                        sop(W_(1), W_(1), W_(3), ALU.max)
                        sop(W_(2), W_(1), W_(0), ALU.subtract)
                        p.op("act", lambda q: q.activation(
                            W_(2), W_(2), AF.Exp, bias=0.0),
                            ins=[g("smv")], outs=[g("smv")])
                        p.op("dve", lambda q: q.tensor_scalar(
                            W_(2), W_(2), 1.0, None, ALU.add),
                            ins=[g("smv")], outs=[g("smv")])
                        p.op("dve", lambda q: q.reciprocal(W_(3), W_(2)),
                             ins=[g("smv")], outs=[g("smv")])
                        for e in range(NE):
                            sop(W_(2), L(e), W_(0), ALU.subtract)
                            p.op("act", lambda q: q.activation(
                                W_(2), W_(2), AF.Exp, bias=0.0),
                                ins=[g("smv")], outs=[g("smv")])
                            sop(L(e), L(e), W_(1), ALU.is_ge)
                            sop(W_(2), W_(2), L(e), ALU.mult)
                            sop(W_(2), W_(2), W_(3), ALU.mult)
                            p.op("dve", lambda q: q.tensor_copy(
                                rrb[0:1, :], W_(2)),
                                ins=[g("smv")], outs=[rrbB])
                            p.op("pe", lambda q: q.matmul(
                                pcm[:, :], onesb[0:1, 0:128], rrb[0:1, :],
                                start=True, stop=True),
                                ins=[rrbB, con], outs=[pcmB])
                            ci = e * 2 + tq
                            p.op("dve", lambda q, ci=ci: q.tensor_copy(
                                cbm[:, ci * 512:(ci + 1) * 512],
                                pcm[:, :]), ins=[pcmB], outs=[cbmB[ci]])

                for half in range(2):
                    gate_comb(half)
                    p.op("dve", lambda q: q.memset(acc[:, :], 0.0),
                         ins=[], outs=[accB])
                    phases = [(e, hq) for e in range(NE) for hq in range(4)]
                    if half == 0:
                        load_moe_w(0, *phases[0])
                    xne_done = set()
                    for pi, (e, hq) in enumerate(phases):
                        wi = pi % 2
                        if pi + 1 < len(phases):
                            load_moe_w(1 - wi, *phases[pi + 1])
                        elif half == 0:
                            load_moe_w(1 - wi, *phases[0])
                        if e not in xne_done:
                            xne_done.add(e)
                            for tq0 in range(2):
                                qt0 = half * 2 + tq0
                                ci0 = e * 2 + tq0
                                for kc in range(8):
                                    p.op("dve",
                                         lambda q, kc=kc, ci0=ci0, qt0=qt0,
                                         tq0=tq0: q.tensor_tensor(
                                             xne[:, tq0 * 4096 + kc * 512:
                                                 tq0 * 4096 + (kc + 1) * 512],
                                             xnT[:, kc * 2048 + qt0 * 512:
                                                 kc * 2048 + qt0 * 512 + 512],
                                             cbm[:, ci0 * 512:
                                                 (ci0 + 1) * 512],
                                             ALU.mult),
                                         ins=[g("xnT", kc, qt0), cbmB[ci0]],
                                         outs=[xneB[tq0]])
                        for tq in range(2):
                            qt = half * 2 + tq
                            for hc in range(6):
                                bb = hc % 2
                                p.mm_group(
                                    [(pa_[bb][:, :],
                                      wbuf[wi][:, kc * 768 + hc * 128:
                                               kc * 768 + hc * 128 + 128],
                                      xnT[:, kc * 2048 + qt * 512:
                                          kc * 2048 + qt * 512 + 512])
                                     for kc in range(8)],
                                    ins=wB[wi][0:8] + [g("xnT", kc, qt)
                                                       for kc in range(8)],
                                    outs=[paB[bb]])
                                p.mm_group(
                                    [(pb_[bb][:, :],
                                      wbuf[wi][:, 6144 + kc * 768 + hc * 128:
                                               6144 + kc * 768 + hc * 128 +
                                               128],
                                      xne[:, tq * 4096 + kc * 512:
                                          tq * 4096 + (kc + 1) * 512])
                                     for kc in range(8)],
                                    ins=wB[wi][8:16] + [xneB[tq]],
                                    outs=[pbB[bb]])
                                p.op("act", lambda q, bb=bb: q.activation(
                                    ssl[bb][:, :], pa_[bb][:, :], AF.Silu),
                                    ins=[paB[bb]], outs=[sslB[bb]])
                                p.op("dve", lambda q, hc=hc, bb=bb:
                                     q.tensor_tensor(
                                         hbuf[:, hc * 512:(hc + 1) * 512],
                                         ssl[bb][:, :], pb_[bb][:, :],
                                         ALU.mult),
                                     ins=[sslB[bb], pbB[bb]], outs=[hcB[hc]])
                            for oc in range(8):
                                bb = oc % 2
                                p.mm_group(
                                    [(pw_[bb][:, :],
                                      wbuf[wi][:, 12288 + kc2 * 1024 +
                                               oc * 128:
                                               12288 + kc2 * 1024 +
                                               oc * 128 + 128],
                                      hbuf[:, kc2 * 512:(kc2 + 1) * 512])
                                     for kc2 in range(6)],
                                    ins=wB[wi][16:22] + hcB,
                                    outs=[pwB[bb]])
                                p.op("dve", lambda q, oc=oc, tq=tq, bb=bb:
                                     q.tensor_tensor(
                                         acc[:, oc * 1024 + tq * 512:
                                             oc * 1024 + tq * 512 + 512],
                                         acc[:, oc * 1024 + tq * 512:
                                             oc * 1024 + tq * 512 + 512],
                                         pw_[bb][:, :], ALU.add),
                                     ins=[pwB[bb]], outs=[accB])

                    ptc = [ptc0, pcm]
                    for tt in range(8):
                        ci3 = tt % 2
                        dtB = dtB2[ci3]
                        tt_g = half * 8 + tt
                        off = tt_g * 128 * D
                        for c in range(8):
                            cp = c % 2
                            p.op("pe", lambda q, c=c, tt=tt, cp=cp:
                                 q.transpose(
                                     ptc[cp][:, 0:128],
                                     acc[:, c * 1024 + tt * 128:
                                         c * 1024 + tt * 128 + 128],
                                     identf[:, :]), ins=[accB, con],
                                 outs=[ptcB[cp] if cp == 0 else pcmB])
                            p.op("dve", lambda q, c=c, cp=cp, ci3=ci3:
                                 q.tensor_copy(
                                     dtile[ci3][:, c * 128:(c + 1) * 128],
                                     ptc[cp][:, 0:128]),
                                 ins=[ptcB[cp] if cp == 0 else pcmB],
                                 outs=[dtB])
                        p.dma("sp", msc[:, 0:1024],
                              bass.AP(ys, off, [[D, 128], [1, D]]),
                              ins=[g("ys")], outs=[g("msc", 0)],
                              dsem="dm0")
                        p.dma("sp", msc[:, 2048:3072],
                              bass.AP(src_dram, off, [[D, 128], [1, D]]),
                              ins=[src_buf], outs=[g("msc", 2)],
                              dsem="dm2")
                        p.op("dve", lambda q, ci3=ci3: q.tensor_tensor(
                            msc[:, 1024:2048], msc[:, 0:1024],
                            dtile[ci3][:, :], ALU.add),
                            ins=[g("msc", 0), dtB], outs=[g("msc", 1)])
                        p.op("dve", lambda q: q.scalar_tensor_tensor(
                            msc[:, 1024:2048], msc[:, 1024:2048], 0.5,
                            msc[:, 2048:3072], ALU.mult, ALU.add),
                            ins=[g("msc", 1), g("msc", 2)],
                            outs=[g("msc", 1)])
                        p.dma("sp",
                              bass.AP(dst_dram, off, [[D, 128], [1, D]]),
                              msc[:, 1024:2048], ins=[g("msc", 1)],
                              outs=[dst_buf], dsem="ds_out")
                p.barrier()

        norm_transpose_phase(x_in, g("x_in"), "ffn", store_stream=ys,
                             store_buf=g("ys"))
        maybe_dump_dram("y0", ys, g("ys"))
        maybe_dump("xnT1", xnT[:, :], (128, 16 * 1024), BF16,
                   [g("xnT", c, t_) for c in range(8) for t_ in range(NT)])
        if upto != "norm1":
            attn_invocation("ffn")
            maybe_dump_dram("y1a", ys, g("ys"))
            if upto != "attn1":
                moe_and_close(x_in, g("x_in"), r1, g("r1"))
                maybe_dump_dram("r1d", r1, g("r1"))
                if upto != "moe1":
                    norm_transpose_phase(r1, g("r1"), "ab", store_stream=h2s,
                                         store_buf=g("h2s"))
                    attn_invocation("ab")
                    maybe_dump_dram("r2d", r2, g("r2"))
                    if upto != "ab":
                        norm_transpose_phase(r2, g("r2"), "ffn",
                                             store_stream=ys,
                                             store_buf=g("ys"))
                        attn_invocation("ffn")
                        moe_and_close(r2, g("r2"), out_t, g("out"))

        if upto is not None:
            final_toks.append(p.dma("sp", out_t.ap(), x_in.ap(),
                                    ins=[g("x_in")]))

        for tok in final_toks:
            p.wait("sp", tok)
        p.wait("sp", ("d_sp", p.cnt["d_sp"]))
        p.barrier()

    return nc


def _prep_inputs(inputs, cig, batch):
    x = inputs["L0_attn_X"]
    tiles = own_tiles(cig)
    x_local = np.concatenate([x[batch, gt * 512:(gt + 1) * 512, :]
                              for gt in tiles], 0)
    qperm = np.concatenate([np.arange(64 * h, 64 * h + 64) for h in FFN_QORD])
    wq_r = inputs["L0_ffn_wq"][:, qperm]
    wo_r = inputs["L0_ffn_wo"][qperm, :]
    tabCf, tabSf, tabCa, tabSa = make_tables(cig)
    pf, pa = make_perms()
    masks = make_masks().reshape(4 * 128, 512)
    m = {
        "x_local": x_local,
        "ffn_wq": wq_r,
        "ffn_wk": inputs["L0_ffn_wk"],
        "ffn_wv": inputs["L0_ffn_wv"],
        "ffn_wo": wo_r,
        "ffn_gate": inputs["L0_ffn_gate"],
        "ffn_w1": inputs["L0_ffn_w1"].reshape(NE * D, HID),
        "ffn_w2": inputs["L0_ffn_w2"].reshape(NE * HID, D),
        "ffn_w3": inputs["L0_ffn_w3"].reshape(NE * D, HID),
        "ffn_g1": inputs["L0_ffn_g1"].reshape(1, D),
        "ffn_g2": inputs["L0_ffn_g2"].reshape(1, D),
        "ab_w_in": inputs["L0_attn_W_in"],
        "ab_w_out": inputs["L0_attn_W_out"],
        "ab_b_in": inputs["L0_attn_b_in"].reshape(1, 3 * D),
        "ab_b_out": inputs["L0_attn_b_out"].reshape(1, D),
        "ab_gamma": inputs["L0_attn_gamma"].reshape(1, D),
        "ab_beta": inputs["L0_attn_beta"].reshape(1, D),
        "tabC_ffn": tabCf, "tabS_ffn": tabSf,
        "tabC_ab": tabCa, "tabS_ab": tabSa,
        "perm_ffn": pf, "perm_ab": pa,
        "masks": masks,
        "ident": np.eye(128, dtype=np.float32),
        "ones_bf": np.ones((1, 128), ml_dtypes.bfloat16),
        "ones_col": np.ones((128, 1), ml_dtypes.bfloat16),
        "cvec": np.tile(np.array([[EPS_LN, EPS_RMS, 1.0, 0.0, 0, 0, 0, 0]],
                                 np.float32), (128, 1)),
        "vmask": make_vmask(cig),
    }
    return {k: np.ascontiguousarray(v) for k, v in m.items()}


def kernel(**inputs):
    nc = build_program()
    in_maps = []
    for core in range(NCORE):
        batch, cig = divmod(core, 4)
        in_maps.append(_prep_inputs(inputs, cig, batch))
    res = run_bass_kernel_spmd(nc, in_maps, core_ids=list(range(NCORE)))
    out = np.zeros((B, S, D), np.float32)
    for core in range(NCORE):
        batch, cig = divmod(core, 4)
        o = res.results[core]["out_local"]
        for slot, gt in enumerate(own_tiles(cig)):
            out[batch, gt * 512:(gt + 1) * 512, :] = \
                o[slot * 512:(slot + 1) * 512]
    return out


# version 11
# speedup vs baseline: 1.1503x; 1.0398x over previous
import math
from contextlib import ExitStack

import numpy as np
import ml_dtypes

import concourse.bass as bass
import concourse.mybir as mybir
from concourse.bass_utils import run_bass_kernel_spmd

F32 = mybir.dt.float32
BF16 = mybir.dt.bfloat16
AF = mybir.ActivationFunctionType
ALU = mybir.AluOpType
AX = mybir.AxisListType

B, S, D = 2, 8192, 1024
NCORE = 8
T = 2048
NT = 4
NTT = 16
HD = 64
NQH, NKH = 16, 4
NABH = 16
HID = 3072
NE = 4
EPS_LN = 1e-5
EPS_RMS = 1.1920928955078125e-07
UBLK = [3, 7, 11, 15]
import os
PIPELINE_AV = os.environ.get("K_PIPE_AV", "1") == "1"
ROT_ROPE = os.environ.get("K_ROT", "1") == "1"

CC_GROUPS = [[0, 1, 2, 3], [4, 5, 6, 7]]

FFN_QORD = [0, 4, 1, 5, 2, 6, 3, 7, 8, 12, 9, 13, 10, 14, 11, 15]
SLOT_OF_F = {FFN_QORD[s]: s for s in range(16)}


def own_tiles(cig):
    return sorted([cig, 7 - cig, 8 + cig, 15 - cig])


_OWNER = {}
for _c in range(4):
    for _slot, _g in enumerate(own_tiles(_c)):
        _OWNER[_g] = (_c, _slot)


def tile_owner(gt):
    return _OWNER[gt]


def _positions(cig):
    return np.concatenate([np.arange(gt * 512, (gt + 1) * 512)
                           for gt in own_tiles(cig)])


def make_tables(cig):
    pos = _positions(cig).astype(np.float64)[None, :]
    inv = 10000.0 ** (-np.arange(0, HD, 2, dtype=np.float64) / HD)
    ang = pos * inv[:, None][np.arange(64) % 32]
    cf = np.cos(ang); sf = np.sin(ang)
    tabC_ffn = np.concatenate([cf, cf], 0) * 0.125
    tabS_ffn = np.concatenate([sf, sf], 0) * 0.125
    inv2 = 10000.0 ** (-np.arange(0, 32, 2, dtype=np.float64) / 32)
    ang2 = pos * inv2[:, None][np.arange(32) // 2]
    ca = np.concatenate([np.cos(ang2), np.ones((32, T))], 0)
    sa = np.concatenate([np.sin(ang2), np.zeros((32, T))], 0)
    tabC_ab = np.concatenate([ca, ca], 0) * 0.125
    tabS_ab = np.concatenate([sa, sa], 0) * 0.125
    return (tabC_ffn.astype(np.float32), tabS_ffn.astype(np.float32),
            tabC_ab.astype(np.float32), tabS_ab.astype(np.float32))


def make_perms():
    pf = np.zeros((128, 128), np.float32)
    for h in range(2):
        o = 64 * h
        for j in range(32):
            pf[o + 32 + j, o + j] = -1.0
            pf[o + j, o + 32 + j] = 1.0
    pa = np.zeros((128, 128), np.float32)
    for h in range(2):
        o = 64 * h
        for i in range(16):
            pa[o + 2 * i + 1, o + 2 * i] = -1.0
            pa[o + 2 * i, o + 2 * i + 1] = 1.0
    return pf, pa


def make_masks():
    out = np.zeros((4, 128, 512), np.float32)
    for r in range(4):
        pr = np.arange(128)[:, None] + r * 128
        c = np.arange(512)[None, :]
        out[r] = (pr <= c).astype(np.float32)
    return out


def make_vmask(cig):
    out = np.zeros((128, 64), np.float32)
    tiles = own_tiles(cig)
    for qt in range(4):
        for j in range(16):
            out[:, qt * 16 + j] = 1.0 if j < tiles[qt] else 0.0
    return out


ENG_SEM = dict(pe="pe", act="act", dve="dve", pool="pool", sp="sp")
DMA_SEM = dict(sp="d_sp", act="d_act", pool="d_pool", dve="d_dve")


class Buf:
    __slots__ = ("name", "w", "readers", "pending")

    def __init__(self, name):
        self.name = name
        self.w = None
        self.readers = []
        self.pending = []


class Prog:
    def __init__(self, nc, es):
        self.nc = nc
        self.es = es
        self.q = dict(pe=nc.tensor, act=nc.scalar, dve=nc.vector,
                      pool=nc.gpsimd, sp=nc.sync)
        self.sem = {}
        self.cnt = {}
        self.waited = {e: {} for e in self.q}
        for name in ("pe", "act", "dve", "pool", "sp",
                     "d_sp", "d_act", "d_pool", "d_dve", "cc"):
            self.sem[name] = es.enter_context(nc.semaphore(name))
            self.cnt[name] = 0

    def getsem(self, name):
        if name not in self.sem:
            self.sem[name] = self.es.enter_context(self.nc.semaphore(name))
            self.cnt[name] = 0
        return self.sem[name]

    def mark(self, instr, sem, inc):
        instr.then_inc(self.sem[sem], inc)
        self.cnt[sem] += inc
        return (sem, self.cnt[sem])

    def wait(self, eng, tok):
        if tok is None:
            return
        sem, v = tok
        if self.waited[eng].get(sem, 0) >= v:
            return
        self.q[eng].wait_ge(self.sem[sem], v)
        self.waited[eng][sem] = v

    def _pre(self, eng, ins, outs):
        need = {}
        def add(tok):
            if tok is not None and tok[1] > need.get(tok[0], -1):
                need[tok[0]] = tok[1]
        for b in ins:
            add(b.w)
        for b in outs:
            add(b.w)
            for t in b.readers:
                add(t)
        for sem, v in need.items():
            self.wait(eng, (sem, v))

    def _post(self, tok, ins, outs):
        for b in ins:
            b.readers.append(tok)
        for b in outs:
            b.w = tok
            b.readers = []

    def op(self, eng, emit, ins=(), outs=(), sem=None, inc=1):
        self._pre(eng, ins, outs)
        instr = emit(self.q[eng])
        tok = self.mark(instr, sem or ENG_SEM[eng], inc)
        self._post(tok, ins, outs)
        return tok

    def dma(self, qname, out_ap, in_ap, ins=(), outs=(), dsem=None, **kw):
        if dsem is not None:
            self.getsem(dsem)
        return self.op(qname, lambda q: q.dma_start(out=out_ap, in_=in_ap, **kw),
                       ins=ins, outs=outs, sem=dsem or DMA_SEM[qname], inc=16)

    def mm_group(self, mms, ins=(), outs=()):
        self._pre("pe", ins, outs)
        n = len(mms)
        instr = None
        for i, (o, l, r) in enumerate(mms):
            instr = self.q["pe"].matmul(o, l, r, start=(i == 0), stop=(i == n - 1))
        tok = self.mark(instr, "pe", 1)
        self._post(tok, ins, outs)
        return tok

    def acc_mm(self, pbuf, first, last, emit, ins, force_mark=False):
        if first:
            self._pre("pe", ins, [pbuf])
        else:
            self._pre("pe", ins, [])
        instr = emit(self.q["pe"])
        pbuf.pending.extend(ins)
        if last or force_mark:
            tok = self.mark(instr, "pe", 1)
            for b in pbuf.pending:
                b.readers.append(tok)
            pbuf.pending = []
            if last:
                pbuf.w = tok
                pbuf.readers = []
            return tok
        return None

    def barrier(self):
        toks = [(s, self.cnt[s]) for s in self.cnt if self.cnt[s] > 0]
        for eng in self.q:
            for t in toks:
                self.wait(eng, t)


def build_program(dbg=(), upto=None):
    nc = bass.Bass("TRN2", target_bir_lowering=False)
    dbg = set(dbg)

    def din(name, shape, dt=F32):
        return nc.dram_tensor(name, list(shape), dt, kind="ExternalInput")

    x_in = din("x_local", (T, D))
    wq = din("ffn_wq", (D, D))
    wk = din("ffn_wk", (D, 256))
    wv = din("ffn_wv", (D, 256))
    wo = din("ffn_wo", (D, D))
    gate = din("ffn_gate", (D, NE))
    w1 = din("ffn_w1", (NE * D, HID))
    w2 = din("ffn_w2", (NE * HID, D))
    w3 = din("ffn_w3", (NE * D, HID))
    w_in = din("ab_w_in", (D, 3 * D))
    w_out = din("ab_w_out", (D, D))
    b_in = din("ab_b_in", (1, 3 * D))
    b_out = din("ab_b_out", (1, D))
    g1v = din("ffn_g1", (1, D))
    g2v = din("ffn_g2", (1, D))
    gammav = din("ab_gamma", (1, D))
    betav = din("ab_beta", (1, D))
    tabCf = din("tabC_ffn", (128, T))
    tabSf = din("tabS_ffn", (128, T))
    tabCa = din("tabC_ab", (128, T))
    tabSa = din("tabS_ab", (128, T))
    permF = din("perm_ffn", (128, 128))
    permA = din("perm_ab", (128, 128))
    masks_in = din("masks", (4 * 128, 512))
    ident_in = din("ident", (128, 128))
    onesb_in = din("ones_bf", (1, 128), BF16)
    onescol_in = din("ones_col", (128, 1), BF16)
    cvec_in = din("cvec", (128, 8))
    vmask_in = din("vmask", (128, 64))

    out_t = nc.dram_tensor("out_local", [T, D], F32, kind="ExternalOutput")

    r1 = nc.dram_tensor("r1", [T, D], F32)
    r2 = nc.dram_tensor("r2", [T, D], F32)
    ys = nc.dram_tensor("ystream", [T, D], F32)
    h2s = nc.dram_tensor("h2stream", [T, D], F32)
    kstg_f = nc.dram_tensor("kstg_f", [256, T], BF16)
    kg_f = nc.dram_tensor("kg_f", [1024, T], BF16)
    vstg_f = nc.dram_tensor("vstg_f", [512, 1040], BF16)
    vg_f = nc.dram_tensor("vg_f", [2048, 1040], BF16)
    kstg_a = nc.dram_tensor("kstg_a", [1024, T], BF16)
    kg_a = nc.dram_tensor("kg_a", [4096, T], BF16)
    vstg_a = nc.dram_tensor("vstg_a", [1024, 2080], BF16)
    vg_a = nc.dram_tensor("vg_a", [4096, 2080], BF16)
    kms_f = nc.dram_tensor("kms_f", [1, NKH], F32)
    kmg_f = nc.dram_tensor("kmg_f", [4, NKH], F32)
    kms_a = nc.dram_tensor("kms_a", [1, NABH], F32)
    kmg_a = nc.dram_tensor("kmg_a", [4, NABH], F32)
    w1b = nc.dram_tensor("w1b", [NE * D, HID], BF16)
    w2b = nc.dram_tensor("w2b", [NE * HID, D], BF16)
    w3b = nc.dram_tensor("w3b", [NE * D, HID], BF16)


    dump_outs = {}

    def dbg_out(name, shape, dt):
        if name in dbg and name not in dump_outs:
            dump_outs[name] = nc.dram_tensor(f"dbg_{name}", list(shape), dt,
                                             kind="ExternalOutput")
        return dump_outs.get(name)

    es = ExitStack()
    with es:
        p = Prog(nc, es)
        _uid = [0]

        def _nm(n):
            _uid[0] += 1
            return f"{n}_{_uid[0]}"

        sb = lambda name, shape, dt: es.enter_context(
            nc.sbuf_tensor(name, list(shape), dt))

        xnT = sb("xnT", (128, 16 * 1024), BF16)
        smv = sb("smv", (128, 4096), F32)
        msc = sb("msc", (128, 4 * 1024), F32)
        identf = sb("identf", (128, 128), F32)
        identb = sb("identb", (128, 128), BF16)
        permFs = sb("permFs", (128, 128), F32)
        permAs = sb("permAs", (128, 128), F32)
        onesb = sb("onesb", (1, 512), BF16)
        onescol = sb("onescol", (128, 1), BF16)
        parms = sb("parms", (128, 32), F32)
        cbias = sb("cbias", (128, 16), F32)
        cvec = sb("cvec_s", (128, 8), F32)
        vmsk = sb("vmsk", (128, 64), F32)
        rrb = sb("rrb", (1, 512), BF16)

        bq = {}

        def grid(name, *dims):
            import itertools
            if not dims:
                bq[(name,)] = Buf(name)
                return
            for idx in itertools.product(*[range(d) for d in dims]):
                bq[(name,) + idx] = Buf(f"{name}{idx}")

        def g(name, *idx):
            return bq[(name,) + tuple(idx)]

        grid("xnT", 8, NT)
        grid("qT", 8, NT)
        grid("msc", 4)
        grid("smv")
        grid("q2s")
        grid("cbias")
        grid("con")
        grid("parms")
        grid("rrb")
        for t_ in ("x_in", "r1", "r2", "ys", "h2s", "kstg_f", "kg_f", "vstg_f",
                   "vg_f", "kstg_a", "kg_a", "vstg_a", "vg_a", "kms_f", "kmg_f",
                   "kms_a", "kmg_a", "out"):
            grid(t_)

        con = g("con")
        rrbB = g("rrb")
        for t_ in ("w1b", "w2b", "w3b"):
            grid(t_)
        moe_cast_done = [False]

        def emit_moe_casts():
            if moe_cast_done[0]:
                return
            moe_cast_done[0] = True
            for srcT, dstT, nel, bname, dsm in (
                    (w1, w1b, NE * D * HID, "w1b", "dc1"),
                    (w3, w3b, NE * D * HID, "w3b", "dc3"),
                    (w2, w2b, NE * HID * D, "w2b", "dc2")):
                nchunk = 8
                step = nel // nchunk
                for ci2 in range(nchunk):
                    p.dma("pool",
                          bass.AP(dstT, ci2 * step, [[step, 1], [1, step]]),
                          bass.AP(srcT, ci2 * step, [[step, 1], [1, step]]),
                          outs=[g(bname)], dsem=dsm)

        for o, i in [(identf[:, :], ident_in.ap()),
                     (permFs[:, :], permF.ap()),
                     (permAs[:, :], permA.ap()),
                     (onesb[0:1, 0:128], onesb_in.ap()),
                     (onescol[:, :], onescol_in.ap()),
                     (cvec[:, :], cvec_in.ap()),
                     (vmsk[:, :], vmask_in.ap())]:
            p.dma("sp", o, i, outs=[con])
        p.dma("pool", identb[:, :], ident_in.ap(), outs=[con])
        for ci, v in enumerate((g1v, g2v, gammav, betav)):
            for c in range(8):
                p.dma("sp", parms[:, ci * 8 + c: ci * 8 + c + 1],
                      bass.AP(v, c * 128, [[1, 128], [1, 1]]), outs=[g("parms")])
        for i in (1, 2, 3):
            p.op("dve", lambda q, i=i: q.tensor_copy(
                onesb[0:1, i * 128:(i + 1) * 128], onesb[0:1, 0:128]),
                ins=[con], outs=[con])
        nc.const_aps.register(F32, EPS_LN, cvec[:, 0:1])
        nc.const_aps.register(F32, EPS_RMS, cvec[:, 1:2])
        nc.const_aps.register(F32, 1.0, cvec[:, 2:3])
        nc.const_aps.register(F32, 0.0, cvec[:, 3:4])
        p.barrier()

        final_toks = []

        def maybe_dump(name, ap, shape, dt, inbufs):
            d = dbg_out(name, shape, dt)
            if d is not None:
                final_toks.append(p.dma("sp", d.ap(), ap, ins=list(inbufs)))

        def maybe_dump_dram(name, src_dram, src_buf):
            d = dbg_out(name, (T, D), F32)
            if d is not None:
                final_toks.append(p.dma("sp", d.ap(), src_dram.ap(),
                                        ins=[src_buf]))

        def ln_rms_tile(src_buf, src_ap, dst_ap, dst_buf, mode, ri=0,
                        x2ap=None, x2=None):
            if x2ap is None:
                x2 = g("msc", 3)
                x2ap = msc[:, 3 * 1024:4 * 1024]
            sv = g("smv")
            C0 = ri * 8
            smc = lambda i: smv[:, C0 + i:C0 + i + 1]
            p.op("act", lambda q: q.activation(x2ap, src_ap, AF.Square),
                 ins=[src_buf], outs=[x2])
            p.op("dve", lambda q: q.tensor_reduce(smc(0), x2ap, AX.X,
                                                  ALU.add), ins=[x2], outs=[sv])
            if mode == "ln":
                p.op("dve", lambda q: q.tensor_reduce(smc(1), src_ap, AX.X,
                                                      ALU.add),
                     ins=[src_buf], outs=[sv])
                p.op("dve", lambda q: q.tensor_scalar(smc(1), smc(1),
                                                      1.0 / D, None, ALU.mult),
                     ins=[sv], outs=[sv])
                p.op("dve", lambda q: q.tensor_tensor(smc(2), smc(1),
                                                      smc(1), ALU.mult),
                     ins=[sv], outs=[sv])
                p.op("dve", lambda q: q.tensor_scalar(smc(3), smc(0),
                                                      1.0 / D, None, ALU.mult),
                     ins=[sv], outs=[sv])
                p.op("dve", lambda q: q.tensor_tensor(smc(2), smc(3),
                                                      smc(2), ALU.subtract),
                     ins=[sv], outs=[sv])
                var_col, eps = smc(2), EPS_LN
            else:
                p.op("dve", lambda q: q.tensor_scalar(smc(2), smc(0),
                                                      1.0 / D, None, ALU.mult),
                     ins=[sv], outs=[sv])
                var_col, eps = smc(2), EPS_RMS
            p.op("act", lambda q: q.activation(smc(4), var_col, AF.Sqrt,
                                               bias=eps), ins=[sv], outs=[sv])
            p.op("dve", lambda q: q.reciprocal(smc(5), smc(4)),
                 ins=[sv], outs=[sv])
            if mode == "ln":
                p.op("dve", lambda q: q.tensor_tensor(smc(6), smc(1),
                                                      smc(5), ALU.mult),
                     ins=[sv], outs=[sv])
                p.op("dve", lambda q: q.tensor_scalar(smc(6), smc(6),
                                                      -1.0, None, ALU.mult),
                     ins=[sv], outs=[sv])
                p.op("dve", lambda q: q.tensor_scalar(dst_ap, src_ap,
                                                      smc(5), smc(6),
                                                      ALU.mult, ALU.add),
                     ins=[src_buf, sv], outs=[dst_buf])
            else:
                p.op("dve", lambda q: q.tensor_scalar(dst_ap, src_ap,
                                                      smc(5), None,
                                                      ALU.mult),
                     ins=[src_buf, sv], outs=[dst_buf])

        def norm_transpose_phase(src_dram, src_buf, kind, store_stream=None,
                                 store_buf=None):
            with ExitStack() as ph:
                pt2 = [ph.enter_context(nc.psum_tensor(_nm("p_tr"), [128, 512],
                                                       BF16)) for _ in (0, 1)]
                zt2 = [ph.enter_context(nc.sbuf_tensor(_nm("zt"), [128, 1024],
                                                       BF16)) for _ in (0, 1)]
                x2b = ph.enter_context(nc.sbuf_tensor(_nm("x2b"), [128, 2048],
                                                      F32))
                ptb2 = [Buf("p_tr0"), Buf("p_tr1")]
                ztb2 = [Buf("zt0"), Buf("zt1")]
                x2B = [Buf("x2b0"), Buf("x2b1")]
                gcol0 = {"ffn": 0, "ffn2": 8, "ab": 16}[kind]
                for tt in range(NTT):
                    ri = tt % 2
                    pt, zt, ptb, ztb = pt2[ri], zt2[ri], ptb2[ri], ztb2[ri]
                    x2ap_ = x2b[:, ri * 1024:(ri + 1) * 1024]
                    xt, xt_ap = g("msc", 2 * ri), msc[:, 2 * ri * 1024:
                                                      (2 * ri + 1) * 1024]
                    yt, yt_ap = g("msc", 2 * ri + 1), msc[:, (2 * ri + 1) * 1024:
                                                          (2 * ri + 2) * 1024]
                    p.dma("sp", xt_ap,
                          bass.AP(src_dram, tt * 128 * D, [[D, 128], [1, D]]),
                          ins=[src_buf], outs=[xt], dsem=f"dm{ri}")
                    kw = dict(ri=ri, x2ap=x2ap_, x2=x2B[ri])
                    if kind == "ffn":
                        ln_rms_tile(xt, xt_ap, yt_ap, yt, "ln", **kw)
                        if store_stream is not None:
                            p.dma("sp", bass.AP(store_stream, tt * 128 * D,
                                                [[D, 128], [1, D]]), yt_ap,
                                  ins=[yt], outs=[store_buf],
                                  dsem=f"ds_ys{ri}")
                        ln_rms_tile(yt, yt_ap, zt[:, :], ztb, "rms", **kw)
                    elif kind == "ffn2":
                        ln_rms_tile(xt, xt_ap, zt[:, :], ztb, "rms", **kw)
                    else:
                        ln_rms_tile(xt, xt_ap, yt_ap, yt, "ln", **kw)
                        if store_stream is not None:
                            p.dma("sp", bass.AP(store_stream, tt * 128 * D,
                                                [[D, 128], [1, D]]), yt_ap,
                                  ins=[yt], outs=[store_buf],
                                  dsem=f"ds_h2{ri}")
                        ln_rms_tile(yt, yt_ap, zt[:, :], ztb, "ln", **kw)
                    qt, sub = divmod(tt, 4)
                    for c in range(8):
                        p.op("pe", lambda q, c=c: q.transpose(
                            pt[:, 0:128], zt[:, c * 128:(c + 1) * 128],
                            identb[:, :]), ins=[ztb, con], outs=[ptb])
                        dst = xnT[:, c * 2048 + qt * 512 + sub * 128:
                                  c * 2048 + qt * 512 + sub * 128 + 128]
                        if kind == "ab":
                            p.op("dve", lambda q, c=c, dst=dst: q.tensor_scalar(
                                dst, pt[:, 0:128], parms[:, 16 + c:17 + c],
                                parms[:, 24 + c:25 + c], ALU.mult, ALU.add),
                                ins=[ptb, g("parms")], outs=[g("xnT", c, qt)])
                        else:
                            p.op("dve", lambda q, c=c, dst=dst: q.tensor_scalar(
                                dst, pt[:, 0:128],
                                parms[:, gcol0 + c:gcol0 + c + 1], None,
                                ALU.mult),
                                ins=[ptb, g("parms")], outs=[g("xnT", c, qt)])
                p.barrier()

        KM2, QM2, KH2, QH2 = 0, 20, 40, 110
        KG16, KMX, KMAP, QP2, KP2, CRW = 200, 266, 290, 310, 330, 350
        RECR = 512

        def attn_invocation(tag):
            ffn = tag == "ffn"
            perm_s = permFs if ffn else permAs
            kstg, vstg, kms, kmg = ((kstg_f, vstg_f, kms_f, kmg_f) if ffn else
                                    (kstg_a, vstg_a, kms_a, kmg_a))
            kg_, vg_ = (kg_f, vg_f) if ffn else (kg_a, vg_a)
            kstgB = g("kstg_f" if ffn else "kstg_a")
            vstgB = g("vstg_f" if ffn else "vstg_a")
            kmsB = g("kms_f" if ffn else "kms_a")
            kmgB = g("kmg_f" if ffn else "kmg_a")
            kgB = g("kg_f" if ffn else "kg_a")
            vgB = g("vg_f" if ffn else "vg_a")
            slot_of = SLOT_OF_F if ffn else {h: h for h in range(16)}
            nkh = 4 if ffn else 16

            with ExitStack() as inv:
                sbi = lambda n, s, d: inv.enter_context(
                    nc.sbuf_tensor(_nm(n), list(s), d))
                qT = sbi("qT", (128, 16 * 1024), BF16)
                if ffn:
                    klocal = sbi("klocal", (128, 2 * T), BF16)
                    vlocal = sbi("vlocal", (128, 4 * 1040), BF16)
                    klB, vlB = Buf("klocal"), Buf("vlocal")
                    cons_masks = sbi("cmask", (128, 4 * 512), BF16)
                    cmB = Buf("cmask")
                    p.dma("pool",
                          cons_masks[:, :].rearrange("p (r c) -> p r c", r=4),
                          bass.AP(masks_in, 0,
                                  [[512, 128], [128 * 512, 4], [1, 512]]),
                          outs=[cmB])
                crowb = sbi("crowb", (1, 16), BF16)
                q2sb = [sbi(f"q2sb{i}", (128, 512), BF16) for i in (0, 1)]
                crB = Buf("crowb")
                q2B = [Buf("q2s0"), Buf("q2s1")]
                for oc_ in range(8):
                    for qt_ in range(NT):
                        bq[("qT", oc_, qt_)] = Buf(f"qT{oc_},{qt_}")

                with ExitStack() as ph:
                    sbp = lambda n, s, d: ph.enter_context(
                        nc.sbuf_tensor(_nm(n), list(s), d))
                    psp = lambda n, s, d=F32: ph.enter_context(
                        nc.psum_tensor(_nm(n), list(s), d))
                    wcol = [sbp(f"wcol{i}", (128, 8 * 512), BF16)
                            for i in (0, 1)]
                    tabC = sbp("tabC", (128, T), F32)
                    tabS = sbp("tabS", (128, T), F32)
                    kslot = sbp("kslot", (128, T), BF16)
                    vslot = sbp("vslot", (128, 16 * 130), BF16)
                    qraw = [sbp(f"qraw{i}", (128, 512), F32) for i in (0, 1)]
                    t1s = [sbp(f"t1s{i}", (128, 512), F32) for i in (0, 1)]
                    t2s = [sbp(f"t2s{i}", (128, 512), F32) for i in (0, 1)]
                    vtmp = [sbp(f"vtmp{i}", (128, 512), BF16) for i in (0, 1)]
                    binb = sbp("binb", (1, 3 * D), BF16) if not ffn else None
                    pq = [psp(f"pq{i}", (128, 512)) for i in (0, 1)]
                    ppm = [psp(f"ppm{i}", (128, 512)) for i in (0, 1)]
                    ptv = [psp(f"ptv{i}", (128, 512), BF16) for i in (0, 1)]
                    _p1 = psp("p1r", (1, 512))
                    p1r = [_p1, _p1]
                    pcb = psp("pcb", (128, 16))
                    wcB = [[Buf(f"wc{i}_{k}") for k in range(8)]
                           for i in (0, 1)]
                    tabB = Buf("tab")
                    kslB, vslB = Buf("kslot"), Buf("vslot")
                    qrB = [Buf("qraw0"), Buf("qraw1")]
                    t1B = [Buf("t1s0"), Buf("t1s1")]
                    t2B = [Buf("t2s0"), Buf("t2s1")]
                    vtB = [Buf("vtmp0"), Buf("vtmp1")]
                    binB = Buf("binb")
                    pqB = [Buf("pq0"), Buf("pq1")]
                    ppmB = [Buf("ppm0"), Buf("ppm1")]
                    ptvB = [Buf("ptv0"), Buf("ptv1")]
                    pcbB = Buf("pcb")
                    _p1B = Buf("p1r")
                    p1B = [_p1B, _p1B]

                    p.dma("sp", tabC[:, :], (tabCf if ffn else tabCa).ap(),
                          outs=[tabB], dsem="dtab")
                    p.dma("sp", tabS[:, :], (tabSf if ffn else tabSa).ap(),
                          outs=[tabB], dsem="dtab")
                    if not ffn:
                        p.dma("pool", binb[:, :], b_in.ap(), outs=[binB])

                    def load_wcols(wb_i, wt, wt_cols, c0, width):
                        for kc in range(8):
                            p.dma("pool",
                                  wcol[wb_i][:, kc * width:(kc + 1) * width],
                                  bass.AP(wt, kc * 128 * wt_cols + c0,
                                          [[wt_cols, 128], [1, width]]),
                                  outs=[wcB[wb_i][kc]], dsem=f"dw{wb_i}")

                    def proj_group(pb, pbuf, wb_i, width, occ, qt, bias_col):
                        mms = []
                        ins = list(wcB[wb_i]) + [con] + [g("xnT", kc, qt)
                                                         for kc in range(8)]
                        if bias_col is not None:
                            mms.append((pb[:, :],
                                        binb[0:1, bias_col:bias_col + 128],
                                        onesb[0:1, 0:512]))
                            ins.append(binB)
                        for kc in range(8):
                            mms.append((pb[:, :],
                                        wcol[wb_i][:, kc * width + occ * 128:
                                                   kc * width + occ * 128 + 128],
                                        xnT[:, kc * 2048 + qt * 512:
                                            kc * 2048 + qt * 512 + 512]))
                        p.mm_group(mms, ins=ins, outs=[pbuf])

                    def rope_out(pb, pbuf, qt, dst_ap, dst_buf, ri):
                        ri = ri if ROT_ROPE else 0
                        p.op("act", lambda q: q.copy(qraw[ri][:, :], pb[:, :]),
                             ins=[pbuf], outs=[qrB[ri]])
                        p.op("pe", lambda q: q.matmul(ppm[ri][:, :],
                                                      perm_s[:, :],
                                                      qraw[ri][:, :],
                                                      start=True, stop=True),
                             ins=[qrB[ri], con], outs=[ppmB[ri]])
                        p.op("dve", lambda q: q.tensor_tensor(
                            t1s[ri][:, :], qraw[ri][:, :],
                            tabC[:, qt * 512:(qt + 1) * 512], ALU.mult),
                            ins=[qrB[ri], tabB], outs=[t1B[ri]])
                        p.op("dve", lambda q: q.tensor_tensor(
                            t2s[ri][:, :], ppm[ri][:, :],
                            tabS[:, qt * 512:(qt + 1) * 512], ALU.mult),
                            ins=[ppmB[ri], tabB], outs=[t2B[ri]])
                        p.op("dve", lambda q, dst_ap=dst_ap: q.tensor_tensor(
                            dst_ap, t1s[ri][:, :], t2s[ri][:, :], ALU.add),
                            ins=[t1B[ri], t2B[ri]], outs=[dst_buf])

                    def sqmax(src_ap, src_buf, col_even, col_odd, ri):
                        ri = ri if ROT_ROPE else 0
                        p.op("act", lambda q: q.activation(q2sb[ri][:, :],
                                                           src_ap, AF.Square),
                             ins=[src_buf], outs=[q2B[ri]])
                        for half2, (base, colx) in enumerate(
                                ((0, col_even), (64, col_odd))):
                            p.op("pe", lambda q, base=base, half2=half2:
                                 q.matmul(
                                     p1r[half2][0:1, :],
                                     onescol[base:base + 64, 0:1],
                                     q2sb[ri][base:base + 64, :], start=True,
                                     stop=True),
                                 ins=[q2B[ri], con], outs=[p1B[half2]])
                            p.op("dve", lambda q, colx=colx, half2=half2:
                                 q.tensor_reduce(
                                     smv[0:1, colx:colx + 1],
                                     p1r[half2][0:1, :], AX.X,
                                     ALU.max), ins=[p1B[half2]],
                                 outs=[g("smv")])

                    kwidth = 256 if ffn else 512
                    kblocks = 1 if ffn else 2
                    for blk in range(kblocks):
                        wb_i = blk % 2
                        if ffn:
                            load_wcols(wb_i, wv, 256, 0, 256)
                        else:
                            load_wcols(wb_i, w_in, 3 * D, 2048 + blk * 512, 512)
                        for occ in range(2 if ffn else 4):
                            oc = blk * 4 + occ
                            p.op("dve", lambda q: q.memset(vslot[:, :], 1.0),
                                 ins=[], outs=[vslB])
                            for qt in range(NT):
                                bb = (oc * NT + qt) % 2
                                proj_group(pq[bb], pqB[bb], wb_i, kwidth, occ,
                                           qt, None if ffn else 2048 + oc * 128)
                                p.op("act", lambda q, bb=bb: q.copy(
                                    vtmp[bb][:, :], pq[bb][:, :]),
                                    ins=[pqB[bb]], outs=[vtB[bb]])
                                for j in range(4):
                                    tt = qt * 4 + j
                                    jp = j % 2
                                    p.op("pe", lambda q, j=j, bb=bb, jp=jp:
                                         q.transpose(
                                             ptv[jp][:, 0:128],
                                             vtmp[bb][:, j * 128:(j + 1) * 128],
                                             identb[:, :]),
                                         ins=[vtB[bb], con], outs=[ptvB[jp]])
                                    for half in (0, 1):
                                        c0 = ((half * 1040 + tt * 65) if ffn
                                              else (tt * 130 + half * 65))
                                        p.op("dve",
                                             lambda q, c0=c0, half=half, jp=jp:
                                             q.tensor_copy(
                                                 vslot[:, c0:c0 + 64],
                                                 ptv[jp][:, half * 64:
                                                         half * 64 + 64]),
                                             ins=[ptvB[jp]], outs=[vslB])
                            if ffn:
                                for half in (0, 1):
                                    kh = 2 * oc + half
                                    p.dma("sp",
                                          bass.AP(vstg, kh * 128 * 1040,
                                                  [[1040, 128], [1, 1040]]),
                                          vslot[:, half * 1040:
                                                (half + 1) * 1040],
                                          ins=[vslB], outs=[vstgB],
                                          dsem=f"dstg_v{half}")
                                    p.op("dve",
                                         lambda q, kh=kh, half=half:
                                         q.tensor_copy(
                                             vlocal[:, kh * 1040:
                                                    (kh + 1) * 1040],
                                             vslot[:, half * 1040:
                                                   (half + 1) * 1040]),
                                         ins=[vslB], outs=[vlB])
                            else:
                                p.dma("sp", bass.AP(vstg, oc * 128 * 2080,
                                                    [[2080, 128], [1, 2080]]),
                                      vslot[:, :], ins=[vslB], outs=[vstgB],
                                      dsem=f"dstg_v{oc % 2}")
                    p.op("pool", lambda q: q.collective_compute(
                        "AllGather", ALU.bypass, replica_groups=CC_GROUPS,
                        ins=[vstg.ap().opt()], outs=[vg_.ap().opt()]),
                        ins=[vstgB], outs=[vgB], sem="cc", inc=1)

                    for blk in range(kblocks):
                        wb_i = blk % 2
                        if ffn:
                            load_wcols(wb_i, wk, 256, 0, 256)
                        else:
                            load_wcols(wb_i, w_in, 3 * D, 1024 + blk * 512, 512)
                        for occ in range(2 if ffn else 4):
                            oc = blk * 4 + occ
                            for qt in range(NT):
                                bb = (oc * NT + qt) % 2
                                proj_group(pq[bb], pqB[bb], wb_i, kwidth, occ,
                                           qt, None if ffn else 1024 + oc * 128)
                                dst = kslot[:, qt * 512:(qt + 1) * 512]
                                rope_out(pq[bb], pqB[bb], qt, dst, kslB, bb)
                                sqmax(dst, kslB, KH2 + (2 * oc) * 4 + qt,
                                      KH2 + (2 * oc + 1) * 4 + qt, bb)
                            for half in (0, 1):
                                kh = 2 * oc + half
                                p.op("dve", lambda q, kh=kh: q.tensor_reduce(
                                    smv[0:1, KM2 + kh:KM2 + kh + 1],
                                    smv[0:1, KH2 + kh * 4:KH2 + kh * 4 + 4],
                                    AX.X, ALU.max), ins=[g("smv")],
                                    outs=[g("smv")])
                            p.dma("sp", bass.AP(kstg, oc * 128 * T,
                                                [[T, 128], [1, T]]),
                                  kslot[:, :], ins=[kslB], outs=[kstgB],
                                  dsem="dstg_k")
                            if ffn:
                                p.op("dve", lambda q, oc=oc: q.tensor_copy(
                                    klocal[:, oc * T:(oc + 1) * T],
                                    kslot[:, :]), ins=[kslB], outs=[klB])

                    p.dma("sp", kms.ap(), smv[0:1, KM2:KM2 + nkh],
                          ins=[g("smv")], outs=[kmsB], dsem="dstg_m")
                    p.op("pool", lambda q: q.collective_compute(
                        "AllGather", ALU.bypass, replica_groups=CC_GROUPS,
                        ins=[kstg.ap().opt()], outs=[kg_.ap().opt()]),
                        ins=[kstgB], outs=[kgB], sem="cc", inc=1)
                    p.op("pool", lambda q: q.collective_compute(
                        "AllGather", ALU.bypass, replica_groups=CC_GROUPS,
                        ins=[kms.ap().opt()], outs=[kmg.ap().opt()]),
                        ins=[kmsB], outs=[kmgB], sem="cc", inc=1)

                    for blk in range(2):
                        wb_i = blk % 2
                        load_wcols(wb_i, wq if ffn else w_in,
                                   D if ffn else 3 * D, blk * 512, 512)
                        for occ in range(4):
                            oc = blk * 4 + occ
                            for qt in range(NT):
                                bb = (oc * NT + qt) % 2
                                proj_group(pq[bb], pqB[bb], wb_i, 512, occ, qt,
                                           None if ffn else oc * 128)
                                dst = qT[:, oc * 2048 + qt * 512:
                                         oc * 2048 + qt * 512 + 512]
                                rope_out(pq[bb], pqB[bb], qt, dst,
                                         g("qT", oc, qt), bb)
                                sqmax(dst, g("qT", oc, qt),
                                      QH2 + (2 * oc) * 4 + qt,
                                      QH2 + (2 * oc + 1) * 4 + qt, bb)
                    for s_ in range(16):
                        p.op("dve", lambda q, s_=s_: q.tensor_reduce(
                            smv[0:1, QM2 + s_:QM2 + s_ + 1],
                            smv[0:1, QH2 + s_ * 4:QH2 + s_ * 4 + 4], AX.X,
                            ALU.max), ins=[g("smv")], outs=[g("smv")])

                    p.dma("sp", smv[0:1, KG16:KG16 + 4 * nkh],
                          bass.AP(kmg, 0, [[4 * nkh, 1], [1, 4 * nkh]]),
                          ins=[kmgB], outs=[g("smv")], dsem="dsm")
                    p.op("dve", lambda q: q.tensor_tensor(
                        smv[0:1, KMX:KMX + nkh], smv[0:1, KG16:KG16 + nkh],
                        smv[0:1, KG16 + nkh:KG16 + 2 * nkh], ALU.max),
                        ins=[g("smv")], outs=[g("smv")])
                    for r_ in (2, 3):
                        p.op("dve", lambda q, r_=r_: q.tensor_tensor(
                            smv[0:1, KMX:KMX + nkh], smv[0:1, KMX:KMX + nkh],
                            smv[0:1, KG16 + r_ * nkh:KG16 + (r_ + 1) * nkh],
                            ALU.max), ins=[g("smv")], outs=[g("smv")])
                    for s_ in range(16):
                        kh = ((FFN_QORD[s_] // 4) if ffn else s_)
                        p.op("dve", lambda q, s_=s_, kh=kh: q.tensor_scalar(
                            smv[0:1, KMAP + s_:KMAP + s_ + 1],
                            smv[0:1, KMAP + s_:KMAP + s_ + 1], 0.0,
                            smv[0:1, KMX + kh:KMX + kh + 1], ALU.mult,
                            ALU.add), ins=[g("smv")], outs=[g("smv")])
                    pair_slots = []
                    if ffn:
                        for kh in range(4):
                            for u in (0, 1):
                                pair_slots.append((slot_of[4 * kh + 2 * u],
                                                   slot_of[4 * kh + 2 * u + 1]))
                    else:
                        for pr in range(8):
                            pair_slots.append((2 * pr, 2 * pr + 1))
                    for pp, (sa, sb_) in enumerate(pair_slots):
                        p.op("dve",
                             lambda q, pp=pp, sa=sa, sb_=sb_: q.tensor_tensor(
                                 smv[0:1, QP2 + pp:QP2 + pp + 1],
                                 smv[0:1, QM2 + sa:QM2 + sa + 1],
                                 smv[0:1, QM2 + sb_:QM2 + sb_ + 1], ALU.max),
                             ins=[g("smv")], outs=[g("smv")])
                        p.op("dve",
                             lambda q, pp=pp, sa=sa, sb_=sb_: q.tensor_tensor(
                                 smv[0:1, KP2 + pp:KP2 + pp + 1],
                                 smv[0:1, KMAP + sa:KMAP + sa + 1],
                                 smv[0:1, KMAP + sb_:KMAP + sb_ + 1], ALU.max),
                             ins=[g("smv")], outs=[g("smv")])
                    p.op("dve", lambda q: q.tensor_tensor(
                        smv[0:1, CRW:CRW + 8], smv[0:1, QP2:QP2 + 8],
                        smv[0:1, KP2:KP2 + 8], ALU.mult),
                        ins=[g("smv")], outs=[g("smv")])
                    p.op("act", lambda q: q.activation(
                        smv[0:1, CRW:CRW + 8], smv[0:1, CRW:CRW + 8],
                        AF.Sqrt, bias=0.0), ins=[g("smv")], outs=[g("smv")])
                    p.op("dve", lambda q: q.tensor_scalar(
                        smv[0:1, CRW:CRW + 8], smv[0:1, CRW:CRW + 8], -8.0,
                        None, ALU.mult), ins=[g("smv")], outs=[g("smv")])
                    p.op("dve", lambda q: q.tensor_copy(
                        crowb[0:1, 0:8], smv[0:1, CRW:CRW + 8]),
                        ins=[g("smv")], outs=[crB])
                    for pp in range(8):
                        p.op("pe", lambda q, pp=pp: q.matmul(
                            pcb[:, pp:pp + 1], onesb[0:1, 0:128],
                            crowb[0:1, pp:pp + 1], start=True, stop=True),
                            ins=[crB, con], outs=[pcbB])
                    p.op("dve", lambda q: q.tensor_copy(cbias[:, 0:8],
                                                        pcb[:, 0:8]),
                         ins=[pcbB], outs=[g("cbias")])
                    p.barrier()

                with ExitStack() as ph:
                    sbp = lambda n, s, d: ph.enter_context(
                        nc.sbuf_tensor(_nm(n), list(s), d))
                    psp = lambda n, s, d=F32: ph.enter_context(
                        nc.psum_tensor(_nm(n), list(s), d))
                    kwin = [sbp(f"kwin{i}", (128, 2048), BF16)
                            for i in range(4)]
                    vwin = [sbp(f"vwin{i}", (128, 2080), BF16)
                            for i in range(4)]
                    pP = sbp("pPex", (128, 3 * 1024), BF16)
                    ctxn = sbp("ctxn", (128, 8 * 512), BF16)
                    wob = sbp("wob", (128, 8 * 1024), BF16)
                    dsc = sbp("dsc", (128, 512), F32)
                    dts = sbp("dts", (128, 4 * 1024), F32)
                    rbc = sbp("rbc", (128, 512), F32)
                    boutb = sbp("boutb", (1, D), BF16) if not ffn else None
                    pS = [psp(f"pS{i}", (128, 1024)) for i in (0, 1)]
                    pAV = [psp(f"pAV{i}", (128, 512)) for i in range(4)]
                    kwinB = [Buf(f"kw{i}") for i in range(4)]
                    vwinB = [Buf(f"vw{i}") for i in range(4)]
                    pPB = [Buf("pP0"), Buf("pP1"), Buf("pP2")]
                    wobB, dscB, rbcB = Buf("wob"), Buf("dsc"), Buf("rbc")
                    boutB = Buf("boutb")
                    pSB = [[Buf("pS00"), Buf("pS01")],
                           [Buf("pS10"), Buf("pS11")]]
                    pAVB = [Buf(f"pAV{i}") for i in range(4)]
                    dtsB = [Buf(f"dts{j}") for j in range(NT)]
                    ctxB = [Buf(f"ctx{c}") for c in range(8)]

                    for kc in range(8):
                        p.dma("pool", wob[:, kc * 1024:(kc + 1) * 1024],
                              bass.AP(wo if ffn else w_out, kc * 128 * D,
                                      [[D, 128], [1, D]]), outs=[wobB],
                              dsem="dwo")
                    if ffn:
                        emit_moe_casts()
                    if not ffn:
                        p.dma("pool", boutb[:, :], b_out.ap(), outs=[boutB])

                    pend = []

                    def _do_av(v_ap_of, vwbuf, heads, av_first, av_last,
                               pslot, mark=False):
                        for v_ in range(2):
                            qi, s_ = heads[v_]
                            p.acc_mm(pAVB[qi], av_first, av_last,
                                     lambda q, qi=qi, v_=v_: q.matmul(
                                         pAV[qi][0:65, :], v_ap_of(v_),
                                         pP[:, pslot * 1024 + v_ * 512:
                                            pslot * 1024 + v_ * 512 + 512],
                                         start=av_first, stop=av_last),
                                     ins=[vwbuf, pPB[pslot]],
                                     force_mark=(mark and v_ == 1))

                    def flush_av(mark=False):
                        while pend:
                            _do_av(*pend.pop(0), mark=mark)

                    it_ctr = [0]

                    def s_exp_av(k_ap_of, kwbuf, v_ap_of, vwbuf, tcol, qt,
                                 heads, av_first, av_last, sb_, pp_pair,
                                 diag_t=None):
                        sb_ = it_ctr[0] % 2
                        pslot = it_ctr[0] % 3
                        it_ctr[0] += 1
                        for v_ in range(2):
                            qi, s_ = heads[v_]
                            base = 64 * (s_ % 2)
                            p.op("pe", lambda q, v_=v_, s_=s_, base=base:
                                 q.matmul(
                                     pS[sb_][:, v_ * 512:v_ * 512 + 512],
                                     k_ap_of(base, tcol),
                                     qT[base:base + 64,
                                        (s_ // 2) * 2048 + qt * 512:
                                        (s_ // 2) * 2048 + qt * 512 + 512],
                                     start=True, stop=True),
                                 ins=[kwbuf, g("qT", s_ // 2, qt)],
                                 outs=[pSB[sb_][v_]])
                        p.op("act", lambda q: q.activation(
                            pP[:, pslot * 1024:pslot * 1024 + 1024],
                            pS[sb_][:, :], AF.Exp,
                            bias=cbias[:, pp_pair:pp_pair + 1], scale=8.0),
                            ins=[pSB[sb_][0], pSB[sb_][1], g("cbias")],
                            outs=[pPB[pslot]])
                        if diag_t is not None:
                            for v_ in range(2):
                                p.op("dve", lambda q, v_=v_: q.tensor_tensor(
                                    pP[:, pslot * 1024 + v_ * 512:
                                       pslot * 1024 + v_ * 512 + 512],
                                    pP[:, pslot * 1024 + v_ * 512:
                                       pslot * 1024 + v_ * 512 + 512],
                                    cons_masks[:, diag_t * 512:
                                               (diag_t + 1) * 512],
                                    ALU.mult), ins=[pPB[pslot], cmB],
                                    outs=[pPB[pslot]])
                        pend.append((v_ap_of, vwbuf, heads, av_first,
                                     av_last, pslot))
                        if len(pend) > (2 if PIPELINE_AV else 0):
                            _do_av(*pend.pop(0))

                    def normalize(qi, s_, qt):
                        p.op("dve", lambda q: q.reciprocal(
                            smv[0:1, RECR:RECR + 512], pAV[qi][64:65, 0:512]),
                            ins=[pAVB[qi]], outs=[g("smv")])
                        p.op("dve", lambda q: q.tensor_copy(
                            rrb[0:1, :], smv[0:1, RECR:RECR + 512]),
                            ins=[g("smv")], outs=[rrbB])
                        p.op("pe", lambda q: q.matmul(
                            pS[0][0:64, 0:512], onesb[0:1, 0:64], rrb[0:1, :],
                            start=True, stop=True), ins=[rrbB, con],
                            outs=[pSB[0][0]])
                        p.op("dve", lambda q: q.tensor_copy(
                            rbc[0:64, :], pS[0][0:64, 0:512]),
                            ins=[pSB[0][0]], outs=[rbcB])
                        base = 64 * (s_ % 2)
                        dst = ctxn[base:base + 64,
                                   (s_ // 2) * 512:(s_ // 2) * 512 + 512]
                        p.op("dve", lambda q, dst=dst: q.tensor_tensor(
                            dst, pAV[qi][0:64, 0:512], rbc[0:64, :], ALU.mult),
                            ins=[pAVB[qi], rbcB], outs=[ctxB[s_ // 2]])

                    for qt in range(NT):
                        if ffn:
                            head_groups = [([(i, slot_of[4 * kh + i])
                                             for i in range(4)], kh)
                                           for kh in range(4)]
                        else:
                            head_groups = [([(i, 4 * grp + i)
                                             for i in range(4)], grp)
                                           for grp in range(4)]
                        for heads, gid in head_groups:
                            it = 0
                            if ffn:
                                kh = gid
                                kbase = 64 * (kh % 2)
                                def load_ffn_blk(j):
                                    r_, sl = tile_owner(j)
                                    kb = j % 2
                                    p.dma("sp",
                                          kwin[kb][kbase:kbase + 64, 0:512],
                                          bass.AP(kg_, (r_ * 256 + kh * 64) * T
                                                  + sl * 512,
                                                  [[T, 64], [1, 512]]),
                                          ins=[kgB], outs=[kwinB[kb]],
                                          dsem=f"dk{kb}")
                                    p.dma("sp", vwin[kb][:, 0:260],
                                          bass.AP(vg_, (r_ * 512 + kh * 128) *
                                                  1040 + sl * 260,
                                                  [[1040, 128], [1, 260]]),
                                          ins=[vgB], outs=[vwinB[kb]],
                                          dsem=f"dv{kb}")
                                    p.op("dve", lambda q, kb=kb, qt=qt, j=j:
                                         q.tensor_scalar(
                                             vwin[kb][:, 0:260],
                                             vwin[kb][:, 0:260],
                                             vmsk[:, qt * 16 + j:
                                                  qt * 16 + j + 1],
                                             None, ALU.mult),
                                         ins=[vwinB[kb], con],
                                         outs=[vwinB[kb]])

                                if UBLK[qt] > 0:
                                    flush_av(mark=True)
                                    load_ffn_blk(0)
                                for j in range(UBLK[qt]):
                                    kb = j % 2
                                    flush_av(mark=True)
                                    if j + 1 < UBLK[qt]:
                                        load_ffn_blk(j + 1)
                                    for t_ in range(4):
                                        for u in (0, 1):
                                            hp = [heads[2 * u],
                                                  heads[2 * u + 1]]
                                            s_exp_av(
                                                lambda base, tcol, kb=kb:
                                                kwin[kb][base:base + 64,
                                                         tcol:tcol + 128],
                                                kwinB[kb],
                                                lambda v_, kb=kb, t_=t_:
                                                vwin[kb][:, t_ * 65:
                                                         t_ * 65 + 65],
                                                vwinB[kb], t_ * 128, qt, hp,
                                                it == 0, False,
                                                u % 2,
                                                gid * 2 + u)
                                        it += 1
                                for t_ in range(4):
                                    last = (t_ == 3)
                                    for u in (0, 1):
                                        hp = [heads[2 * u], heads[2 * u + 1]]
                                        s_exp_av(
                                            lambda base, tcol, kh=kh, qt=qt:
                                            klocal[base:base + 64,
                                                   (kh // 2) * T + qt * 512 +
                                                   tcol:
                                                   (kh // 2) * T + qt * 512 +
                                                   tcol + 128],
                                            klB,
                                            lambda v_, kh=kh, qt=qt, t_=t_:
                                            vlocal[:, kh * 1040 +
                                                   (qt * 4 + t_) * 65:
                                                   kh * 1040 +
                                                   (qt * 4 + t_) * 65 + 65],
                                            vlB, t_ * 128, qt, hp,
                                            it == 0, last,
                                            u % 2,
                                            gid * 2 + u, diag_t=t_)
                                    it += 1
                            else:
                                grp = gid
                                def load_ab_win(r_, st):
                                    for pr in (0, 1):
                                        pair = grp * 2 + pr
                                        idx = st * 2 + pr
                                        p.dma("sp", kwin[idx][:, :],
                                              bass.AP(kg_, (r_ * 1024 +
                                                            pair * 128) * T,
                                                      [[T, 128], [1, T]]),
                                              ins=[kgB], outs=[kwinB[idx]],
                                              dsem=f"dk{idx}")
                                        p.dma("sp", vwin[idx][:, :],
                                              bass.AP(vg_, (r_ * 1024 +
                                                            pair * 128) * 2080,
                                                      [[2080, 128],
                                                       [1, 2080]]),
                                              ins=[vgB], outs=[vwinB[idx]],
                                              dsem=f"dv{idx}")

                                flush_av(mark=True)
                                load_ab_win(0, 0)
                                for r_ in range(4):
                                    st = r_ % 2
                                    flush_av(mark=True)
                                    if r_ + 1 < 4:
                                        load_ab_win(r_ + 1, 1 - st)
                                    for jj in range(16):
                                        for pr in (0, 1):
                                            idx = st * 2 + pr
                                            hp = [heads[2 * pr],
                                                  heads[2 * pr + 1]]
                                            s_exp_av(
                                                lambda base, tcol, idx=idx:
                                                kwin[idx][base:base + 64,
                                                          tcol:tcol + 128],
                                                kwinB[idx],
                                                lambda v_, idx=idx, jj=jj:
                                                vwin[idx][:, jj * 130 +
                                                          v_ * 65:
                                                          jj * 130 +
                                                          v_ * 65 + 65],
                                                vwinB[idx], jj * 128, qt, hp,
                                                r_ == 0 and jj == 0,
                                                r_ == 3 and jj == 15,
                                                it % 2, grp * 2 + pr)
                                            it += 1
                            flush_av()
                            for qi, s_ in heads:
                                normalize(qi, s_, qt)

                        for oc in range(8):
                            mms = []
                            ins = [wobB, con] + [ctxB[kc] for kc in range(8)]
                            if not ffn:
                                mms.append((pAV[0][:, :],
                                            boutb[0:1, oc * 128:oc * 128 + 128],
                                            onesb[0:1, 0:512]))
                                ins.append(boutB)
                            for kc in range(8):
                                mms.append((pAV[0][:, :],
                                            wob[:, kc * 1024 + oc * 128:
                                                kc * 1024 + oc * 128 + 128],
                                            ctxn[:, kc * 512:(kc + 1) * 512]))
                            p.mm_group(mms, ins=ins, outs=[pAVB[0]])
                            p.op("act", lambda q: q.copy(dsc[:, :],
                                                         pAV[0][:, :]),
                                 ins=[pAVB[0]], outs=[dscB])
                            for j in range(4):
                                p.op("pe", lambda q, j=j: q.transpose(
                                    pAV[1][:, 0:128],
                                    dsc[:, j * 128:(j + 1) * 128],
                                    identf[:, :]), ins=[dscB, con],
                                    outs=[pAVB[1]])
                                p.op("dve", lambda q, j=j, oc=oc: q.tensor_copy(
                                    dts[:, j * 1024 + oc * 128:
                                        j * 1024 + oc * 128 + 128],
                                    pAV[1][:, 0:128]), ins=[pAVB[1]],
                                    outs=[dtsB[j]])

                        for j in range(4):
                            off = (qt * 512 + j * 128) * D
                            if ffn:
                                p.dma("sp", msc[:, 0:1024],
                                      bass.AP(ys, off, [[D, 128], [1, D]]),
                                      ins=[g("ys")], outs=[g("msc", 0)],
                                      dsem="dm0")
                                p.op("dve", lambda q, j=j: q.tensor_tensor(
                                    msc[:, 1024:2048], msc[:, 0:1024],
                                    dts[:, j * 1024:(j + 1) * 1024], ALU.add),
                                    ins=[g("msc", 0), dtsB[j]],
                                    outs=[g("msc", 1)])
                                p.dma("sp",
                                      bass.AP(ys, off, [[D, 128], [1, D]]),
                                      msc[:, 1024:2048], ins=[g("msc", 1)],
                                      outs=[g("ys")], dsem="ds_ys0")
                            else:
                                p.dma("sp", msc[:, 0:1024],
                                      bass.AP(r1, off, [[D, 128], [1, D]]),
                                      ins=[g("r1")], outs=[g("msc", 0)],
                                      dsem="dm0")
                                p.dma("sp", msc[:, 2048:3072],
                                      bass.AP(h2s, off, [[D, 128], [1, D]]),
                                      ins=[g("h2s")], outs=[g("msc", 2)],
                                      dsem="dm2")
                                p.op("dve", lambda q, j=j: q.tensor_tensor(
                                    msc[:, 1024:2048], msc[:, 2048:3072],
                                    dts[:, j * 1024:(j + 1) * 1024], ALU.add),
                                    ins=[g("msc", 2), dtsB[j]],
                                    outs=[g("msc", 1)])
                                p.op("dve", lambda q: q.tensor_tensor(
                                    msc[:, 1024:2048], msc[:, 0:1024],
                                    msc[:, 1024:2048], ALU.add),
                                    ins=[g("msc", 0), g("msc", 1)],
                                    outs=[g("msc", 1)])
                                p.dma("sp",
                                      bass.AP(r2, off, [[D, 128], [1, D]]),
                                      msc[:, 1024:2048], ins=[g("msc", 1)],
                                      outs=[g("r2")], dsem="ds_r2")
                    p.barrier()

        GL0, GW0 = 0, 2048

        def moe_and_close(src_dram, src_buf, dst_dram, dst_buf):
            norm_transpose_phase(ys, g("ys"), "ffn2")
            with ExitStack() as mo:
                sbm = lambda n, s, d: mo.enter_context(
                    nc.sbuf_tensor(_nm(n), list(s), d))
                psm = lambda n: mo.enter_context(
                    nc.psum_tensor(_nm(n), [128, 512], F32))
                acc = sbm("acc", (128, 8 * 1024), F32)
                cbm = sbm("cbm", (128, 8 * 512), BF16)
                gateb = sbm("gateb", (128, 32), BF16)
                wbuf = [sbm(f"wmoe{i}", (128, 36 * 512), BF16) for i in (0, 1)]
                xne = sbm("xne", (128, 2 * 8 * 512), BF16)
                hbuf = sbm("hbuf", (128, 6 * 512), BF16)
                _ssl = sbm("ssl0", (128, 512), BF16)
                ssl = [_ssl, _ssl]
                dtile = [sbm("dtile0", (128, 1024), F32)] * 2
                pa_ = [psm(f"pa{i}") for i in (0, 1)]
                pb_ = [psm(f"pb{i}") for i in (0, 1)]
                pw_ = [psm(f"pw{i}") for i in (0, 1)]
                pcm = psm("pcm")
                ptc0 = psm("ptc0")
                accB = Buf("acc")
                cbmB = [Buf(f"cbm{i}") for i in range(8)]
                gateB = Buf("gateb")
                xneB = [Buf("xne0"), Buf("xne1")]
                _sslB = Buf("ssl0")
                sslB = [_sslB, _sslB]
                paB = [Buf("pa0"), Buf("pa1")]
                pbB = [Buf("pb0"), Buf("pb1")]
                pwB = [Buf("pw0"), Buf("pw1")]
                pcmB = Buf("pcm")
                ptcB = [Buf("ptc0"), Buf("ptc1")]
                _dt = Buf("dt0")
                dtB2 = [_dt, _dt]
                hcB = [Buf(f"hc{i}") for i in range(6)]
                wB = [[Buf(f"w{i}_{k}") for k in range(22)] for i in (0, 1)]
                pgt = [pa_[0], pa_[1], pb_[0], pb_[1]]
                pgtB = [paB[0], paB[1], pbB[0], pbB[1]]

                p.dma("pool", gateb[:, :].rearrange("p (k e) -> p k e", k=8),
                      bass.AP(gate, 0, [[4, 128], [512, 8], [1, 4]]),
                      outs=[gateB], dsem="dsm")

                def load_moe_w(wi, e, hq):
                    for kc in range(8):
                        p.dma("pool",
                              wbuf[wi][:, kc * 768:(kc + 1) * 768],
                              bass.AP(w1b, (e * 1024 + kc * 128) * HID +
                                      hq * 768, [[HID, 128], [1, 768]]),
                              ins=[g("w1b")], outs=[wB[wi][kc]],
                              dsem=f"dw{wi}a")
                        p.dma("pool",
                              wbuf[wi][:, 6144 + kc * 768:
                                       6144 + (kc + 1) * 768],
                              bass.AP(w3b, (e * 1024 + kc * 128) * HID +
                                      hq * 768, [[HID, 128], [1, 768]]),
                              ins=[g("w3b")], outs=[wB[wi][8 + kc]],
                              dsem=f"dw{wi}b")
                    for kc2 in range(6):
                        p.dma("pool",
                              wbuf[wi][:, 12288 + kc2 * 1024:
                                       12288 + (kc2 + 1) * 1024],
                              bass.AP(w2b, (e * HID + hq * 768 +
                                           kc2 * 128) * D,
                                      [[D, 128], [1, D]]),
                              ins=[g("w2b")], outs=[wB[wi][16 + kc2]],
                              dsem=f"dw{wi}c")

                def gate_comb(half):
                    for tq in range(2):
                        qt = half * 2 + tq
                        L = lambda e: smv[0:1, e * 512:(e + 1) * 512]
                        W_ = lambda i: smv[0:1, 2048 + i * 512:
                                           2048 + (i + 1) * 512]
                        for e in range(NE):
                            p.mm_group(
                                [(pgt[e][0:1, :],
                                  gateb[:, kc * 4 + e:kc * 4 + e + 1],
                                  xnT[:, kc * 2048 + qt * 512:
                                      kc * 2048 + qt * 512 + 512])
                                 for kc in range(8)],
                                ins=[gateB] + [g("xnT", kc, qt)
                                               for kc in range(8)],
                                outs=[pgtB[e]])
                            p.op("dve", lambda q, e=e: q.tensor_copy(
                                L(e), pgt[e][0:1, :]),
                                ins=[pgtB[e]], outs=[g("smv")])
                        sop = lambda o, a, b2, alu: p.op(
                            "dve", lambda q: q.tensor_tensor(o, a, b2, alu),
                            ins=[g("smv")], outs=[g("smv")])
                        sop(W_(0), L(0), L(1), ALU.max)
                        sop(W_(1), L(0), L(1), ALU.min)
                        sop(W_(2), L(2), L(3), ALU.max)
                        sop(W_(3), L(2), L(3), ALU.min)
                        sop(W_(1), W_(1), W_(3), ALU.max)
                        sop(W_(3), W_(0), W_(2), ALU.min)
                        sop(W_(0), W_(0), W_(2), ALU.max)
                        sop(W_(1), W_(1), W_(3), ALU.max)
                        sop(W_(2), W_(1), W_(0), ALU.subtract)
                        p.op("act", lambda q: q.activation(
                            W_(2), W_(2), AF.Exp, bias=0.0),
                            ins=[g("smv")], outs=[g("smv")])
                        p.op("dve", lambda q: q.tensor_scalar(
                            W_(2), W_(2), 1.0, None, ALU.add),
                            ins=[g("smv")], outs=[g("smv")])
                        p.op("dve", lambda q: q.reciprocal(W_(3), W_(2)),
                             ins=[g("smv")], outs=[g("smv")])
                        for e in range(NE):
                            sop(W_(2), L(e), W_(0), ALU.subtract)
                            p.op("act", lambda q: q.activation(
                                W_(2), W_(2), AF.Exp, bias=0.0),
                                ins=[g("smv")], outs=[g("smv")])
                            sop(L(e), L(e), W_(1), ALU.is_ge)
                            sop(W_(2), W_(2), L(e), ALU.mult)
                            sop(W_(2), W_(2), W_(3), ALU.mult)
                            p.op("dve", lambda q: q.tensor_copy(
                                rrb[0:1, :], W_(2)),
                                ins=[g("smv")], outs=[rrbB])
                            p.op("pe", lambda q: q.matmul(
                                pcm[:, :], onesb[0:1, 0:128], rrb[0:1, :],
                                start=True, stop=True),
                                ins=[rrbB, con], outs=[pcmB])
                            ci = e * 2 + tq
                            p.op("dve", lambda q, ci=ci: q.tensor_copy(
                                cbm[:, ci * 512:(ci + 1) * 512],
                                pcm[:, :]), ins=[pcmB], outs=[cbmB[ci]])

                for half in range(2):
                    gate_comb(half)
                    p.op("dve", lambda q: q.memset(acc[:, :], 0.0),
                         ins=[], outs=[accB])
                    phases = [(e, hq) for e in range(NE) for hq in range(4)]
                    if half == 0:
                        load_moe_w(0, *phases[0])
                    xne_done = set()
                    for pi, (e, hq) in enumerate(phases):
                        wi = pi % 2
                        if pi + 1 < len(phases):
                            load_moe_w(1 - wi, *phases[pi + 1])
                        elif half == 0:
                            load_moe_w(1 - wi, *phases[0])
                        if e not in xne_done:
                            xne_done.add(e)
                            for tq0 in range(2):
                                qt0 = half * 2 + tq0
                                ci0 = e * 2 + tq0
                                for kc in range(8):
                                    p.op("dve",
                                         lambda q, kc=kc, ci0=ci0, qt0=qt0,
                                         tq0=tq0: q.tensor_tensor(
                                             xne[:, tq0 * 4096 + kc * 512:
                                                 tq0 * 4096 + (kc + 1) * 512],
                                             xnT[:, kc * 2048 + qt0 * 512:
                                                 kc * 2048 + qt0 * 512 + 512],
                                             cbm[:, ci0 * 512:
                                                 (ci0 + 1) * 512],
                                             ALU.mult),
                                         ins=[g("xnT", kc, qt0), cbmB[ci0]],
                                         outs=[xneB[tq0]])
                        for tq in range(2):
                            qt = half * 2 + tq
                            for hc in range(6):
                                bb = hc % 2
                                p.mm_group(
                                    [(pa_[bb][:, :],
                                      wbuf[wi][:, kc * 768 + hc * 128:
                                               kc * 768 + hc * 128 + 128],
                                      xnT[:, kc * 2048 + qt * 512:
                                          kc * 2048 + qt * 512 + 512])
                                     for kc in range(8)],
                                    ins=wB[wi][0:8] + [g("xnT", kc, qt)
                                                       for kc in range(8)],
                                    outs=[paB[bb]])
                                p.mm_group(
                                    [(pb_[bb][:, :],
                                      wbuf[wi][:, 6144 + kc * 768 + hc * 128:
                                               6144 + kc * 768 + hc * 128 +
                                               128],
                                      xne[:, tq * 4096 + kc * 512:
                                          tq * 4096 + (kc + 1) * 512])
                                     for kc in range(8)],
                                    ins=wB[wi][8:16] + [xneB[tq]],
                                    outs=[pbB[bb]])
                                p.op("act", lambda q, bb=bb: q.activation(
                                    ssl[bb][:, :], pa_[bb][:, :], AF.Silu),
                                    ins=[paB[bb]], outs=[sslB[bb]])
                                p.op("dve", lambda q, hc=hc, bb=bb:
                                     q.tensor_tensor(
                                         hbuf[:, hc * 512:(hc + 1) * 512],
                                         ssl[bb][:, :], pb_[bb][:, :],
                                         ALU.mult),
                                     ins=[sslB[bb], pbB[bb]], outs=[hcB[hc]])
                            for oc in range(8):
                                bb = oc % 2
                                p.mm_group(
                                    [(pw_[bb][:, :],
                                      wbuf[wi][:, 12288 + kc2 * 1024 +
                                               oc * 128:
                                               12288 + kc2 * 1024 +
                                               oc * 128 + 128],
                                      hbuf[:, kc2 * 512:(kc2 + 1) * 512])
                                     for kc2 in range(6)],
                                    ins=wB[wi][16:22] + hcB,
                                    outs=[pwB[bb]])
                                p.op("dve", lambda q, oc=oc, tq=tq, bb=bb:
                                     q.tensor_tensor(
                                         acc[:, oc * 1024 + tq * 512:
                                             oc * 1024 + tq * 512 + 512],
                                         acc[:, oc * 1024 + tq * 512:
                                             oc * 1024 + tq * 512 + 512],
                                         pw_[bb][:, :], ALU.add),
                                     ins=[pwB[bb]], outs=[accB])

                    ptc = [ptc0, pcm]
                    for tt in range(8):
                        ci3 = tt % 2
                        dtB = dtB2[ci3]
                        tt_g = half * 8 + tt
                        off = tt_g * 128 * D
                        for c in range(8):
                            cp = c % 2
                            p.op("pe", lambda q, c=c, tt=tt, cp=cp:
                                 q.transpose(
                                     ptc[cp][:, 0:128],
                                     acc[:, c * 1024 + tt * 128:
                                         c * 1024 + tt * 128 + 128],
                                     identf[:, :]), ins=[accB, con],
                                 outs=[ptcB[cp] if cp == 0 else pcmB])
                            p.op("dve", lambda q, c=c, cp=cp, ci3=ci3:
                                 q.tensor_copy(
                                     dtile[ci3][:, c * 128:(c + 1) * 128],
                                     ptc[cp][:, 0:128]),
                                 ins=[ptcB[cp] if cp == 0 else pcmB],
                                 outs=[dtB])
                        p.dma("sp", msc[:, 0:1024],
                              bass.AP(ys, off, [[D, 128], [1, D]]),
                              ins=[g("ys")], outs=[g("msc", 0)],
                              dsem="dm0")
                        p.dma("sp", msc[:, 2048:3072],
                              bass.AP(src_dram, off, [[D, 128], [1, D]]),
                              ins=[src_buf], outs=[g("msc", 2)],
                              dsem="dm2")
                        p.op("dve", lambda q, ci3=ci3: q.tensor_tensor(
                            msc[:, 1024:2048], msc[:, 0:1024],
                            dtile[ci3][:, :], ALU.add),
                            ins=[g("msc", 0), dtB], outs=[g("msc", 1)])
                        p.op("dve", lambda q: q.scalar_tensor_tensor(
                            msc[:, 1024:2048], msc[:, 1024:2048], 0.5,
                            msc[:, 2048:3072], ALU.mult, ALU.add),
                            ins=[g("msc", 1), g("msc", 2)],
                            outs=[g("msc", 1)])
                        p.dma("sp",
                              bass.AP(dst_dram, off, [[D, 128], [1, D]]),
                              msc[:, 1024:2048], ins=[g("msc", 1)],
                              outs=[dst_buf], dsem="ds_out")
                p.barrier()

        norm_transpose_phase(x_in, g("x_in"), "ffn", store_stream=ys,
                             store_buf=g("ys"))
        maybe_dump_dram("y0", ys, g("ys"))
        maybe_dump("xnT1", xnT[:, :], (128, 16 * 1024), BF16,
                   [g("xnT", c, t_) for c in range(8) for t_ in range(NT)])
        if upto != "norm1":
            attn_invocation("ffn")
            maybe_dump_dram("y1a", ys, g("ys"))
            if upto != "attn1":
                moe_and_close(x_in, g("x_in"), r1, g("r1"))
                maybe_dump_dram("r1d", r1, g("r1"))
                if upto != "moe1":
                    norm_transpose_phase(r1, g("r1"), "ab", store_stream=h2s,
                                         store_buf=g("h2s"))
                    attn_invocation("ab")
                    maybe_dump_dram("r2d", r2, g("r2"))
                    if upto != "ab":
                        norm_transpose_phase(r2, g("r2"), "ffn",
                                             store_stream=ys,
                                             store_buf=g("ys"))
                        attn_invocation("ffn")
                        moe_and_close(r2, g("r2"), out_t, g("out"))

        if upto is not None:
            final_toks.append(p.dma("sp", out_t.ap(), x_in.ap(),
                                    ins=[g("x_in")]))

        for tok in final_toks:
            p.wait("sp", tok)
        p.wait("sp", ("d_sp", p.cnt["d_sp"]))
        p.barrier()

    return nc


def _prep_inputs(inputs, cig, batch):
    x = inputs["L0_attn_X"]
    tiles = own_tiles(cig)
    x_local = np.concatenate([x[batch, gt * 512:(gt + 1) * 512, :]
                              for gt in tiles], 0)
    qperm = np.concatenate([np.arange(64 * h, 64 * h + 64) for h in FFN_QORD])
    wq_r = inputs["L0_ffn_wq"][:, qperm]
    wo_r = inputs["L0_ffn_wo"][qperm, :]
    tabCf, tabSf, tabCa, tabSa = make_tables(cig)
    pf, pa = make_perms()
    masks = make_masks().reshape(4 * 128, 512)
    m = {
        "x_local": x_local,
        "ffn_wq": wq_r,
        "ffn_wk": inputs["L0_ffn_wk"],
        "ffn_wv": inputs["L0_ffn_wv"],
        "ffn_wo": wo_r,
        "ffn_gate": inputs["L0_ffn_gate"],
        "ffn_w1": inputs["L0_ffn_w1"].reshape(NE * D, HID),
        "ffn_w2": inputs["L0_ffn_w2"].reshape(NE * HID, D),
        "ffn_w3": inputs["L0_ffn_w3"].reshape(NE * D, HID),
        "ffn_g1": inputs["L0_ffn_g1"].reshape(1, D),
        "ffn_g2": inputs["L0_ffn_g2"].reshape(1, D),
        "ab_w_in": inputs["L0_attn_W_in"],
        "ab_w_out": inputs["L0_attn_W_out"],
        "ab_b_in": inputs["L0_attn_b_in"].reshape(1, 3 * D),
        "ab_b_out": inputs["L0_attn_b_out"].reshape(1, D),
        "ab_gamma": inputs["L0_attn_gamma"].reshape(1, D),
        "ab_beta": inputs["L0_attn_beta"].reshape(1, D),
        "tabC_ffn": tabCf, "tabS_ffn": tabSf,
        "tabC_ab": tabCa, "tabS_ab": tabSa,
        "perm_ffn": pf, "perm_ab": pa,
        "masks": masks,
        "ident": np.eye(128, dtype=np.float32),
        "ones_bf": np.ones((1, 128), ml_dtypes.bfloat16),
        "ones_col": np.ones((128, 1), ml_dtypes.bfloat16),
        "cvec": np.tile(np.array([[EPS_LN, EPS_RMS, 1.0, 0.0, 0, 0, 0, 0]],
                                 np.float32), (128, 1)),
        "vmask": make_vmask(cig),
    }
    return {k: np.ascontiguousarray(v) for k, v in m.items()}


def kernel(**inputs):
    nc = build_program()
    in_maps = []
    for core in range(NCORE):
        batch, cig = divmod(core, 4)
        in_maps.append(_prep_inputs(inputs, cig, batch))
    res = run_bass_kernel_spmd(nc, in_maps, core_ids=list(range(NCORE)))
    out = np.zeros((B, S, D), np.float32)
    for core in range(NCORE):
        batch, cig = divmod(core, 4)
        o = res.results[core]["out_local"]
        for slot, gt in enumerate(own_tiles(cig)):
            out[batch, gt * 512:(gt + 1) * 512, :] = \
                o[slot * 512:(slot + 1) * 512]
    return out
